# version 06
# baseline (speedup 1.0000x reference)
import contextlib

import numpy as np
import concourse.bass as bass
import concourse.mybir as mybir
from concourse.bass_utils import run_bass_kernel_spmd

F32 = mybir.dt.float32
BF16 = mybir.dt.bfloat16
AF = mybir.ActivationFunctionType
ALU = mybir.AluOpType
AX = mybir.AxisListType

NCORES = 8
B, T, D = 16, 2048, 1024
BPC = B // NCORES
NTOK = BPC * T
DC = D // 128


LAST_TOTAL = None
LAST_NSTEPS = None
NOCLEAR = ("dg",)
OWN_SKIP = ("tensor",)


class Prog:

    def __init__(self, nc):
        self.nc = nc
        self.steps = []
        self.n_cc = 0

    def add(self, eng, fn, inc=1, key=None):
        self.steps.append((eng, fn, inc, key or eng))

    def dma(self, out, in_, q="sync", **kw):
        key = "dg" if q == "gpsimd" else "ds"
        self.add(q, lambda e, o=out, i=in_, kw=kw: e.dma_start(o, i, **kw), 16, key)

    def cast_dma(self, out, in_, **kw):
        self.add("gpsimd", lambda e, o=out, i=in_, kw=kw: e.dma_start(o, i, **kw), 16, "dg")

    def collective(self, kind, op, rg, ins, outs):
        key = f"cc{self.n_cc}"
        self.n_cc += 1
        self.add("gpsimd", lambda e, kind=kind, op=op, rg=rg, ins=ins, outs=outs:
                 e.collective_compute(kind, op, replica_groups=rg, ins=ins, outs=outs),
                 1, key)

    def emit(self, sems, al=None):
        global LAST_TOTAL, LAST_NSTEPS
        engines_used = [en for en in ["sync", "tensor", "vector", "scalar", "gpsimd"]
                        if any(s[0] == en for s in self.steps)]
        comp_keys = [en for en in engines_used
                     if any(s[0] == en and s[3] == en for s in self.steps)]
        keys = comp_keys + ["ds", "dg"] + [f"cc{i}" for i in range(self.n_cc)]
        sems = dict(sems)
        for k in comp_keys:
            if k not in sems:
                sems[k] = al.sem(f"eng_{k}")
        for i in range(self.n_cc):
            sems[f"cc{i}"] = al.sem(f"ccsem{i}")
        pre = []
        cum = {k: 0 for k in keys}
        for (_, _, inc, key) in self.steps:
            pre.append(dict(cum))
            cum[key] += inc
        self.total = dict(cum)
        LAST_TOTAL = dict(cum)
        LAST_NSTEPS = len(self.steps)
        done = al.sem("prog_done") if al is not None else None
        n_other = len([e for e in engines_used if e != "gpsimd"])
        with self.nc.Block() as block:
            for en in ["sync", "tensor", "vector", "scalar", "gpsimd"]:
                sub = [(i, self.steps[i]) for i in range(len(self.steps))
                       if self.steps[i][0] == en]
                if not sub and en != "gpsimd":
                    continue

                def mk(sub, en):
                    def body(eng):
                        last = {k: 0 for k in keys}
                        for i, (_, fn, inc, key) in sub:
                            for k in keys:
                                if k == en and en in OWN_SKIP:
                                    continue
                                v = pre[i][k]
                                if v > last[k]:
                                    eng.wait_ge(sems[k], v)
                                    last[k] = v
                            fn(eng).then_inc(sems[key], inc)
                        if done is None:
                            return
                        if en != "gpsimd":
                            eng.nop().then_inc(done, 1)
                        else:
                            for k in keys:
                                if cum[k] > last[k] and k != "gpsimd":
                                    eng.wait_ge(sems[k], cum[k])
                            if n_other:
                                eng.wait_ge(done, n_other)
                            for k in keys:
                                if k in NOCLEAR:
                                    continue
                                eng.sem_clear(sems[k])
                            eng.sem_clear(done)
                    return body

                getattr(block, en)(mk(sub, en))


class Alloc:
    def __init__(self, nc, stack, ctr=None):
        self.nc = nc
        self.stack = stack
        self.ctr = ctr if ctr is not None else [0]

    @property
    def n(self):
        return self.ctr[0]

    def _id(self):
        self.ctr[0] += 1
        return self.ctr[0]

    def sb(self, shape, dt=F32, name=None):
        i = self._id()
        name = f"{name}_{i}" if name else f"sb{i}"
        return self.stack.enter_context(self.nc.sbuf_tensor(name, list(shape), dt))

    def sbl(self, k, shape, dt=F32, name=None):
        return [self.sb(shape, dt, name and f"{name}{i}") for i in range(k)]

    def ps(self, shape, name=None, dt=F32):
        i = self._id()
        name = f"{name}_{i}" if name else f"ps{i}"
        return self.stack.enter_context(self.nc.psum_tensor(name, list(shape), dt))

    def sem(self, name=None):
        i = self._id()
        return self.stack.enter_context(self.nc.semaphore(f"{name or 'sem'}_{i}"))


class FuseCtx:

    def __init__(self, nc, al, p, kk, ones_b):
        self.nc = nc
        self.al = al
        self.p = p
        self.kk = kk
        self.ones_b = ones_b
        self.x_d = None
        self.xo_d = None
        self._drams = {}

    def dram(self, name, shape, kind="ExternalInput"):
        if name not in self._drams:
            self._drams[name] = self.nc.dram_tensor(name, list(shape), F32, kind=kind)
        return self._drams[name]

    def prep_wtiled(self, name, noc, free):
        key = name + "_bf"
        if key not in self._drams:
            src = self.dram(name, [noc, 128, free])
            dst = self.nc.dram_tensor(key, [noc, 128, free], BF16)
            self._drams[key] = dst
            for oc in range(noc):
                self.p.cast_dma(dst.ap()[oc], src.ap()[oc])
        return self._drams[key]

    def precast(self, name, shape):
        key = name + "_bf"
        if key not in self._drams:
            src = self.dram(name, shape)
            dst = self.nc.dram_tensor(key, list(shape), BF16)
            self._drams[key] = dst
            flat_r = 1
            for s in shape[:-1]:
                flat_r *= s
            last = shape[-1]
            for r0 in range(0, flat_r, 128):
                rr = min(128, flat_r - r0)
                sv = bass.AP(src, r0 * last, [[last, rr], [1, last]])
                dv = bass.AP(dst, r0 * last, [[last, rr], [1, last]])
                self.p.cast_dma(dv, sv)
        return self._drams[key]

    @contextlib.contextmanager
    def scope(self):
        with contextlib.ExitStack() as stack:
            yield Alloc(self.nc, stack, self.al.ctr)


def _standalone_ctx(stack):
    nc = bass.Bass("TRN2", target_bir_lowering=False)
    x_d = nc.dram_tensor("x", [D, NTOK], F32, kind="ExternalInput")
    xo_d = nc.dram_tensor("xo", [D, NTOK], F32, kind="ExternalOutput")
    al = Alloc(nc, stack)
    p = Prog(nc)
    sems = {k: al.sem(f"ch_{k}") for k in ("c", "ds", "dg")}
    kk = Konst(al, p)
    ones_b = al.sb((128, 1), BF16, "onesb")
    p.add("vector", lambda e: e.memset(ones_b[:], 1.0))
    FC = FuseCtx(nc, al, p, kk, ones_b)
    FC.x_d = x_d
    FC.xo_d = xo_d

    def finish():
        p.emit(sems, al)
        return nc

    return FC, finish


class Konst:

    def __init__(self, al, p, n=24):
        self.tile = al.sb((128, n), F32, "konst")
        self.p = p
        self.map = {}
        self.n = 0

    def __call__(self, v):
        v = float(v)
        if v not in self.map:
            i = self.n
            self.n += 1
            self.map[v] = i
            self.p.add("vector", lambda e, i=i, v=v: e.memset(self.tile[:, i:i + 1], v))
        i = self.map[v]
        return self.tile[:, i:i + 1]


def load_scale_cols(p, dst, src_d, n):
    for c in range(n):
        p.dma(dst[:, c:c + 1], src_d.ap()[c * 128:(c + 1) * 128].unsqueeze(1))


def wcol_load(p, wbuf, w_d, oc, cchunks, rows=None):
    src = w_d.ap()[:, oc * 128:(oc + 1) * 128].rearrange("(c p) o -> p c o", p=128)
    p.cast_dma(wbuf[:].rearrange("p (c o) -> p c o", c=cchunks), src)




class LNBufs:

    def __init__(self, al, nc, nt, tag="", with_x=True):
        self.nt = nt
        self.xt = al.sbl(DC, (128, nt), F32, f"lnx{tag}") if with_x else None
        self.xb = al.sb((128, nt), BF16, f"lnxb{tag}")
        self.sqb = al.sb((128, nt), BF16, f"lnsqb{tag}")
        self.mean = al.sb((1, nt), F32, f"lnmean{tag}")
        self.msq = al.sb((1, nt), F32, f"lnmsq{tag}")
        self.var = al.sb((1, nt), F32, f"lnvar{tag}")
        self.rstd = al.sb((1, nt), F32, f"lnrstd{tag}")
        self.mbb = al.sb((128, nt), F32, f"lnmbb{tag}")
        self.rbb = al.sb((128, nt), F32, f"lnrbb{tag}")
        self.sq = al.sb((128, nt), F32, f"lnsq{tag}")
        self.psA = al.ps((1, nt), f"lnpsA{tag}")
        self.psB = al.ps((1, nt), f"lnpsB{tag}")
        self.rows_d = nc.dram_tensor(f"lnrows{tag}_{al._id()}", [2, nt], F32)


def emit_rowstats(p, lb, ones_b, tiles, nch, ndim, kk, nt=None, eps=1e-5):
    nt = nt or lb.nt
    for c in range(nch):
        p.add("vector", lambda e, c=c: e.tensor_copy(lb.xb[:, 0:nt], tiles[c][:, 0:nt]))
        p.add("tensor", lambda e, c=c: e.matmul(
            lb.psA[0:1, 0:nt], ones_b[:], lb.xb[:, 0:nt],
            start=(c == 0), stop=(c == nch - 1)))
    for c in range(nch):
        p.add("scalar", lambda e, c=c: e.activation(lb.sqb[:, 0:nt], tiles[c][:, 0:nt], AF.Square))
        p.add("tensor", lambda e, c=c: e.matmul(
            lb.psB[0:1, 0:nt], ones_b[:], lb.sqb[:, 0:nt],
            start=(c == 0), stop=(c == nch - 1)))
    p.add("vector", lambda e: e.tensor_scalar_mul(lb.mean[:, 0:nt], lb.psA[0:1, 0:nt], 1.0 / ndim))
    p.add("vector", lambda e: e.tensor_scalar_mul(lb.msq[:, 0:nt], lb.psB[0:1, 0:nt], 1.0 / ndim))
    p.add("vector", lambda e: e.tensor_tensor(lb.var[:, 0:nt], lb.mean[:, 0:nt], lb.mean[:, 0:nt], ALU.mult))
    p.add("vector", lambda e: e.tensor_tensor(lb.var[:, 0:nt], lb.msq[:, 0:nt], lb.var[:, 0:nt], ALU.subtract))
    p.add("vector", lambda e, eps=eps: e.tensor_scalar_add(lb.var[:, 0:nt], lb.var[:, 0:nt], float(eps)))
    p.add("scalar", lambda e: e.activation(lb.rstd[:, 0:nt], lb.var[:, 0:nt], AF.Ln))
    p.add("vector", lambda e: e.tensor_scalar_mul(lb.rstd[:, 0:nt], lb.rstd[:, 0:nt], -0.5))
    p.add("scalar", lambda e: e.activation(lb.rstd[:, 0:nt], lb.rstd[:, 0:nt], AF.Exp))
    p.dma(lb.rows_d.ap()[0:1, 0:nt], lb.mean[:, 0:nt])
    p.dma(lb.rows_d.ap()[1:2, 0:nt], lb.rstd[:, 0:nt])
    p.dma(lb.mbb[:, 0:nt], lb.rows_d.ap()[0:1, 0:nt].broadcast_to([128, nt]))
    p.dma(lb.rbb[:, 0:nt], lb.rows_d.ap()[1:2, 0:nt].broadcast_to([128, nt]))


def emit_ln(p, lb, ones_b, xn_tiles, xn_col0, kk, x_d=None, tok0=0,
            xt=None, nt=None):
    nt = nt or lb.nt
    if xt is None:
        xt = lb.xt
        for c in range(DC):
            p.dma(xt[c][:, 0:nt], x_d.ap()[c * 128:(c + 1) * 128, tok0:tok0 + nt])
    emit_rowstats(p, lb, ones_b, xt, DC, D, kk, nt=nt)
    for c in range(DC):
        p.add("vector", lambda e, c=c: e.tensor_tensor(lb.sq[:, 0:nt], xt[c][:, 0:nt], lb.mbb[:, 0:nt], ALU.subtract))
        p.add("vector", lambda e, c=c: e.tensor_tensor(
            xn_tiles[c][:, xn_col0:xn_col0 + nt], lb.sq[:, 0:nt], lb.rbb[:, 0:nt], ALU.mult))
    return xt



DBG = False


def _tap(p, nc, name, src_ap, shape, q="sync"):
    if not DBG:
        return
    d = nc.dram_tensor(f"dbg_{name}", list(shape), F32, kind="ExternalOutput")
    if src_ap.dtype == F32:
        p.dma(d.ap(), src_ap, q=q)
    else:
        p.cast_dma(d.ap(), src_ap)


def build_ffn0(FC=None):
    standalone = FC is None
    with contextlib.ExitStack() as outer:
        if standalone:
            FC, _finish = _standalone_ctx(outer)
        nc, p, kk, ones_b = FC.nc, FC.p, FC.kk, FC.ones_b
        x_d, xo_d = FC.x_d, FC.xo_d
        w1_b = FC.prep_wtiled("w1Tt", 32, 1024)
        w2_b = FC.prep_wtiled("w2Tt", 32, 1024)
        h1_b = FC.prep_wtiled("h1Tt", 32, 4096)
        h2_b = FC.prep_wtiled("h2Tt", 32, 4096)
        w3_b = FC.prep_wtiled("w3Tt", 8, 4096)
        s1_d = FC.dram("w1s", [4096])
        s2_d = FC.dram("w2s", [4096])
        sh1_d = FC.dram("h1s", [4096])
        sh2_d = FC.dram("h2s", [4096])
        s3_d = FC.dram("w3s", [1024])

        NT = 512
        NB = NTOK // NT
        scope_cm = FC.scope()
        al = scope_cm.__enter__()
        lb = LNBufs(al, nc, NT)
        xn = al.sbl(DC, (128, NT), BF16, "xn")
        h_a = al.sbl(32, (128, NT), BF16, "ha")
        h_b = al.sbl(32, (128, NT), BF16, "hb")
        silu_t = al.sb((128, NT), F32, "silu")
        yo = al.sb((128, NT), F32, "yo")
        wbuf = al.sb((128, 1024), BF16, "wbuf")
        wbuf2 = al.sb((128, 1024), BF16, "wbuf2")
        whbuf = al.sb((128, 4096), BF16, "whbuf")
        s1 = al.sb((128, 32), F32, "s1")
        s2 = al.sb((128, 32), F32, "s2")
        sh1 = al.sb((128, 32), F32, "sh1")
        sh2 = al.sb((128, 32), F32, "sh2")
        s3 = al.sb((128, 8), F32, "s3")
        ps1 = al.ps((128, NT), "ps1")
        ps2 = al.ps((128, NT), "ps2")

        load_scale_cols(p, s1, s1_d, 32)
        load_scale_cols(p, s2, s2_d, 32)
        load_scale_cols(p, sh1, sh1_d, 32)
        load_scale_cols(p, sh2, sh2_d, 32)
        load_scale_cols(p, s3, s3_d, 8)

        for tb in range(NB):
            col0 = tb * NT
            xt = emit_ln(p, lb, ones_b, xn, 0, kk, x_d=x_d, tok0=col0)
            if tb == 0:
                _tap(p, nc, "mean", lb.mean[:], (1, NT))
                _tap(p, nc, "rstd", lb.rstd[:], (1, NT))
                _tap(p, nc, "xn0", xn[0][:], (128, NT))
                _tap(p, nc, "s1col", s1[:], (128, 32))
            for oc in range(32):
                p.dma(wbuf[:], w1_b.ap()[oc])
                for c in range(DC):
                    p.add("tensor", lambda e, c=c: e.matmul(
                        ps1[:], wbuf[:, c * 128:(c + 1) * 128], xn[c][:],
                        start=(c == 0), stop=(c == DC - 1)))
                p.add("scalar", lambda e, oc=oc: e.activation(
                    silu_t[:], ps1[:], AF.Silu, scale=s1[:, oc:oc + 1]))
                p.dma(wbuf2[:], w2_b.ap()[oc])
                for c in range(DC):
                    p.add("tensor", lambda e, c=c: e.matmul(
                        ps2[:], wbuf2[:, c * 128:(c + 1) * 128], xn[c][:],
                        start=(c == 0), stop=(c == DC - 1)))
                p.add("vector", lambda e, oc=oc: e.scalar_tensor_tensor(
                    h_a[oc][:], ps2[:], s2[:, oc:oc + 1], silu_t[:], ALU.mult, ALU.mult))
                if tb == 0 and oc == 0:
                    _tap(p, nc, "silu0", silu_t[:], (128, NT))
                    _tap(p, nc, "ha0", h_a[0][:], (128, NT))
                    _tap(p, nc, "wbuf0", wbuf[:], (128, 1024))
            for oc in range(32):
                p.dma(whbuf[:], h1_b.ap()[oc])
                for c in range(32):
                    p.add("tensor", lambda e, c=c: e.matmul(
                        ps1[:], whbuf[:, c * 128:(c + 1) * 128], h_a[c][:],
                        start=(c == 0), stop=(c == 31)))
                p.add("scalar", lambda e, oc=oc: e.activation(
                    h_b[oc][:], ps1[:], AF.Gelu, scale=sh1[:, oc:oc + 1]))
            for oc in range(32):
                p.dma(whbuf[:], h2_b.ap()[oc])
                for c in range(32):
                    p.add("tensor", lambda e, c=c: e.matmul(
                        ps1[:], whbuf[:, c * 128:(c + 1) * 128], h_b[c][:],
                        start=(c == 0), stop=(c == 31)))
                p.add("scalar", lambda e, oc=oc: e.activation(
                    h_a[oc][:], ps1[:], AF.Gelu, scale=sh2[:, oc:oc + 1]))
            for oc in range(DC):
                p.dma(whbuf[:], w3_b.ap()[oc])
                for c in range(32):
                    p.add("tensor", lambda e, c=c: e.matmul(
                        ps1[:], whbuf[:, c * 128:(c + 1) * 128], h_a[c][:],
                        start=(c == 0), stop=(c == 31)))
                p.add("vector", lambda e, oc=oc: e.scalar_tensor_tensor(
                    yo[:], ps1[:], s3[:, oc:oc + 1], xt[oc][:], ALU.mult, ALU.add))
                p.dma(xo_d.ap()[oc * 128:(oc + 1) * 128, col0:col0 + NT], yo[:])

        scope_cm.__exit__(None, None, None)
        if standalone:
            return _finish()


def ffn0_inputs(shared):
    return {k: shared[k] for k in ("w1Tt", "w2Tt", "h1Tt", "h2Tt", "w3Tt",
                                   "w1s", "w2s", "h1s", "h2s", "w3s")}




def build_attn_a(FC=None):
    standalone = FC is None
    with contextlib.ExitStack() as outer:
        if standalone:
            FC, _finish = _standalone_ctx(outer)
        nc, p, kk, ones_b = FC.nc, FC.p, FC.kk, FC.ones_b
        x_d, xo_d = FC.x_d, FC.xo_d
        wq_b = FC.prep_wtiled("aa_Wqt", 8, 1024)
        wk_b = FC.prep_wtiled("aa_Wkt", 8, 1024)
        wv_b = FC.precast("aa_Wv", [D, D])
        wo_b = FC.prep_wtiled("aa_Wot", 8, 1024)
        bo_d = FC.dram("aa_bo", [D])
        th_d = FC.dram("aa_theta", [1, 8])
        ga_d = FC.dram("aa_gamma", [D])
        be_d = FC.dram("aa_beta", [D])
        m128_d = FC.dram("m128", [128, 128])
        decay_d = nc.dram_tensor("decay", [T, T], F32)

        NH, HD = 8, 128
        NTT = T // 128
        scope_cm = FC.scope()
        al = scope_cm.__enter__()
        lb = LNBufs(al, nc, 512, with_x=False)
        xnb = al.sbl(DC, (128, T), BF16, "xnb")
        xtb = al.sbl(DC, (128, 512), BF16, "xtb")
        q_fm = al.sbl(NH, (128, T), BF16, "qfm")
        k_fm = al.sbl(NH, (128, T), BF16, "kfm")
        v_tm = al.sbl(NTT, (128, D), BF16, "vtm")
        u16k = al.sb((128, 8192), BF16, "u16k")
        wbuf = al.sb((128, 1024), BF16, "wbuf")
        dec = al.sb((128, 1024), F32, "dec")
        srow = al.sb((128, T), F32, "srow")
        p_bf = al.sb((128, T), BF16, "pbf")
        a_f32 = al.sb((128, 512), F32, "af32")
        e_t = al.sb((128, 512), F32, "et")
        acc_t = al.sb((128, 512), F32, "acct")
        n_i = al.sb((128, 512), mybir.dt.int32, "ni")
        nf = al.sb((128, 512), F32, "nf")
        ngt = al.sb((128, 8), F32, "ngt")
        m128 = al.sb((128, 128), F32, "m128s")
        nm = al.sb((128, 1), F32, "nm")
        zrow = al.sb((128, 1), F32, "zrow")
        rz = al.sb((128, 1), F32, "rz")
        bo_c = al.sb((128, 8), F32, "boc")
        ga_c = al.sb((128, 8), F32, "gac")
        be_c = al.sb((128, 8), F32, "bec")
        xold = al.sb((128, 1024), F32, "xold")
        t1 = al.sb((128, 1024), F32, "t1")
        ps_big = al.ps((128, 2048), "psbig")
        ps_pv = al.ps((128, 512), "pspv")

        p.dma(m128[:], m128_d.ap())
        p.dma(ngt[:], th_d.ap().broadcast_to([128, 8]))
        p.add("vector", lambda e: e.tensor_scalar_mul(ngt[:], ngt[:], -1.0))
        load_scale_cols(p, bo_c, bo_d, 8)
        load_scale_cols(p, ga_c, ga_d, 8)
        load_scale_cols(p, be_c, be_d, 8)

        for tt in range(NTT):
            t_end = (tt + 1) * 128
            for sb in range((t_end + 511) // 512):
                w = min(512, t_end - sb * 512)
                p.add("gpsimd", lambda e, tt=tt, sb=sb, w=w: e.iota(
                    n_i[:, 0:w], [[-1, w]], base=tt * 128 - sb * 512,
                    channel_multiplier=1))
                p.add("vector", lambda e, w=w: e.tensor_copy(nf[:, 0:w], n_i[:, 0:w]))
                p.add("vector", lambda e, w=w: e.tensor_scalar_max(nf[:, 0:w], nf[:, 0:w], 0.0))
                for h in range(NH):
                    if h == 0:
                        p.add("scalar", lambda e, w=w, h=h: e.activation(
                            acc_t[:, 0:w], nf[:, 0:w], AF.Exp, scale=ngt[:, h:h + 1]))
                    else:
                        p.add("scalar", lambda e, w=w, h=h: e.activation(
                            e_t[:, 0:w], nf[:, 0:w], AF.Exp, scale=ngt[:, h:h + 1]))
                        p.add("vector", lambda e, w=w: e.tensor_add(
                            acc_t[:, 0:w], acc_t[:, 0:w], e_t[:, 0:w]))
                p.add("vector", lambda e, w=w: e.tensor_scalar_mul(
                    acc_t[:, 0:w], acc_t[:, 0:w], 1.0 / (NH * float(np.sqrt(HD)))))
                p.dma(decay_d.ap()[tt * 128:(tt + 1) * 128, sb * 512:sb * 512 + w],
                      acc_t[:, 0:w])

        for b in range(BPC):
            tok0 = b * T
            for tb in range(T // 512):
                for c in range(DC):
                    p.cast_dma(xtb[c][:], x_d.ap()[c * 128:(c + 1) * 128,
                                                   tok0 + tb * 512:tok0 + (tb + 1) * 512])
                emit_ln(p, lb, ones_b, xnb, tb * 512, kk, xt=xtb, nt=512)
            for w_b, dst in ((wq_b, q_fm), (wk_b, k_fm)):
                for oc in range(NH):
                    p.dma(wbuf[:], w_b.ap()[oc])
                    for tb in range(4):
                        for c in range(DC):
                            p.add("tensor", lambda e, oc=oc, tb=tb, c=c, dst=dst: e.matmul(
                                ps_big[:, 0:512], wbuf[:, c * 128:(c + 1) * 128],
                                xnb[c][:, tb * 512:(tb + 1) * 512],
                                start=(c == 0), stop=(c == DC - 1)))
                        p.add("vector", lambda e, oc=oc, tb=tb, dst=dst: e.tensor_copy(
                            dst[oc][:, tb * 512:(tb + 1) * 512], ps_big[:, 0:512]))
            for c in range(DC):
                p.dma(u16k[:, c * 1024:(c + 1) * 1024],
                      wv_b.ap()[c * 128:(c + 1) * 128, :])
            for tt in range(NTT):
                for half in range(2):
                    for c in range(DC):
                        p.add("tensor", lambda e, tt=tt, c=c, half=half: e.matmul(
                            ps_big[:, half * 512:(half + 1) * 512],
                            xnb[c][:, tt * 128:(tt + 1) * 128],
                            u16k[:, c * 1024 + half * 512:c * 1024 + (half + 1) * 512],
                            start=(c == 0), stop=(c == DC - 1)))
                    p.add("vector", lambda e, tt=tt, half=half: e.tensor_copy(
                        v_tm[tt][:, half * 512:(half + 1) * 512],
                        ps_big[:, half * 512:(half + 1) * 512]))

            for h in range(NH):
                for g in range(4):
                    g_end = (g + 1) * 512
                    nsc = g_end // 128
                    for ttg in range(4):
                        tt = g * 4 + ttg
                        t_end = (tt + 1) * 128
                        for sb in range((t_end + 511) // 512):
                            w = min(512, t_end - sb * 512)
                            p.add("tensor", lambda e, h=h, tt=tt, sb=sb, w=w: e.matmul(
                                ps_big[:, sb * 512:sb * 512 + w],
                                q_fm[h][:, tt * 128:(tt + 1) * 128],
                                k_fm[h][:, sb * 512:sb * 512 + w],
                                start=True, stop=True))
                        for half in range((t_end + 1023) // 1024):
                            w = min(1024, t_end - half * 1024)
                            p.dma(dec[:, 0:w],
                                  decay_d.ap()[tt * 128:(tt + 1) * 128,
                                               half * 1024:half * 1024 + w])
                            p.add("vector", lambda e, half=half, w=w: e.tensor_tensor(
                                srow[:, half * 1024:half * 1024 + w],
                                ps_big[:, half * 1024:half * 1024 + w],
                                dec[:, 0:w], ALU.mult))
                        p.add("vector", lambda e, tt=tt, t_end=t_end: e.tensor_add(
                            srow[:, tt * 128:t_end], srow[:, tt * 128:t_end], m128[:]))
                        p.add("vector", lambda e, t_end=t_end: e.reduce_max(
                            nm[:], srow[:, 0:t_end], axis=AX.X, negate=True))
                        p.add("scalar", lambda e, t_end=t_end: e.activation(
                            srow[:, 0:t_end], srow[:, 0:t_end], AF.Exp,
                            bias=nm[:], accum_out=zrow[:]))
                        p.add("vector", lambda e: e.reciprocal(rz[:], zrow[:]))
                        p.add("vector", lambda e, t_end=t_end: e.tensor_scalar_mul(
                            p_bf[:, 0:t_end], srow[:, 0:t_end], rz[:]))
                        if t_end < g_end:
                            p.add("vector", lambda e, t_end=t_end, g_end=g_end: e.memset(
                                p_bf[:, t_end:g_end], 0.0))
                        for sc in range(nsc):
                            p.dma(u16k[:, sc * 512 + ttg * 128:sc * 512 + (ttg + 1) * 128],
                                  p_bf[:, sc * 128:(sc + 1) * 128], transpose=True)
                    for sc in range(nsc):
                        p.add("tensor", lambda e, h=h, sc=sc, nsc=nsc: e.matmul(
                            ps_pv[:], v_tm[sc][:, h * 128:(h + 1) * 128],
                            u16k[:, sc * 512:(sc + 1) * 512],
                            start=(sc == 0), stop=(sc == nsc - 1)))
                    p.add("vector", lambda e: e.tensor_copy(a_f32[:], ps_pv[:]))
                    emit_rowstats(p, lb, ones_b, [a_f32], 1, 128, kk, nt=512)
                    p.add("vector", lambda e: e.tensor_tensor(
                        lb.sq[:, 0:512], a_f32[:], lb.mbb[:, 0:512], ALU.subtract))
                    p.add("vector", lambda e: e.tensor_tensor(
                        lb.sq[:, 0:512], lb.sq[:, 0:512], lb.rbb[:, 0:512], ALU.mult))
                    p.add("vector", lambda e, h=h, g=g: e.tensor_scalar(
                        xnb[h][:, g * 512:(g + 1) * 512], lb.sq[:, 0:512],
                        ga_c[:, h:h + 1], be_c[:, h:h + 1], ALU.mult, ALU.add))

            for oc in range(DC):
                p.dma(wbuf[:], wo_b.ap()[oc])
                for tb in range(4):
                    for c in range(DC):
                        p.add("tensor", lambda e, oc=oc, tb=tb, c=c: e.matmul(
                            ps_big[:, 0:512], wbuf[:, c * 128:(c + 1) * 128],
                            xnb[c][:, tb * 512:(tb + 1) * 512],
                            start=(c == 0), stop=(c == DC - 1)))
                    p.add("vector", lambda e, oc=oc: e.tensor_scalar_add(
                        t1[:, 0:512], ps_big[:, 0:512], bo_c[:, oc:oc + 1]))
                    p.dma(xold[:, 0:512], x_d.ap()[oc * 128:(oc + 1) * 128,
                                                   tok0 + tb * 512:tok0 + (tb + 1) * 512])
                    p.add("vector", lambda e: e.tensor_add(t1[:, 0:512], t1[:, 0:512], xold[:, 0:512]))
                    p.dma(xo_d.ap()[oc * 128:(oc + 1) * 128,
                                    tok0 + tb * 512:tok0 + (tb + 1) * 512], t1[:, 0:512])
        scope_cm.__exit__(None, None, None)
        if standalone:
            return _finish()


def attn_a_inputs(sh):
    return {k: sh[k] for k in ("aa_Wqt", "aa_Wkt", "aa_Wv", "aa_Wot", "aa_bo",
                               "aa_theta", "aa_gamma", "aa_beta", "m128")}




def build_mamba(FC=None):
    standalone = FC is None
    with contextlib.ExitStack() as outer:
        if standalone:
            FC, _finish = _standalone_ctx(outer)
        nc, p, kk, ones_b = FC.nc, FC.p, FC.kk, FC.ones_b
        x_d, xo_d = FC.x_d, FC.xo_d
        win_b = FC.prep_wtiled("mb_WinTt", 32, 1024)
        wdt_b = FC.prep_wtiled("mb_WdtTt", 16, 2048)
        wout_b = FC.prep_wtiled("mb_WoutTt", 8, 2048)
        cw_d = FC.dram("conv_w", [2048, 4])
        cb_d = FC.dram("conv_b", [2048])
        alog_d = FC.dram("A_log", [2048, 16])
        bp_d = FC.dram("B_param", [2048, 16])
        cp_d = FC.dram("C_param", [2048, 16])
        dp_d = FC.dram("D_param", [2048])
        bdt_d = FC.dram("b_dt", [2048])
        dtb_d = FC.dram("dt_bias", [2048])

        NI = 16
        LB = 512
        NLB = T // LB
        scope_cm = FC.scope()
        al = scope_cm.__enter__()
        lb_ = LNBufs(al, nc, 512, with_x=False)
        xtb = al.sbl(DC, (128, 512), BF16, "xtb")
        xnb = al.sbl(DC, (128, T), BF16, "xnb")
        xbp = al.sbl(NI, (128, 3 + LB), F32, "xbp")
        u_f = al.sbl(NI, (128, LB), F32, "uf")
        u_b = al.sbl(NI, (128, LB), BF16, "ub")
        y_b = al.sbl(NI, (128, LB), BF16, "yb")
        carry = al.sb((128, NI * 16), F32, "carry")
        cw = al.sb((128, 64), F32, "cw")
        cb_c = al.sb((128, 16), F32, "cbc")
        A_sb = al.sb((128, 256), F32, "Asb")
        B_sb = al.sb((128, 256), F32, "Bsb")
        C_sb = al.sb((128, 256), F32, "Csb")
        D_c = al.sb((128, 16), F32, "Dc")
        bias_c = al.sb((128, 16), F32, "biasc")
        tmp16 = al.sb((128, 16), F32, "tmp16")
        wbuf = al.sb((128, 1024), BF16, "wbuf")
        wdtbuf = al.sb((128, 2048), BF16, "wdtbuf")
        sdt = al.sb((128, LB), F32, "sdt")
        tA = al.sb((128, LB), F32, "tA")
        tL = al.sb((128, LB), F32, "tL")
        dt_t = al.sb((128, LB), F32, "dtt")
        wdu = al.sb((128, LB), F32, "wdu")
        dA_t = al.sb((128, LB), F32, "dAt")
        dBu_t = al.sb((128, LB), F32, "dBut")
        s_t = al.sb((128, LB), F32, "st")
        y0 = al.sb((128, LB), F32, "y0")
        y1 = al.sb((128, LB), F32, "y1")
        g_t = al.sb((128, LB), F32, "gt")
        xold = al.sb((128, LB), F32, "xold")
        t1 = al.sb((128, LB), F32, "t1")
        ps1 = al.ps((128, 512), "ps1")
        ps2 = al.ps((128, 512), "ps2")

        load_scale_cols(p, cb_c, cb_d, 16)
        load_scale_cols(p, D_c, dp_d, 16)
        load_scale_cols(p, bias_c, bdt_d, 16)
        load_scale_cols(p, tmp16, dtb_d, 16)
        p.add("vector", lambda e: e.tensor_add(bias_c[:], bias_c[:], tmp16[:]))
        for c in range(NI):
            p.dma(cw[:, c * 4:(c + 1) * 4], cw_d.ap()[c * 128:(c + 1) * 128, :])
            p.dma(A_sb[:, c * 16:(c + 1) * 16], alog_d.ap()[c * 128:(c + 1) * 128, :])
            p.dma(B_sb[:, c * 16:(c + 1) * 16], bp_d.ap()[c * 128:(c + 1) * 128, :])
            p.dma(C_sb[:, c * 16:(c + 1) * 16], cp_d.ap()[c * 128:(c + 1) * 128, :])
        p.add("scalar", lambda e: e.activation(A_sb[:], A_sb[:], AF.Exp))
        p.add("vector", lambda e: e.tensor_scalar_mul(A_sb[:], A_sb[:], -1.0))

        for b in range(BPC):
            tok0 = b * T
            for tb in range(T // 512):
                for c in range(DC):
                    p.cast_dma(xtb[c][:], x_d.ap()[c * 128:(c + 1) * 128,
                                                   tok0 + tb * 512:tok0 + (tb + 1) * 512])
                emit_ln(p, lb_, ones_b, xnb, tb * 512, kk, xt=xtb, nt=512)

            for li in range(NLB):
                l0 = li * LB
                for oc in range(NI):
                    if li == 0:
                        p.add("vector", lambda e, oc=oc: e.memset(xbp[oc][:, 0:3], 0.0))
                    else:
                        p.add("vector", lambda e, oc=oc: e.tensor_copy(
                            xbp[oc][:, 0:3], xbp[oc][:, LB:LB + 3]))
                for oc in range(NI):
                    p.dma(wbuf[:], win_b.ap()[oc])
                    for c in range(DC):
                        p.add("tensor", lambda e, oc=oc, c=c, l0=l0: e.matmul(
                            ps1[:], wbuf[:, c * 128:(c + 1) * 128],
                            xnb[c][:, l0:l0 + LB], start=(c == 0), stop=(c == DC - 1)))
                    p.add("vector", lambda e, oc=oc: e.tensor_copy(xbp[oc][:, 3:3 + LB], ps1[:]))
                    p.add("vector", lambda e, oc=oc: e.tensor_scalar_mul(
                        y0[:], xbp[oc][:, 0:LB], cw[:, oc * 4:oc * 4 + 1]))
                    for kkk in range(1, 4):
                        p.add("vector", lambda e, oc=oc, kkk=kkk: e.scalar_tensor_tensor(
                            y0[:], xbp[oc][:, kkk:kkk + LB], cw[:, oc * 4 + kkk:oc * 4 + kkk + 1],
                            y0[:], ALU.mult, ALU.add))
                    p.add("scalar", lambda e, oc=oc: e.activation(
                        u_f[oc][:], y0[:], AF.Silu, bias=cb_c[:, oc:oc + 1]))
                    p.add("vector", lambda e, oc=oc: e.tensor_copy(u_b[oc][:], u_f[oc][:]))

                for oc in range(NI):
                    p.dma(wdtbuf[:], wdt_b.ap()[oc])
                    for c in range(NI):
                        p.add("tensor", lambda e, oc=oc, c=c: e.matmul(
                            ps1[:], wdtbuf[:, c * 128:(c + 1) * 128], u_b[c][:],
                            start=(c == 0), stop=(c == NI - 1)))
                    p.add("vector", lambda e, oc=oc: e.tensor_scalar_add(
                        sdt[:], ps1[:], bias_c[:, oc:oc + 1]))
                    k_m1 = kk(-1.0)
                    k_p1 = kk(1.0)
                    p.add("scalar", lambda e: e.activation(tA[:], sdt[:], AF.Abs))
                    p.add("scalar", lambda e, k_m1=k_m1: e.activation(tA[:], tA[:], AF.Exp, scale=k_m1))
                    p.add("scalar", lambda e, k_p1=k_p1: e.activation(tL[:], tA[:], AF.Ln, bias=k_p1))
                    p.add("vector", lambda e: e.tensor_scalar_max(dt_t[:], sdt[:], 0.0))
                    p.add("vector", lambda e: e.tensor_add(dt_t[:], dt_t[:], tL[:]))
                    p.add("vector", lambda e, oc=oc: e.tensor_tensor(
                        wdu[:], dt_t[:], u_f[oc][:], ALU.mult))
                    for n in range(16):
                        cn = oc * 16 + n
                        p.add("scalar", lambda e, cn=cn: e.activation(
                            dA_t[:], dt_t[:], AF.Exp, scale=A_sb[:, cn:cn + 1]))
                        p.add("vector", lambda e, cn=cn: e.tensor_scalar_mul(
                            dBu_t[:], wdu[:], B_sb[:, cn:cn + 1]))
                        if li == 0:
                            p.add("vector", lambda e: e.tensor_tensor_scan(
                                s_t[:], dA_t[:], dBu_t[:], 0.0, ALU.mult, ALU.add))
                        else:
                            p.add("vector", lambda e, cn=cn: e.tensor_tensor_scan(
                                s_t[:], dA_t[:], dBu_t[:], carry[:, cn:cn + 1],
                                ALU.mult, ALU.add))
                        p.add("vector", lambda e, cn=cn: e.tensor_copy(
                            carry[:, cn:cn + 1], s_t[:, LB - 1:LB]))
                        if n == 0:
                            p.add("vector", lambda e, cn=cn: e.tensor_scalar_mul(
                                y0[:], s_t[:], C_sb[:, cn:cn + 1]))
                        else:
                            src_y, dst_y = (y0, y1) if n % 2 == 1 else (y1, y0)
                            p.add("vector", lambda e, cn=cn, s=src_y, d=dst_y: e.scalar_tensor_tensor(
                                d[:], s_t[:], C_sb[:, cn:cn + 1], s[:], ALU.mult, ALU.add))
                    y_fin = y1 if 15 % 2 == 1 else y0
                    p.dma(wbuf[:], win_b.ap()[16 + oc])
                    for c in range(DC):
                        p.add("tensor", lambda e, oc=oc, c=c, l0=l0: e.matmul(
                            ps2[:], wbuf[:, c * 128:(c + 1) * 128],
                            xnb[c][:, l0:l0 + LB], start=(c == 0), stop=(c == DC - 1)))
                    p.add("scalar", lambda e: e.activation(g_t[:], ps2[:], AF.Silu))
                    p.add("vector", lambda e, oc=oc, y_fin=y_fin: e.scalar_tensor_tensor(
                        t1[:], u_f[oc][:], D_c[:, oc:oc + 1], y_fin[:], ALU.mult, ALU.add))
                    p.add("vector", lambda e, oc=oc: e.tensor_tensor(
                        y_b[oc][:], t1[:], g_t[:], ALU.mult))

                for oc in range(DC):
                    p.dma(wdtbuf[:], wout_b.ap()[oc])
                    for c in range(NI):
                        p.add("tensor", lambda e, oc=oc, c=c: e.matmul(
                            ps1[:], wdtbuf[:, c * 128:(c + 1) * 128], y_b[c][:],
                            start=(c == 0), stop=(c == NI - 1)))
                    p.dma(xold[:], x_d.ap()[oc * 128:(oc + 1) * 128, tok0 + l0:tok0 + l0 + LB])
                    p.add("vector", lambda e: e.tensor_tensor(t1[:], ps1[:], xold[:], ALU.add))
                    p.dma(xo_d.ap()[oc * 128:(oc + 1) * 128, tok0 + l0:tok0 + l0 + LB], t1[:])
        scope_cm.__exit__(None, None, None)
        if standalone:
            return _finish()


def mamba_inputs(sh):
    return {
        "mb_WinTt": sh["mb_WinTt"], "mb_WdtTt": sh["mb_WdtTt"], "mb_WoutTt": sh["mb_WoutTt"],
        "conv_w": sh["mb_conv_w"], "conv_b": sh["mb_conv_b"], "A_log": sh["mb_A_log"],
        "B_param": sh["mb_B"], "C_param": sh["mb_C"], "D_param": sh["mb_D"],
        "b_dt": sh["mb_b_dt"], "dt_bias": sh["mb_dt_bias"],
    }




def build_rails(FC=None):
    standalone = FC is None
    with contextlib.ExitStack() as outer:
        if standalone:
            FC, _finish = _standalone_ctx(outer)
        nc, p, kk, ones_b = FC.nc, FC.p, FC.kk, FC.ones_b
        x_d, xo_d = FC.x_d, FC.xo_d
        rm_b = FC.precast("rmBD", [4, 8, 128, 128])
        rup_b = FC.precast("rupBD", [4, 8, 128, 512])
        wc_b = FC.precast("wcT", [16384, 1024])

        NT = 512
        scope_cm = FC.scope()
        al = scope_cm.__enter__()
        lb = LNBufs(al, nc, NT)
        xn = al.sbl(DC, (128, NT), BF16, "xn")
        rm_sb = al.sb((128, 4096), BF16, "rmsb")
        rup_sb = al.sb((128, 16384), BF16, "rupsb")
        y_sb = al.sbl(32, (128, NT), BF16, "ysb")
        z_scr = al.sb((128, NT), BF16, "zscr")
        wc_sb = al.sb((128, 512), BF16, "wcsb")
        t1 = al.sb((128, NT), F32, "t1")
        psz = al.ps((128, 512), "psz")
        ps_o = al.ps((128, 2048), "pso")

        for h in range(4):
            for rb in range(8):
                i = h * 8 + rb
                p.dma(rm_sb[:, i * 128:(i + 1) * 128], rm_b.ap()[h, rb])
                p.dma(rup_sb[:, i * 512:(i + 1) * 512], rup_b.ap()[h, rb])

        for tb in range(NTOK // NT):
            col0 = tb * NT
            xt = emit_ln(p, lb, ones_b, xn, 0, kk, x_d=x_d, tok0=col0)
            for i in range(32):
                rb = i % 8
                p.add("tensor", lambda e, i=i, rb=rb: e.matmul(
                    psz[:], rm_sb[:, i * 128:(i + 1) * 128], xn[rb][:],
                    start=True, stop=True))
                p.add("vector", lambda e, i=i: e.tensor_copy(y_sb[i][:], psz[:]))
            for half in range(2):
                p.dma(wc_sb[:], wc_b.ap()[0:128, half * 512:(half + 1) * 512])
                for ch in range(128):
                    i, zt = ch // 4, ch % 4
                    p.add("tensor", lambda e, i=i, zt=zt: e.matmul(
                        psz[:], rup_sb[:, i * 512 + zt * 128:i * 512 + (zt + 1) * 128],
                        y_sb[i][:], start=True, stop=True))
                    p.add("scalar", lambda e: e.activation(z_scr[:], psz[:], AF.Gelu))
                    if ch > 0:
                        p.dma(wc_sb[:], wc_b.ap()[ch * 128:(ch + 1) * 128,
                                                  half * 512:(half + 1) * 512])
                    for oi in range(4):
                        p.add("tensor", lambda e, ch=ch, oi=oi: e.matmul(
                            ps_o[:, oi * 512:(oi + 1) * 512],
                            wc_sb[:, oi * 128:(oi + 1) * 128], z_scr[:],
                            start=(ch == 0), stop=(ch == 127)))
                for oi in range(4):
                    oc = half * 4 + oi
                    p.dma(lb.sq[:, 0:NT], x_d.ap()[oc * 128:(oc + 1) * 128, col0:col0 + NT])
                    p.add("vector", lambda e, oi=oi: e.tensor_tensor(
                        t1[:], ps_o[:, oi * 512:(oi + 1) * 512], lb.sq[:, 0:NT], ALU.add))
                    p.dma(xo_d.ap()[oc * 128:(oc + 1) * 128, col0:col0 + NT], t1[:])
        scope_cm.__exit__(None, None, None)
        if standalone:
            return _finish()


def rails_inputs(sh):
    return {"rmBD": sh["rf_rmBD"], "rupBD": sh["rf_rupBD"], "wcT": sh["rf_wcT"]}




def build_attn_a1(FC=None):
    standalone = FC is None
    with contextlib.ExitStack() as outer:
        if standalone:
            FC, _finish = _standalone_ctx(outer)
        nc, p, kk, ones_b = FC.nc, FC.p, FC.kk, FC.ones_b
        x_d, xo_d = FC.x_d, FC.xo_d
        wq_b2 = FC.precast("ma_Wq", [D, D])
        bq_d = FC.dram("ma_bq", [1, D])
        wd_b = FC.prep_wtiled("ma_Wdt", 2, 1024)
        bd_d = FC.dram("ma_bdown", [256])
        wu_b2 = FC.precast("ma_Wup", [256, 2048])
        bu_d = FC.dram("ma_bup", [1, 2048])
        wo_b2 = FC.prep_wtiled("ma_Wot", 8, 1024)
        bo_d = FC.dram("ma_bout", [D])
        cos_d = FC.dram("ma_cos", [T, 32])
        sin_d = FC.dram("ma_sin", [T, 32])

        scope_cm = FC.scope()
        al = scope_cm.__enter__()
        lb = LNBufs(al, nc, 512)
        xn = al.sbl(DC, (128, NTOK), BF16, "xn")
        lat = al.sbl(2, (128, NTOK), BF16, "lat")
        wq_sb = al.sbl(DC, (128, 1024), BF16, "wqsb")
        wu_sb = al.sbl(2, (128, 2048), BF16, "wusb")
        wbuf = al.sb((128, 1024), BF16, "wbuf")
        bq_b = al.sb((128, 1024), F32, "bqb")
        bu_b = al.sb((128, 2048), F32, "bub")
        bd_c = al.sb((128, 2), F32, "bdc")
        bo_c = al.sb((128, 8), F32, "boc")
        cos_t = al.sb((128, 32), F32, "cost")
        sin_t = al.sb((128, 32), F32, "sint")
        q_t = al.sb((128, 1024), F32, "qt")
        kv_t = al.sb((128, 2048), F32, "kvt")
        q_ro = al.sb((128, 1024), F32, "qro")
        k_ro = al.sb((128, 1024), F32, "kro")
        v_c = al.sb((128, 1024), F32, "vc")
        tr1 = al.sb((128, 512), F32, "tr1")
        tr2 = al.sb((128, 512), F32, "tr2")
        S1 = al.sb((128, 256), F32, "S1")
        S2 = al.sb((128, 256), F32, "S2")
        nmx = al.sb((128, 16), F32, "nmx")
        zs = al.sb((128, 16), F32, "zs")
        rzz = al.sb((128, 16), F32, "rzz")
        o_acc = al.sb((128, 1024), F32, "oacc")
        tmps = al.sb((128, 1024), F32, "tmps")
        o_bf = al.sb((128, 1024), BF16, "obf")
        a_tr = al.sb((128, 1024), BF16, "atr")
        xold = al.sb((128, 512), F32, "xold")
        t1 = al.sb((128, 512), F32, "t1")
        ps1 = al.ps((128, 512), "ps1")

        p.dma(bq_b[:], bq_d.ap().broadcast_to([128, 1024]))
        p.dma(bu_b[:], bu_d.ap().broadcast_to([128, 2048]))
        load_scale_cols(p, bd_c, bd_d, 2)
        load_scale_cols(p, bo_c, bo_d, 8)
        for c in range(DC):
            p.dma(wq_sb[c][:], wq_b2.ap()[c * 128:(c + 1) * 128, :])
        for c in range(2):
            p.dma(wu_sb[c][:], wu_b2.ap()[c * 128:(c + 1) * 128, :])

        for tb in range(NTOK // 512):
            emit_ln(p, lb, ones_b, xn, tb * 512, kk, x_d=x_d, tok0=tb * 512)
        for oc in range(2):
            p.dma(wbuf[:], wd_b.ap()[oc])
            for tb in range(NTOK // 512):
                for c in range(DC):
                    p.add("tensor", lambda e, oc=oc, tb=tb, c=c: e.matmul(
                        ps1[:], wbuf[:, c * 128:(c + 1) * 128],
                        xn[c][:, tb * 512:(tb + 1) * 512],
                        start=(c == 0), stop=(c == DC - 1)))
                p.add("vector", lambda e, oc=oc, tb=tb: e.tensor_scalar_add(
                    lat[oc][:, tb * 512:(tb + 1) * 512], ps1[:], bd_c[:, oc:oc + 1]))

        H = 16
        for tt in range(NTOK // 128):
            pos0 = (tt % 16) * 128
            for half in range(2):
                for c in range(DC):
                    p.add("tensor", lambda e, tt=tt, c=c, half=half: e.matmul(
                        ps1[:], xn[c][:, tt * 128:(tt + 1) * 128],
                        wq_sb[c][:, half * 512:(half + 1) * 512],
                        start=(c == 0), stop=(c == DC - 1)))
                p.add("vector", lambda e, half=half: e.tensor_tensor(
                    q_t[:, half * 512:(half + 1) * 512], ps1[:],
                    bq_b[:, half * 512:(half + 1) * 512], ALU.add))
            for half in range(4):
                for c in range(2):
                    p.add("tensor", lambda e, tt=tt, c=c, half=half: e.matmul(
                        ps1[:], lat[c][:, tt * 128:(tt + 1) * 128],
                        wu_sb[c][:, half * 512:(half + 1) * 512],
                        start=(c == 0), stop=(c == 1)))
                p.add("vector", lambda e, half=half: e.tensor_tensor(
                    kv_t[:, half * 512:(half + 1) * 512], ps1[:],
                    bu_b[:, half * 512:(half + 1) * 512], ALU.add))
            p.dma(cos_t[:], cos_d.ap()[pos0:pos0 + 128, :])
            p.dma(sin_t[:], sin_d.ap()[pos0:pos0 + 128, :])
            cosb = cos_t[:].unsqueeze(1).broadcast_to([128, 16, 32])
            sinb = sin_t[:].unsqueeze(1).broadcast_to([128, 16, 32])
            q1 = bass.AP(q_t, 0, [[1024, 128], [64, 16], [1, 32]])
            q2 = bass.AP(q_t, 32, [[1024, 128], [64, 16], [1, 32]])
            qoA = bass.AP(q_ro, 0, [[1024, 128], [64, 16], [1, 32]])
            qoB = bass.AP(q_ro, 32, [[1024, 128], [64, 16], [1, 32]])
            tv = tr1[:].rearrange("p (h d) -> p h d", h=16)
            tv2 = tr2[:].rearrange("p (h d) -> p h d", h=16)
            k1 = bass.AP(kv_t, 0, [[2048, 128], [128, 16], [1, 32]])
            k2 = bass.AP(kv_t, 32, [[2048, 128], [128, 16], [1, 32]])
            koA = bass.AP(k_ro, 0, [[1024, 128], [64, 16], [1, 32]])
            koB = bass.AP(k_ro, 32, [[1024, 128], [64, 16], [1, 32]])
            for (a1, a2, oA, oB) in ((q1, q2, qoA, qoB), (k1, k2, koA, koB)):
                p.add("vector", lambda e, a1=a1, cb=cosb: e.tensor_tensor(tv, a1, cb, ALU.mult))
                p.add("vector", lambda e, a2=a2, sb=sinb: e.tensor_tensor(tv2, a2, sb, ALU.mult))
                p.add("vector", lambda e, oA=oA: e.tensor_tensor(oA, tv, tv2, ALU.subtract))
                p.add("vector", lambda e, a1=a1, sb=sinb: e.tensor_tensor(tv, a1, sb, ALU.mult))
                p.add("vector", lambda e, a2=a2, cb=cosb: e.tensor_tensor(tv2, a2, cb, ALU.mult))
                p.add("vector", lambda e, oB=oB: e.tensor_tensor(oB, tv, tv2, ALU.add))
            vsrc = bass.AP(kv_t, 64, [[2048, 128], [128, 16], [1, 64]])
            p.add("vector", lambda e: e.tensor_copy(
                v_c[:].rearrange("p (h d) -> p h d", h=16), vsrc))
            qv = q_ro[:].rearrange("p (h d) -> p h d", h=16)
            for g in range(16):
                kg = k_ro[:, g * 64:(g + 1) * 64].unsqueeze(1).broadcast_to([128, 16, 64])
                p.add("vector", lambda e, kg=kg: e.tensor_tensor(
                    tmps[:].rearrange("p (h d) -> p h d", h=16), qv, kg, ALU.mult))
                sout = bass.AP(S1, g, [[256, 128], [16, 16]])
                p.add("vector", lambda e, sout=sout: e.tensor_reduce(
                    sout, tmps[:].rearrange("p (h d) -> p h d", h=16),
                    axis=AX.X, op=ALU.add))
            sv = S1[:].rearrange("p (h g) -> p h g", h=16)
            p.add("vector", lambda e: e.reduce_max(nmx[:], sv, axis=AX.X, negate=True))
            p.add("vector", lambda e: e.tensor_tensor(
                S2[:].rearrange("p (h g) -> p h g", h=16), sv,
                nmx[:].unsqueeze(2).broadcast_to([128, 16, 16]), ALU.add))
            k_s = kk(0.125)
            p.add("scalar", lambda e, k_s=k_s: e.activation(S2[:], S2[:], AF.Exp, scale=k_s))
            p.add("vector", lambda e: e.tensor_reduce(
                zs[:], S2[:].rearrange("p (h g) -> p h g", h=16), axis=AX.X, op=ALU.add))
            p.add("vector", lambda e: e.reciprocal(rzz[:], zs[:]))
            p.add("vector", lambda e: e.tensor_tensor(
                S2[:].rearrange("p (h g) -> p h g", h=16),
                S2[:].rearrange("p (h g) -> p h g", h=16),
                rzz[:].unsqueeze(2).broadcast_to([128, 16, 16]), ALU.mult))
            for g in range(16):
                pg = bass.AP(S2, g, [[256, 128], [16, 16], [0, 64]])
                vg = v_c[:, g * 64:(g + 1) * 64].unsqueeze(1).broadcast_to([128, 16, 64])
                if g == 0:
                    p.add("vector", lambda e, pg=pg, vg=vg: e.tensor_tensor(
                        o_acc[:].rearrange("p (h d) -> p h d", h=16), pg, vg, ALU.mult))
                else:
                    p.add("vector", lambda e, pg=pg, vg=vg: e.tensor_tensor(
                        tmps[:].rearrange("p (h d) -> p h d", h=16), pg, vg, ALU.mult))
                    p.add("vector", lambda e: e.tensor_add(o_acc[:], o_acc[:], tmps[:]))
            p.add("vector", lambda e: e.tensor_copy(o_bf[:], o_acc[:]))
            for c in range(DC):
                p.dma(a_tr[:, c * 128:(c + 1) * 128],
                      o_bf[:, c * 128:(c + 1) * 128], transpose=True)
            for c in range(DC):
                p.add("vector", lambda e, c=c, tt=tt: e.tensor_copy(
                    xn[c][:, tt * 128:(tt + 1) * 128], a_tr[:, c * 128:(c + 1) * 128]))

        for oc in range(DC):
            p.dma(wbuf[:], wo_b2.ap()[oc])
            for tb in range(NTOK // 512):
                for c in range(DC):
                    p.add("tensor", lambda e, oc=oc, tb=tb, c=c: e.matmul(
                        ps1[:], wbuf[:, c * 128:(c + 1) * 128],
                        xn[c][:, tb * 512:(tb + 1) * 512],
                        start=(c == 0), stop=(c == DC - 1)))
                p.add("vector", lambda e, oc=oc: e.tensor_scalar_add(
                    t1[:], ps1[:], bo_c[:, oc:oc + 1]))
                p.dma(xold[:], x_d.ap()[oc * 128:(oc + 1) * 128, tb * 512:(tb + 1) * 512])
                p.add("vector", lambda e: e.tensor_add(t1[:], t1[:], xold[:]))
                p.dma(xo_d.ap()[oc * 128:(oc + 1) * 128, tb * 512:(tb + 1) * 512], t1[:])
        scope_cm.__exit__(None, None, None)
        if standalone:
            return _finish()


def attn_a1_inputs(sh):
    return {k: sh[k] for k in ("ma_Wq", "ma_bq", "ma_Wdt", "ma_bdown", "ma_Wup",
                               "ma_bup", "ma_Wot", "ma_bout", "ma_cos", "ma_sin")}




def build_attn_b(FC=None):
    standalone = FC is None
    with contextlib.ExitStack() as outer:
        if standalone:
            FC, _finish = _standalone_ctx(outer)
        nc, p, kk, ones_b = FC.nc, FC.p, FC.kk, FC.ones_b
        x_d, xo_d = FC.x_d, FC.xo_d
        qw_d = FC.dram("ab_qw_sl", [128, 1024])
        kw_b = FC.precast("ab_k_w", [D, D])
        vw_b = FC.precast("ab_v_w", [D, D])
        ow_b = FC.prep_wtiled("ab_owt", 8, 1024)
        qb_d = FC.dram("ab_qb_sl", [1, 128])
        kb_d = FC.dram("ab_k_b", [1, D])
        vb_d = FC.dram("ab_v_b", [1, D])
        ob_d = FC.dram("ab_out_b", [D])
        bk_d = FC.dram("ab_bias_k", [1, D])
        bv_d = FC.dram("ab_bias_v", [1, D])
        gw_d = FC.dram("ab_grep_w", [64, 8])
        gb_d = FC.dram("ab_grep_b", [1, 8])
        ga_d = FC.dram("ab_grep_a", [1, 16])
        pb_d = FC.dram("ab_pb", [2, 272])

        cc_in = {}
        cc_out = {}
        for nm in ("k", "v"):
            cc_in[nm] = nc.dram_tensor(f"ccin_{nm}", [NTOK, D], BF16)
            cc_out[nm] = nc.dram_tensor(f"ccout_{nm}", [B * T, D], BF16)
        cc_in["xn"] = nc.dram_tensor("ccin_xn", [D, NTOK], BF16)
        cc_out["xn"] = nc.dram_tensor("ccout_xn", [NCORES * D, NTOK], BF16)
        scr_d = nc.dram_tensor("scr1", [1, 64], F32)

        scope_cm = FC.scope()
        al = scope_cm.__enter__()
        lb = LNBufs(al, nc, 512)
        xn = al.sbl(DC, (128, NTOK), BF16, "xn")
        wbuf = al.sb((128, 1024), BF16, "wbuf")
        biasrow = al.sb((128, 1024), F32, "biasrow")
        qtmp = al.sb((128, 1024), BF16, "qtmp")
        kf = al.sb((128, 17 * 1024), BF16, "kf")
        vf = al.sb((128, 17 * 1024), BF16, "vf")
        wsb = [kf[:, c * 1024:(c + 1) * 1024] for c in range(DC)]
        q_nn = al.sb((128, 2048), BF16, "qnn")
        qwsl = al.sb((128, 1024), BF16, "qwsl")
        qxt = al.sb((128, 1024), BF16, "qxt")
        qb_b = al.sb((128, 128), F32, "qbb")
        gv_b = al.sb((128, 64), F32, "gvb")
        ga_b = al.sb((128, 16), F32, "gab")
        gb_b = al.sb((128, 1), F32, "gbb")
        pb_b = al.sbl(2, (128, 272), F32, "pbb")
        gw_sb = al.sb((64, 8), F32, "gwsb")
        gv_col = al.sb((64, 1), F32, "gvcol")
        gb_row = al.sb((1, 8), F32, "gbrow")
        gb_one = al.sb((1, 1), F32, "gbone")
        S = al.sb((128, 272), F32, "S")
        tmp272 = al.sb((128, 272), F32, "tmp272")
        P_bf = al.sb((128, 272), BF16, "Pbf")
        nmx = al.sb((128, 16), F32, "nmx16")
        zs = al.sb((128, 16), F32, "zs16")
        rzz = al.sb((128, 16), F32, "rzz16")
        gdot = al.sb((128, 16), F32, "gdot")
        g0 = al.sb((128, 16), F32, "g0")
        tmps = al.sb((128, 1088), F32, "tmps")
        o_acc = al.sb((128, 1024), F32, "oacc")
        o_bf = al.sb((128, 1024), BF16, "obf")
        a_tr = al.sb((128, 1024), BF16, "atr")
        ob_c = al.sb((128, 8), F32, "obc")
        xold = al.sb((128, 512), F32, "xold")
        t1 = al.sb((128, 512), F32, "t1")
        ps1 = al.ps((128, 512), "ps1")
        ps2 = al.ps((128, 512), "ps2")

        load_scale_cols(p, ob_c, ob_d, 8)
        p.dma(gw_sb[:], gw_d.ap())
        p.add("vector", lambda e: e.tensor_reduce(gv_col[:], gw_sb[:], axis=AX.X, op=ALU.add))
        p.dma(gb_row[:], gb_d.ap())
        p.add("vector", lambda e: e.tensor_reduce(gb_one[:], gb_row[:], axis=AX.X, op=ALU.add))
        p.dma(scr_d.ap()[0:1, 0:1], gb_one[:])
        p.dma(scr_d.ap()[0:1, 1:2], gv_col[:].rearrange("p o -> o p")
              if False else gv_col[0:1, :])
        p.dma(gb_b[:], scr_d.ap()[0:1, 0:1].broadcast_to([128, 1]))
        gv_d = nc.dram_tensor("gvscr", [64, 1], F32)
        p.dma(gv_d.ap(), gv_col[:])
        p.dma(gv_b[:], bass.AP(gv_d, 0, [[0, 128], [1, 64]]))
        p.dma(ga_b[:], ga_d.ap().broadcast_to([128, 16]))
        for nl in range(2):
            p.dma(pb_b[nl][:], pb_d.ap()[nl:nl + 1, :].broadcast_to([128, 272]))
        p.cast_dma(qwsl[:], qw_d.ap())
        p.dma(qb_b[:], qb_d.ap().broadcast_to([128, 128]))

        for tb in range(NTOK // 512):
            emit_ln(p, lb, ones_b, xn, tb * 512, kk, x_d=x_d, tok0=tb * 512)

        for c in range(DC):
            p.dma(cc_in["xn"].ap()[c * 128:(c + 1) * 128, :], xn[c][:])
        p.collective("AllGather", ALU.bypass, [list(range(NCORES))],
                     [cc_in["xn"].ap()], [cc_out["xn"].ap()])
        for nm, w_b, b_d in (("k", kw_b, kb_d), ("v", vw_b, vb_d)):
            for c in range(DC):
                p.dma(wsb[c], w_b.ap()[c * 128:(c + 1) * 128, :])
            p.dma(biasrow[:], b_d.ap().broadcast_to([128, 1024]))
            for tt in range(NTOK // 128):
                for half in range(2):
                    for c in range(DC):
                        p.add("tensor", lambda e, tt=tt, c=c, half=half: e.matmul(
                            ps1[:], xn[c][:, tt * 128:(tt + 1) * 128],
                            wsb[c][:, half * 512:(half + 1) * 512],
                            start=(c == 0), stop=(c == DC - 1)))
                    p.add("vector", lambda e, half=half: e.tensor_tensor(
                        qtmp[:, half * 512:(half + 1) * 512], ps1[:],
                        biasrow[:, half * 512:(half + 1) * 512], ALU.add))
                p.dma(cc_in[nm].ap()[tt * 128:(tt + 1) * 128, :], qtmp[:])
            p.collective("AllGather", ALU.bypass, [list(range(NCORES))],
                         [cc_in[nm].ap()], [cc_out[nm].ap()])

        p.cast_dma(kf[:, 16 * 1024:].rearrange("p (o d) -> p o d", o=1),
                   bk_d.ap().unsqueeze(0).broadcast_to([128, 1, 1024]))
        p.cast_dma(vf[:, 16 * 1024:].rearrange("p (o d) -> p o d", o=1),
                   bv_d.ap().unsqueeze(0).broadcast_to([128, 1, 1024]))

        if DBG:
            dbg_kf = nc.dram_tensor("dbg_kf", [128, 2048], BF16, kind="ExternalOutput")
            dbg_qnn = nc.dram_tensor("dbg_qnn", [128, 2048], BF16, kind="ExternalOutput")
            dbg_S = nc.dram_tensor("dbg_S", [128, 272], F32, kind="ExternalOutput")
            dbg_gv = nc.dram_tensor("dbg_gv", [128, 64], F32, kind="ExternalOutput")
            dbg_gb = nc.dram_tensor("dbg_gb", [128, 1], F32, kind="ExternalOutput")
            dbg_g0 = nc.dram_tensor("dbg_g0", [128, 16], F32, kind="ExternalOutput")
            dbg_gd = nc.dram_tensor("dbg_gd", [128, 16], F32, kind="ExternalOutput")
            dbg_Sr = nc.dram_tensor("dbg_Sr", [128, 272], F32, kind="ExternalOutput")
            dbg_pb = nc.dram_tensor("dbg_pb", [128, 272], F32, kind="ExternalOutput")
            dbg_t272 = nc.dram_tensor("dbg_t272", [128, 272], F32, kind="ExternalOutput")
            p.dma(dbg_gv.ap(), gv_b[:])
            p.dma(dbg_gb.ap(), gb_b[:])
        for bt in range(16):
            for s2 in range(16):
                p.dma(kf[:, s2 * 1024:(s2 + 1) * 1024],
                      cc_out["k"].ap()[s2 * T + bt * 128:s2 * T + (bt + 1) * 128, :])
                p.dma(vf[:, s2 * 1024:(s2 + 1) * 1024],
                      cc_out["v"].ap()[s2 * T + bt * 128:s2 * T + (bt + 1) * 128, :])
            if DBG and bt == 0:
                p.dma(dbg_kf.ap(), kf[:, 0:2048])
            for tq in range(16):
                rbase = (tq // 2) * D
                cbase = (tq % 2) * T + bt * 128
                for c in range(DC):
                    p.dma(qxt[:, c * 128:(c + 1) * 128],
                          cc_out["xn"].ap()[rbase + c * 128:rbase + (c + 1) * 128,
                                            cbase:cbase + 128])
                for c in range(DC):
                    p.add("tensor", lambda e, c=c: e.matmul(
                        ps1[:, 0:128], qxt[:, c * 128:(c + 1) * 128],
                        qwsl[:, c * 128:(c + 1) * 128],
                        start=(c == 0), stop=(c == DC - 1)))
                p.add("vector", lambda e, tq=tq: e.tensor_tensor(
                    q_nn[:, tq * 128:(tq + 1) * 128], ps1[:, 0:128], qb_b[:], ALU.add))
            if DBG and bt == 0:
                p.dma(dbg_qnn.ap(), q_nn[:])
            for nl in range(2):
                for tq in range(16):
                    kwin = kf[:].rearrange("p (r h) -> p r h", h=64)[:, tq * 17:tq * 17 + 17, :]
                    qs = q_nn[:, tq * 128 + nl * 64:tq * 128 + nl * 64 + 64]\
                        .unsqueeze(1).broadcast_to([128, 17, 64])
                    p.add("vector", lambda e, qs=qs, kwin=kwin: e.tensor_tensor(
                        tmps[:].rearrange("p (r h) -> p r h", h=64), qs, kwin, ALU.mult))
                    p.add("vector", lambda e, tq=tq: e.tensor_reduce(
                        S[:, tq * 17:tq * 17 + 17],
                        tmps[:].rearrange("p (r h) -> p r h", h=64), axis=AX.X, op=ALU.add))
                qg = bass.AP(q_nn, nl * 64, [[2048, 128], [128, 16], [1, 64]])
                p.add("vector", lambda e, qg=qg: e.tensor_tensor(
                    tmps[:, 0:1024].rearrange("p (t d) -> p t d", t=16), qg,
                    gv_b[:].unsqueeze(1).broadcast_to([128, 16, 64]), ALU.mult))
                p.add("vector", lambda e: e.tensor_reduce(
                    gdot[:], tmps[:, 0:1024].rearrange("p (t d) -> p t d", t=16),
                    axis=AX.X, op=ALU.add))
                p.add("scalar", lambda e: e.activation(g0[:], gdot[:], AF.Sigmoid, bias=gb_b[:]))
                p.add("vector", lambda e: e.tensor_tensor(g0[:], g0[:], ga_b[:], ALU.mult))
                if DBG and bt == 0 and nl == 0:
                    p.dma(dbg_g0.ap(), g0[:])
                    p.dma(dbg_gd.ap(), gdot[:])
                    p.dma(dbg_Sr.ap(), S[:])
                    p.dma(dbg_pb.ap(), pb_b[0][:])
                p.add("vector", lambda e, nl=nl: e.tensor_tensor(
                    tmp272[:].rearrange("p (t s) -> p t s", t=16),
                    g0[:].unsqueeze(2).broadcast_to([128, 16, 17]),
                    pb_b[nl][:].rearrange("p (t s) -> p t s", t=16), ALU.mult))
                p.add("vector", lambda e: e.scalar_tensor_tensor(
                    S[:], S[:], 1.0 / 256.0, tmp272[:], ALU.mult, ALU.add))
                if DBG and bt == 0 and nl == 0:
                    p.dma(dbg_S.ap(), S[:])
                    p.dma(dbg_t272.ap(), tmp272[:])
                sv = S[:].rearrange("p (t s) -> p t s", t=16)
                p.add("vector", lambda e, sv=sv: e.reduce_max(nmx[:], sv, axis=AX.X, negate=True))
                p.add("vector", lambda e, sv=sv: e.tensor_tensor(
                    sv, sv, nmx[:].unsqueeze(2).broadcast_to([128, 16, 17]), ALU.add))
                p.add("scalar", lambda e: e.activation(S[:], S[:], AF.Exp))
                p.add("vector", lambda e, sv=sv: e.tensor_reduce(zs[:], sv, axis=AX.X, op=ALU.add))
                p.add("vector", lambda e: e.reciprocal(rzz[:], zs[:]))
                p.add("vector", lambda e, sv=sv: e.tensor_tensor(
                    sv, sv, rzz[:].unsqueeze(2).broadcast_to([128, 16, 17]), ALU.mult))
                p.add("vector", lambda e: e.tensor_copy(P_bf[:], S[:]))
                for tq in range(16):
                    pg = P_bf[:, tq * 17:tq * 17 + 17].unsqueeze(2).broadcast_to([128, 17, 64])
                    vwin = vf[:].rearrange("p (r h) -> p r h", h=64)[:, tq * 17:tq * 17 + 17, :]
                    oT = bass.AP(tmps, 0, [[1088, 128], [1, 17], [17, 64]])
                    p.add("vector", lambda e, pg=pg, vwin=vwin, oT=oT: e.tensor_tensor(
                        oT, pg, vwin, ALU.mult))
                    p.add("vector", lambda e, tq=tq: e.tensor_reduce(
                        o_acc[:, tq * 64:(tq + 1) * 64],
                        tmps[:, 0:1088].rearrange("p (d s) -> p d s", d=64),
                        axis=AX.X, op=ALU.add))
                p.add("vector", lambda e: e.tensor_copy(o_bf[:], o_acc[:]))
                for c in range(DC):
                    p.dma(a_tr[:, c * 128:(c + 1) * 128],
                          o_bf[:, c * 128:(c + 1) * 128], transpose=True)
                for c in range(DC):
                    p.add("vector", lambda e, c=c, nl=nl, bt=bt: e.tensor_copy(
                        xn[c][:, nl * T + bt * 128:nl * T + (bt + 1) * 128],
                        a_tr[:, c * 128:(c + 1) * 128]))

        for oc in range(DC):
            p.dma(wbuf[:], ow_b.ap()[oc])
            for tb in range(NTOK // 512):
                for c in range(DC):
                    p.add("tensor", lambda e, oc=oc, tb=tb, c=c: e.matmul(
                        ps2[:], wbuf[:, c * 128:(c + 1) * 128],
                        xn[c][:, tb * 512:(tb + 1) * 512],
                        start=(c == 0), stop=(c == DC - 1)))
                p.add("vector", lambda e, oc=oc: e.tensor_scalar_add(
                    t1[:], ps2[:], ob_c[:, oc:oc + 1]))
                p.dma(xold[:], x_d.ap()[oc * 128:(oc + 1) * 128, tb * 512:(tb + 1) * 512])
                p.add("vector", lambda e: e.tensor_add(t1[:], t1[:], xold[:]))
                p.dma(xo_d.ap()[oc * 128:(oc + 1) * 128, tb * 512:(tb + 1) * 512], t1[:])
        for nm2 in ("k", "v", "xn"):
            p.dma(qtmp[:], cc_out[nm2].ap()[0:128, 0:1024])
        scope_cm.__exit__(None, None, None)
        if standalone:
            return _finish()


def attn_b_inputs(sh, core):
    d = {k: sh[k] for k in ("ab_k_w", "ab_v_w", "ab_owt", "ab_k_b", "ab_v_b",
                            "ab_out_b", "ab_bias_k", "ab_bias_v", "ab_grep_w",
                            "ab_grep_b", "ab_grep_a")}
    d["ab_qw_sl"] = sh["ab_qw_sl"][core]
    d["ab_qb_sl"] = sh["ab_qb_sl"][core]
    d["ab_pb"] = sh["ab_pb"][core]
    return d




def _t(a):
    return np.ascontiguousarray(np.asarray(a).T)


def _tiled(w_in_out, noc):
    w = np.asarray(w_in_out, np.float32)
    nin, nout = w.shape
    ncc = nin // 128
    assert noc * 128 == nout
    t = w.reshape(ncc, 128, noc, 128).transpose(2, 1, 0, 3).reshape(noc, 128, ncc * 128)
    return np.ascontiguousarray(t)


def host_prep(inputs):
    sh = {}
    sh["w1Tt"] = _tiled(_t(inputs["L0_ffn_w1_w"]), 32)
    sh["w2Tt"] = _tiled(_t(inputs["L0_ffn_w2_w"]), 32)
    sh["h1Tt"] = _tiled(_t(inputs["L0_ffn_h1_w"]), 32)
    sh["h2Tt"] = _tiled(_t(inputs["L0_ffn_h2_w"]), 32)
    sh["w3Tt"] = _tiled(_t(inputs["L0_ffn_w3_w"]), 8)
    sh["w1s"] = np.asarray(inputs["L0_ffn_w1_s"])
    sh["w2s"] = np.asarray(inputs["L0_ffn_w2_s"])
    sh["h1s"] = np.asarray(inputs["L0_ffn_h1_s"])
    sh["h2s"] = np.asarray(inputs["L0_ffn_h2_s"])
    sh["w3s"] = np.asarray(inputs["L0_ffn_w3_s"])

    sh["aa_Wqt"] = _tiled(np.asarray(inputs["L0_attn_a_Wq"]), 8)
    sh["aa_Wkt"] = _tiled(np.asarray(inputs["L0_attn_a_Wk"]), 8)
    sh["aa_Wv"] = np.asarray(inputs["L0_attn_a_Wv"])
    sh["aa_Wot"] = _tiled(np.asarray(inputs["L0_attn_a_Wo"]), 8)
    sh["aa_bo"] = np.asarray(inputs["L0_attn_a_bo"])
    sh["aa_theta"] = np.ascontiguousarray(np.asarray(inputs["L0_attn_a_theta"]).reshape(1, 8))
    sh["aa_gamma"] = np.asarray(inputs["L0_attn_a_gamma"])
    sh["aa_beta"] = np.asarray(inputs["L0_attn_a_beta"])
    ii = np.arange(128)
    sh["m128"] = np.where(ii[:, None] >= ii[None, :], 0.0, -1e30).astype(np.float32)

    sh["mb_WinTt"] = _tiled(_t(inputs["L1_attn_b_W_in"]), 32)
    sh["mb_WdtTt"] = _tiled(_t(inputs["L1_attn_b_W_dt"]), 16)
    sh["mb_WoutTt"] = _tiled(_t(inputs["L1_attn_b_W_out"]), 8)
    sh["mb_conv_w"] = np.asarray(inputs["L1_attn_b_conv_w"])
    sh["mb_conv_b"] = np.asarray(inputs["L1_attn_b_conv_b"])
    sh["mb_A_log"] = np.asarray(inputs["L1_attn_b_A_log"])
    sh["mb_B"] = np.asarray(inputs["L1_attn_b_B_param"])
    sh["mb_C"] = np.asarray(inputs["L1_attn_b_C_param"])
    sh["mb_D"] = np.asarray(inputs["L1_attn_b_D_param"])
    sh["mb_b_dt"] = np.asarray(inputs["L1_attn_b_b_dt"])
    sh["mb_dt_bias"] = np.asarray(inputs["L1_attn_b_dt_bias"])

    rm = np.asarray(inputs["L1_ffn_rail_maker"])
    rup = np.asarray(inputs["L1_ffn_rail_up_proj"])
    rmBD = np.zeros((4, 8, 128, 128), np.float32)
    rupBD = np.zeros((4, 8, 128, 512), np.float32)
    for h in range(4):
        for rb in range(8):
            for rr in range(8):
                r = rb * 8 + rr
                rmBD[h, rb, rr * 16:(rr + 1) * 16, rr * 16:(rr + 1) * 16] = rm[h, r]
                rupBD[h, rb, rr * 16:(rr + 1) * 16, rr * 64:(rr + 1) * 64] = rup[h, r].T
    sh["rf_rmBD"] = rmBD
    sh["rf_rupBD"] = rupBD
    sh["rf_wcT"] = _t(inputs["L1_ffn_w_cproj"])

    sh["ma_Wq"] = np.asarray(inputs["L1_attn_a_Wq"])
    sh["ma_bq"] = np.ascontiguousarray(np.asarray(inputs["L1_attn_a_bq"]).reshape(1, -1))
    sh["ma_Wdt"] = _tiled(np.asarray(inputs["L1_attn_a_Wdown"]), 2)
    sh["ma_bdown"] = np.asarray(inputs["L1_attn_a_bdown"])
    sh["ma_Wup"] = np.asarray(inputs["L1_attn_a_Wup"])
    sh["ma_bup"] = np.ascontiguousarray(np.asarray(inputs["L1_attn_a_bup"]).reshape(1, -1))
    sh["ma_Wot"] = _tiled(np.asarray(inputs["L1_attn_a_Wout"]), 8)
    sh["ma_bout"] = np.asarray(inputs["L1_attn_a_bout"])
    inv = (1.0 / 10000.0 ** (np.arange(0, 64, 2, dtype=np.float32) / np.float32(64.0))).astype(np.float32)
    fr = (np.arange(T, dtype=np.float32)[:, None] * inv[None, :]).astype(np.float32)
    sh["ma_cos"] = np.cos(fr).astype(np.float32)
    sh["ma_sin"] = np.sin(fr).astype(np.float32)

    qw = np.asarray(inputs["L0_attn_b_q_w"])
    qb = np.asarray(inputs["L0_attn_b_q_b"])
    sh["ab_qw_sl"] = [np.ascontiguousarray(
        qw[:, c * 128:(c + 1) * 128].reshape(8, 128, 128).transpose(1, 0, 2).reshape(128, 1024))
        for c in range(NCORES)]
    sh["ab_qb_sl"] = [np.ascontiguousarray(qb[c * 128:(c + 1) * 128].reshape(1, 128))
                      for c in range(NCORES)]
    sh["ab_k_w"] = np.asarray(inputs["L0_attn_b_k_w"])
    sh["ab_v_w"] = np.asarray(inputs["L0_attn_b_v_w"])
    sh["ab_owt"] = _tiled(np.asarray(inputs["L0_attn_b_out_w"]), 8)
    sh["ab_k_b"] = np.ascontiguousarray(np.asarray(inputs["L0_attn_b_k_b"]).reshape(1, -1))
    sh["ab_v_b"] = np.ascontiguousarray(np.asarray(inputs["L0_attn_b_v_b"]).reshape(1, -1))
    sh["ab_out_b"] = np.asarray(inputs["L0_attn_b_out_b"])
    sh["ab_bias_k"] = np.ascontiguousarray(np.asarray(inputs["L0_attn_b_bias_k"]).reshape(1, -1))
    sh["ab_bias_v"] = np.ascontiguousarray(np.asarray(inputs["L0_attn_b_bias_v"]).reshape(1, -1))
    sh["ab_grep_w"] = np.asarray(inputs["L0_attn_b_grep_w"])
    sh["ab_grep_b"] = np.ascontiguousarray(np.asarray(inputs["L0_attn_b_grep_b"]).reshape(1, -1))
    sh["ab_grep_a"] = np.ascontiguousarray(np.asarray(inputs["L0_attn_b_grep_a"]).reshape(1, -1))
    rel_pos = np.arange(17)[None, :] - np.arange(16)[:, None]
    nb = 16
    buckets = (rel_pos > 0).astype(np.int32) * nb
    rp = np.abs(rel_pos)
    max_exact = 8
    is_small = rp < max_exact
    large = max_exact + (np.log(rp.astype(np.float32) / max_exact + 1e-6)
                         / np.log(128.0 / 8) * (nb - max_exact)).astype(np.int32)
    large = np.minimum(large, nb - 1)
    buckets = buckets + np.where(is_small, rp, large)
    rel_emb = np.asarray(inputs["L0_attn_b_rel_emb"])
    pb_all = []
    for c in range(NCORES):
        m = np.zeros((2, 272), np.float32)
        for nl in range(2):
            n = 2 * c + nl
            for tq in range(16):
                for s in range(17):
                    m[nl, tq * 17 + s] = rel_emb[buckets[n, s], tq]
        pb_all.append(m)
    sh["ab_pb"] = pb_all

    x = np.asarray(inputs["L0_attn_a_x"])
    xs = []
    for c in range(NCORES):
        xc = x[BPC * c:BPC * (c + 1)].reshape(NTOK, D)
        xs.append(np.ascontiguousarray(xc.T))
    return sh, xs


def run_launch(build_fn, per_core_maps):
    nc = build_fn()
    res = run_bass_kernel_spmd(nc, per_core_maps, core_ids=list(range(NCORES)))
    return res


FUSED = True


def build_fused():
    nc = bass.Bass("TRN2", target_bir_lowering=False)
    x0_d = nc.dram_tensor("x", [D, NTOK], F32, kind="ExternalInput")
    xo_d = nc.dram_tensor("xo", [D, NTOK], F32, kind="ExternalOutput")
    stages_x = [x0_d] + [nc.dram_tensor(f"xstage{i}", [D, NTOK], F32)
                         for i in range(1, 8)] + [xo_d]
    with contextlib.ExitStack() as stack:
        al = Alloc(nc, stack)
        p = Prog(nc)
        sems = {k: al.sem(f"ch_{k}") for k in ("c", "ds", "dg")}
        kk = Konst(al, p)
        ones_b = al.sb((128, 1), BF16, "onesb")
        p.add("vector", lambda e: e.memset(ones_b[:], 1.0))
        FC = FuseCtx(nc, al, p, kk, ones_b)
        seq = [build_attn_a, build_ffn0, build_attn_b, build_ffn0,
               build_attn_a1, build_rails, build_mamba, build_rails]
        for i, fn in enumerate(seq):
            FC.x_d = stages_x[i]
            FC.xo_d = stages_x[i + 1]
            fn(FC)
        p.emit(sems, al)
    return nc


def fused_inputs(sh, core):
    d = {}
    d.update(attn_a_inputs(sh))
    d.update(ffn0_inputs(sh))
    d.update(attn_b_inputs(sh, core))
    d.update(attn_a1_inputs(sh))
    d.update(rails_inputs(sh))
    d.update(mamba_inputs(sh))
    return d


def kernel(**inputs):
    sh, xs = host_prep(inputs)
    if FUSED:
        nc = build_fused()
        maps = [dict(fused_inputs(sh, c), x=xs[c]) for c in range(NCORES)]
        res = run_bass_kernel_spmd(nc, maps, core_ids=list(range(NCORES)))
        xs = [np.asarray(res.results[c]["xo"]) for c in range(NCORES)]
    else:
        stages = [
            (build_attn_a, lambda c: attn_a_inputs(sh)),
            (build_ffn0, lambda c: ffn0_inputs(sh)),
            (build_attn_b, lambda c: attn_b_inputs(sh, c)),
            (build_ffn0, lambda c: ffn0_inputs(sh)),
            (build_attn_a1, lambda c: attn_a1_inputs(sh)),
            (build_rails, lambda c: rails_inputs(sh)),
            (build_mamba, lambda c: mamba_inputs(sh)),
            (build_rails, lambda c: rails_inputs(sh)),
        ]
        built = {}
        for build_fn, mapfn in stages:
            if build_fn not in built:
                built[build_fn] = build_fn()
            nc = built[build_fn]
            maps = [dict(mapfn(c), x=xs[c]) for c in range(NCORES)]
            res = run_bass_kernel_spmd(nc, maps, core_ids=list(range(NCORES)))
            xs = [np.asarray(res.results[c]["xo"]) for c in range(NCORES)]
    out = np.stack([x.T.reshape(BPC, T, D) for x in xs]).reshape(B, T, D)
    return np.ascontiguousarray(out)


# version 07
# speedup vs baseline: 1.1367x; 1.1367x over previous
import contextlib

import numpy as np
import concourse.bass as bass
import concourse.mybir as mybir
from concourse.bass_utils import run_bass_kernel_spmd

F32 = mybir.dt.float32
BF16 = mybir.dt.bfloat16
AF = mybir.ActivationFunctionType
ALU = mybir.AluOpType
AX = mybir.AxisListType

NCORES = 8
B, T, D = 16, 2048, 1024
BPC = B // NCORES
NTOK = BPC * T
DC = D // 128


LAST_TOTAL = None
LAST_NSTEPS = None
NOCLEAR = ("dg",)
OWN_SKIP = ("tensor",)


class Prog:

    def __init__(self, nc):
        self.nc = nc
        self.steps = []
        self.n_cc = 0

    def mark(self):
        return len(self.steps)

    def add(self, eng, fn, inc=1, key=None, after=None):
        if key is None:
            key = "tensor" if eng == "tensor" else "c"
        self.steps.append((eng, fn, inc, key, after))

    def dma(self, out, in_, q="sync", after=None, **kw):
        key = "dg" if q == "gpsimd" else "ds"
        self.add(q, lambda e, o=out, i=in_, kw=kw: e.dma_start(o, i, **kw), 16, key,
                 after=after)

    def cast_dma(self, out, in_, **kw):
        self.add("gpsimd", lambda e, o=out, i=in_, kw=kw: e.dma_start(o, i, **kw), 16, "dg")

    def collective(self, kind, op, rg, ins, outs):
        key = f"cc{self.n_cc}"
        self.n_cc += 1
        self.add("gpsimd", lambda e, kind=kind, op=op, rg=rg, ins=ins, outs=outs:
                 e.collective_compute(kind, op, replica_groups=rg, ins=ins, outs=outs),
                 1, key)

    def emit(self, sems, al=None):
        global LAST_TOTAL, LAST_NSTEPS
        engines_used = [en for en in ["sync", "tensor", "vector", "scalar", "gpsimd"]
                        if any(s[0] == en for s in self.steps)]
        comp_keys = [k for k in ("tensor", "c")
                     if any(s[3] == k for s in self.steps)]
        keys = comp_keys + ["ds", "dg"] + [f"cc{i}" for i in range(self.n_cc)]
        sems = dict(sems)
        for k in comp_keys:
            if k not in sems:
                sems[k] = al.sem(f"eng_{k}")
        for i in range(self.n_cc):
            sems[f"cc{i}"] = al.sem(f"ccsem{i}")
        pre = []
        cum = {k: 0 for k in keys}
        for (_, _, inc, key, _aft) in self.steps:
            pre.append(dict(cum))
            cum[key] += inc
        self.total = dict(cum)
        LAST_TOTAL = dict(cum)
        LAST_NSTEPS = len(self.steps)
        done = al.sem("prog_done") if al is not None else None
        n_other = len([e for e in engines_used if e != "gpsimd"])
        with self.nc.Block() as block:
            for en in ["sync", "tensor", "vector", "scalar", "gpsimd"]:
                sub = [(i, self.steps[i]) for i in range(len(self.steps))
                       if self.steps[i][0] == en]
                if not sub and en != "gpsimd":
                    continue

                def mk(sub, en):
                    def body(eng):
                        last = {k: 0 for k in keys}
                        for i, (_, fn, inc, key, aft) in sub:
                            for k in keys:
                                if k == "tensor" and en in OWN_SKIP:
                                    continue
                                v = pre[i][k]
                                if aft is not None and k in comp_keys:
                                    v = pre[aft][k]
                                if v > last[k]:
                                    eng.wait_ge(sems[k], v)
                                    last[k] = v
                            fn(eng).then_inc(sems[key], inc)
                        if done is None:
                            return
                        if en != "gpsimd":
                            eng.nop().then_inc(done, 1)
                        else:
                            for k in keys:
                                if cum[k] > last[k] and k != "gpsimd":
                                    eng.wait_ge(sems[k], cum[k])
                            if n_other:
                                eng.wait_ge(done, n_other)
                            for k in keys:
                                if k in NOCLEAR:
                                    continue
                                eng.sem_clear(sems[k])
                            eng.sem_clear(done)
                    return body

                getattr(block, en)(mk(sub, en))


class Alloc:
    def __init__(self, nc, stack, ctr=None):
        self.nc = nc
        self.stack = stack
        self.ctr = ctr if ctr is not None else [0]

    @property
    def n(self):
        return self.ctr[0]

    def _id(self):
        self.ctr[0] += 1
        return self.ctr[0]

    def sb(self, shape, dt=F32, name=None):
        i = self._id()
        name = f"{name}_{i}" if name else f"sb{i}"
        return self.stack.enter_context(self.nc.sbuf_tensor(name, list(shape), dt))

    def sbl(self, k, shape, dt=F32, name=None):
        return [self.sb(shape, dt, name and f"{name}{i}") for i in range(k)]

    def ps(self, shape, name=None, dt=F32):
        i = self._id()
        name = f"{name}_{i}" if name else f"ps{i}"
        return self.stack.enter_context(self.nc.psum_tensor(name, list(shape), dt))

    def sem(self, name=None):
        i = self._id()
        return self.stack.enter_context(self.nc.semaphore(f"{name or 'sem'}_{i}"))


class FuseCtx:

    def __init__(self, nc, al, p, kk, ones_b):
        self.nc = nc
        self.al = al
        self.p = p
        self.kk = kk
        self.ones_b = ones_b
        self.x_d = None
        self.xo_d = None
        self._drams = {}

    def dram(self, name, shape, kind="ExternalInput"):
        if name not in self._drams:
            self._drams[name] = self.nc.dram_tensor(name, list(shape), F32, kind=kind)
        return self._drams[name]

    def prep_wtiled(self, name, noc, free):
        key = name + "_bf"
        if key not in self._drams:
            src = self.dram(name, [noc, 128, free])
            dst = self.nc.dram_tensor(key, [noc, 128, free], BF16)
            self._drams[key] = dst
            for oc in range(noc):
                self.p.cast_dma(dst.ap()[oc], src.ap()[oc])
        return self._drams[key]

    def precast(self, name, shape):
        key = name + "_bf"
        if key not in self._drams:
            src = self.dram(name, shape)
            dst = self.nc.dram_tensor(key, list(shape), BF16)
            self._drams[key] = dst
            flat_r = 1
            for s in shape[:-1]:
                flat_r *= s
            last = shape[-1]
            for r0 in range(0, flat_r, 128):
                rr = min(128, flat_r - r0)
                sv = bass.AP(src, r0 * last, [[last, rr], [1, last]])
                dv = bass.AP(dst, r0 * last, [[last, rr], [1, last]])
                self.p.cast_dma(dv, sv)
        return self._drams[key]

    @contextlib.contextmanager
    def scope(self):
        with contextlib.ExitStack() as stack:
            yield Alloc(self.nc, stack, self.al.ctr)


def _standalone_ctx(stack):
    nc = bass.Bass("TRN2", target_bir_lowering=False)
    x_d = nc.dram_tensor("x", [D, NTOK], F32, kind="ExternalInput")
    xo_d = nc.dram_tensor("xo", [D, NTOK], F32, kind="ExternalOutput")
    al = Alloc(nc, stack)
    p = Prog(nc)
    sems = {k: al.sem(f"ch_{k}") for k in ("c", "ds", "dg")}
    kk = Konst(al, p)
    ones_b = al.sb((128, 1), BF16, "onesb")
    p.add("vector", lambda e: e.memset(ones_b[:], 1.0))
    FC = FuseCtx(nc, al, p, kk, ones_b)
    FC.x_d = x_d
    FC.xo_d = xo_d

    def finish():
        p.emit(sems, al)
        return nc

    return FC, finish


class Konst:

    def __init__(self, al, p, n=24):
        self.tile = al.sb((128, n), F32, "konst")
        self.p = p
        self.map = {}
        self.n = 0

    def __call__(self, v):
        v = float(v)
        if v not in self.map:
            i = self.n
            self.n += 1
            self.map[v] = i
            self.p.add("vector", lambda e, i=i, v=v: e.memset(self.tile[:, i:i + 1], v))
        i = self.map[v]
        return self.tile[:, i:i + 1]


def load_scale_cols(p, dst, src_d, n):
    for c in range(n):
        p.dma(dst[:, c:c + 1], src_d.ap()[c * 128:(c + 1) * 128].unsqueeze(1))


def wcol_load(p, wbuf, w_d, oc, cchunks, rows=None):
    src = w_d.ap()[:, oc * 128:(oc + 1) * 128].rearrange("(c p) o -> p c o", p=128)
    p.cast_dma(wbuf[:].rearrange("p (c o) -> p c o", c=cchunks), src)




class LNBufs:

    def __init__(self, al, nc, nt, tag="", with_x=True):
        self.nt = nt
        self.xt = al.sbl(DC, (128, nt), F32, f"lnx{tag}") if with_x else None
        self.xb = al.sb((128, nt), BF16, f"lnxb{tag}")
        self.sqb = al.sb((128, nt), BF16, f"lnsqb{tag}")
        self.mean = al.sb((1, nt), F32, f"lnmean{tag}")
        self.msq = al.sb((1, nt), F32, f"lnmsq{tag}")
        self.var = al.sb((1, nt), F32, f"lnvar{tag}")
        self.rstd = al.sb((1, nt), F32, f"lnrstd{tag}")
        self.mbb = al.sb((128, nt), F32, f"lnmbb{tag}")
        self.rbb = al.sb((128, nt), F32, f"lnrbb{tag}")
        self.sq = al.sb((128, nt), F32, f"lnsq{tag}")
        self.psA = al.ps((1, nt), f"lnpsA{tag}")
        self.psB = al.ps((1, nt), f"lnpsB{tag}")
        self.rows_d = nc.dram_tensor(f"lnrows{tag}_{al._id()}", [2, nt], F32)


def emit_rowstats(p, lb, ones_b, tiles, nch, ndim, kk, nt=None, eps=1e-5):
    nt = nt or lb.nt
    for c in range(nch):
        p.add("vector", lambda e, c=c: e.tensor_copy(lb.xb[:, 0:nt], tiles[c][:, 0:nt]))
        p.add("tensor", lambda e, c=c: e.matmul(
            lb.psA[0:1, 0:nt], ones_b[:], lb.xb[:, 0:nt],
            start=(c == 0), stop=(c == nch - 1)))
    for c in range(nch):
        p.add("scalar", lambda e, c=c: e.activation(lb.sqb[:, 0:nt], tiles[c][:, 0:nt], AF.Square))
        p.add("tensor", lambda e, c=c: e.matmul(
            lb.psB[0:1, 0:nt], ones_b[:], lb.sqb[:, 0:nt],
            start=(c == 0), stop=(c == nch - 1)))
    p.add("vector", lambda e: e.tensor_scalar_mul(lb.mean[:, 0:nt], lb.psA[0:1, 0:nt], 1.0 / ndim))
    p.add("vector", lambda e: e.tensor_scalar_mul(lb.msq[:, 0:nt], lb.psB[0:1, 0:nt], 1.0 / ndim))
    p.add("vector", lambda e: e.tensor_tensor(lb.var[:, 0:nt], lb.mean[:, 0:nt], lb.mean[:, 0:nt], ALU.mult))
    p.add("vector", lambda e: e.tensor_tensor(lb.var[:, 0:nt], lb.msq[:, 0:nt], lb.var[:, 0:nt], ALU.subtract))
    p.add("vector", lambda e, eps=eps: e.tensor_scalar_add(lb.var[:, 0:nt], lb.var[:, 0:nt], float(eps)))
    p.add("scalar", lambda e: e.activation(lb.rstd[:, 0:nt], lb.var[:, 0:nt], AF.Ln))
    p.add("vector", lambda e: e.tensor_scalar_mul(lb.rstd[:, 0:nt], lb.rstd[:, 0:nt], -0.5))
    p.add("scalar", lambda e: e.activation(lb.rstd[:, 0:nt], lb.rstd[:, 0:nt], AF.Exp))
    p.dma(lb.rows_d.ap()[0:1, 0:nt], lb.mean[:, 0:nt])
    p.dma(lb.rows_d.ap()[1:2, 0:nt], lb.rstd[:, 0:nt])
    p.dma(lb.mbb[:, 0:nt], lb.rows_d.ap()[0:1, 0:nt].broadcast_to([128, nt]))
    p.dma(lb.rbb[:, 0:nt], lb.rows_d.ap()[1:2, 0:nt].broadcast_to([128, nt]))


def emit_ln(p, lb, ones_b, xn_tiles, xn_col0, kk, x_d=None, tok0=0,
            xt=None, nt=None):
    nt = nt or lb.nt
    if xt is None:
        xt = lb.xt
        for c in range(DC):
            p.dma(xt[c][:, 0:nt], x_d.ap()[c * 128:(c + 1) * 128, tok0:tok0 + nt])
    emit_rowstats(p, lb, ones_b, xt, DC, D, kk, nt=nt)
    for c in range(DC):
        p.add("vector", lambda e, c=c: e.tensor_tensor(lb.sq[:, 0:nt], xt[c][:, 0:nt], lb.mbb[:, 0:nt], ALU.subtract))
        p.add("vector", lambda e, c=c: e.tensor_tensor(
            xn_tiles[c][:, xn_col0:xn_col0 + nt], lb.sq[:, 0:nt], lb.rbb[:, 0:nt], ALU.mult))
    return xt



DBG = False


def _tap(p, nc, name, src_ap, shape, q="sync"):
    if not DBG:
        return
    d = nc.dram_tensor(f"dbg_{name}", list(shape), F32, kind="ExternalOutput")
    if src_ap.dtype == F32:
        p.dma(d.ap(), src_ap, q=q)
    else:
        p.cast_dma(d.ap(), src_ap)


def build_ffn0(FC=None):
    standalone = FC is None
    with contextlib.ExitStack() as outer:
        if standalone:
            FC, _finish = _standalone_ctx(outer)
        nc, p, kk, ones_b = FC.nc, FC.p, FC.kk, FC.ones_b
        x_d, xo_d = FC.x_d, FC.xo_d
        w1_b = FC.prep_wtiled("w1Tt", 32, 1024)
        w2_b = FC.prep_wtiled("w2Tt", 32, 1024)
        h1_b = FC.prep_wtiled("h1Tt", 32, 4096)
        h2_b = FC.prep_wtiled("h2Tt", 32, 4096)
        w3_b = FC.prep_wtiled("w3Tt", 8, 4096)
        s1_d = FC.dram("w1s", [4096])
        s2_d = FC.dram("w2s", [4096])
        sh1_d = FC.dram("h1s", [4096])
        sh2_d = FC.dram("h2s", [4096])
        s3_d = FC.dram("w3s", [1024])

        NT = 512
        NB = NTOK // NT
        scope_cm = FC.scope()
        al = scope_cm.__enter__()
        lb = LNBufs(al, nc, NT)
        xn = al.sbl(DC, (128, NT), BF16, "xn")
        h_a = al.sbl(32, (128, NT), BF16, "ha")
        h_b = al.sbl(32, (128, NT), BF16, "hb")
        silu_t = al.sb((128, NT), F32, "silu")
        yo = al.sb((128, NT), F32, "yo")
        wbufs = al.sbl(2, (128, 1024), BF16, "wbuf")
        wbufs2 = al.sbl(2, (128, 1024), BF16, "wbuf2")
        whbufs = al.sbl(2, (128, 4096), BF16, "whbuf")
        s1 = al.sb((128, 32), F32, "s1")
        s2 = al.sb((128, 32), F32, "s2")
        sh1 = al.sb((128, 32), F32, "sh1")
        sh2 = al.sb((128, 32), F32, "sh2")
        s3 = al.sb((128, 8), F32, "s3")
        psr = [al.ps((128, NT), f"psr{j}") for j in range(4)]

        load_scale_cols(p, s1, s1_d, 32)
        load_scale_cols(p, s2, s2_d, 32)
        load_scale_cols(p, sh1, sh1_d, 32)
        load_scale_cols(p, sh2, sh2_d, 32)
        load_scale_cols(p, s3, s3_d, 8)

        for tb in range(NB):
            col0 = tb * NT
            xt = emit_ln(p, lb, ones_b, xn, 0, kk, x_d=x_d, tok0=col0)
            if tb == 0:
                _tap(p, nc, "mean", lb.mean[:], (1, NT))
                _tap(p, nc, "rstd", lb.rstd[:], (1, NT))
                _tap(p, nc, "xn0", xn[0][:], (128, NT))
                _tap(p, nc, "s1col", s1[:], (128, 32))
            mks = []
            for oc in range(32):
                aft = mks[oc - 2] if oc >= 2 else None
                wb, wb2 = wbufs[oc % 2], wbufs2[oc % 2]
                pa, pb = psr[(2 * oc) % 4], psr[(2 * oc + 1) % 4]
                p.dma(wb[:], w1_b.ap()[oc], after=aft)
                for c in range(DC):
                    p.add("tensor", lambda e, c=c, wb=wb, pa=pa: e.matmul(
                        pa[:], wb[:, c * 128:(c + 1) * 128], xn[c][:],
                        start=(c == 0), stop=(c == DC - 1)), after=aft)
                p.add("scalar", lambda e, oc=oc, pa=pa: e.activation(
                    silu_t[:], pa[:], AF.Silu, scale=s1[:, oc:oc + 1]))
                p.dma(wb2[:], w2_b.ap()[oc], after=aft)
                for c in range(DC):
                    p.add("tensor", lambda e, c=c, wb2=wb2, pb=pb: e.matmul(
                        pb[:], wb2[:, c * 128:(c + 1) * 128], xn[c][:],
                        start=(c == 0), stop=(c == DC - 1)), after=aft)
                p.add("vector", lambda e, oc=oc, pb=pb: e.scalar_tensor_tensor(
                    h_a[oc][:], pb[:], s2[:, oc:oc + 1], silu_t[:], ALU.mult, ALU.mult))
                mks.append(p.mark())
            mks = []
            for oc in range(32):
                aft = mks[oc - 2] if oc >= 2 else None
                wh, pa = whbufs[oc % 2], psr[oc % 4]
                p.dma(wh[:], h1_b.ap()[oc], after=aft)
                for c in range(32):
                    p.add("tensor", lambda e, c=c, wh=wh, pa=pa: e.matmul(
                        pa[:], wh[:, c * 128:(c + 1) * 128], h_a[c][:],
                        start=(c == 0), stop=(c == 31)), after=aft)
                p.add("scalar", lambda e, oc=oc, pa=pa: e.activation(
                    h_b[oc][:], pa[:], AF.Gelu, scale=sh1[:, oc:oc + 1]))
                mks.append(p.mark())
            mks = []
            for oc in range(32):
                aft = mks[oc - 2] if oc >= 2 else None
                wh, pa = whbufs[oc % 2], psr[oc % 4]
                p.dma(wh[:], h2_b.ap()[oc], after=aft)
                for c in range(32):
                    p.add("tensor", lambda e, c=c, wh=wh, pa=pa: e.matmul(
                        pa[:], wh[:, c * 128:(c + 1) * 128], h_b[c][:],
                        start=(c == 0), stop=(c == 31)), after=aft)
                p.add("scalar", lambda e, oc=oc, pa=pa: e.activation(
                    h_a[oc][:], pa[:], AF.Gelu, scale=sh2[:, oc:oc + 1]))
                mks.append(p.mark())
            mks = []
            for oc in range(DC):
                aft = mks[oc - 2] if oc >= 2 else None
                wh, pa = whbufs[oc % 2], psr[oc % 4]
                p.dma(wh[:], w3_b.ap()[oc], after=aft)
                for c in range(32):
                    p.add("tensor", lambda e, c=c, wh=wh, pa=pa: e.matmul(
                        pa[:], wh[:, c * 128:(c + 1) * 128], h_a[c][:],
                        start=(c == 0), stop=(c == 31)), after=aft)
                p.add("vector", lambda e, oc=oc, pa=pa: e.scalar_tensor_tensor(
                    yo[:], pa[:], s3[:, oc:oc + 1], xt[oc][:], ALU.mult, ALU.add))
                p.dma(xo_d.ap()[oc * 128:(oc + 1) * 128, col0:col0 + NT], yo[:])
                mks.append(p.mark())

        scope_cm.__exit__(None, None, None)
        if standalone:
            return _finish()


def ffn0_inputs(shared):
    return {k: shared[k] for k in ("w1Tt", "w2Tt", "h1Tt", "h2Tt", "w3Tt",
                                   "w1s", "w2s", "h1s", "h2s", "w3s")}




def build_attn_a(FC=None):
    standalone = FC is None
    with contextlib.ExitStack() as outer:
        if standalone:
            FC, _finish = _standalone_ctx(outer)
        nc, p, kk, ones_b = FC.nc, FC.p, FC.kk, FC.ones_b
        x_d, xo_d = FC.x_d, FC.xo_d
        wq_b = FC.prep_wtiled("aa_Wqt", 8, 1024)
        wk_b = FC.prep_wtiled("aa_Wkt", 8, 1024)
        wv_b = FC.precast("aa_Wv", [D, D])
        wo_b = FC.prep_wtiled("aa_Wot", 8, 1024)
        bo_d = FC.dram("aa_bo", [D])
        th_d = FC.dram("aa_theta", [1, 8])
        ga_d = FC.dram("aa_gamma", [D])
        be_d = FC.dram("aa_beta", [D])
        m128_d = FC.dram("m128", [128, 128])
        decay_d = nc.dram_tensor("decay", [T, T], F32)

        NH, HD = 8, 128
        NTT = T // 128
        scope_cm = FC.scope()
        al = scope_cm.__enter__()
        lb = LNBufs(al, nc, 512, with_x=False)
        xnb = al.sbl(DC, (128, T), BF16, "xnb")
        xtb = al.sbl(DC, (128, 512), BF16, "xtb")
        q_fm = al.sbl(NH, (128, T), BF16, "qfm")
        k_fm = al.sbl(NH, (128, T), BF16, "kfm")
        v_tm = al.sbl(NTT, (128, D), BF16, "vtm")
        u16k = al.sb((128, 8192), BF16, "u16k")
        wbuf = al.sb((128, 1024), BF16, "wbuf")
        dec = al.sb((128, 1024), F32, "dec")
        srow = al.sb((128, T), F32, "srow")
        p_bf = al.sb((128, T), BF16, "pbf")
        a_f32 = al.sb((128, 512), F32, "af32")
        e_t = al.sb((128, 512), F32, "et")
        acc_t = al.sb((128, 512), F32, "acct")
        n_i = al.sb((128, 512), mybir.dt.int32, "ni")
        nf = al.sb((128, 512), F32, "nf")
        ngt = al.sb((128, 8), F32, "ngt")
        m128 = al.sb((128, 128), F32, "m128s")
        nm = al.sb((128, 1), F32, "nm")
        zrow = al.sb((128, 1), F32, "zrow")
        rz = al.sb((128, 1), F32, "rz")
        bo_c = al.sb((128, 8), F32, "boc")
        ga_c = al.sb((128, 8), F32, "gac")
        be_c = al.sb((128, 8), F32, "bec")
        xold = al.sb((128, 1024), F32, "xold")
        t1 = al.sb((128, 1024), F32, "t1")
        ps_big = al.ps((128, 2048), "psbig")
        ps_pv = al.ps((128, 512), "pspv")

        p.dma(m128[:], m128_d.ap())
        p.dma(ngt[:], th_d.ap().broadcast_to([128, 8]))
        p.add("vector", lambda e: e.tensor_scalar_mul(ngt[:], ngt[:], -1.0))
        load_scale_cols(p, bo_c, bo_d, 8)
        load_scale_cols(p, ga_c, ga_d, 8)
        load_scale_cols(p, be_c, be_d, 8)

        for tt in range(NTT):
            t_end = (tt + 1) * 128
            for sb in range((t_end + 511) // 512):
                w = min(512, t_end - sb * 512)
                p.add("gpsimd", lambda e, tt=tt, sb=sb, w=w: e.iota(
                    n_i[:, 0:w], [[-1, w]], base=tt * 128 - sb * 512,
                    channel_multiplier=1))
                p.add("vector", lambda e, w=w: e.tensor_copy(nf[:, 0:w], n_i[:, 0:w]))
                p.add("vector", lambda e, w=w: e.tensor_scalar_max(nf[:, 0:w], nf[:, 0:w], 0.0))
                for h in range(NH):
                    if h == 0:
                        p.add("scalar", lambda e, w=w, h=h: e.activation(
                            acc_t[:, 0:w], nf[:, 0:w], AF.Exp, scale=ngt[:, h:h + 1]))
                    else:
                        p.add("scalar", lambda e, w=w, h=h: e.activation(
                            e_t[:, 0:w], nf[:, 0:w], AF.Exp, scale=ngt[:, h:h + 1]))
                        p.add("vector", lambda e, w=w: e.tensor_add(
                            acc_t[:, 0:w], acc_t[:, 0:w], e_t[:, 0:w]))
                p.add("vector", lambda e, w=w: e.tensor_scalar_mul(
                    acc_t[:, 0:w], acc_t[:, 0:w], 1.0 / (NH * float(np.sqrt(HD)))))
                p.dma(decay_d.ap()[tt * 128:(tt + 1) * 128, sb * 512:sb * 512 + w],
                      acc_t[:, 0:w])

        for b in range(BPC):
            tok0 = b * T
            for tb in range(T // 512):
                for c in range(DC):
                    p.cast_dma(xtb[c][:], x_d.ap()[c * 128:(c + 1) * 128,
                                                   tok0 + tb * 512:tok0 + (tb + 1) * 512])
                emit_ln(p, lb, ones_b, xnb, tb * 512, kk, xt=xtb, nt=512)
            for w_b, dst in ((wq_b, q_fm), (wk_b, k_fm)):
                for oc in range(NH):
                    p.dma(wbuf[:], w_b.ap()[oc])
                    for tb in range(4):
                        for c in range(DC):
                            p.add("tensor", lambda e, oc=oc, tb=tb, c=c, dst=dst: e.matmul(
                                ps_big[:, 0:512], wbuf[:, c * 128:(c + 1) * 128],
                                xnb[c][:, tb * 512:(tb + 1) * 512],
                                start=(c == 0), stop=(c == DC - 1)))
                        p.add("vector", lambda e, oc=oc, tb=tb, dst=dst: e.tensor_copy(
                            dst[oc][:, tb * 512:(tb + 1) * 512], ps_big[:, 0:512]))
            for c in range(DC):
                p.dma(u16k[:, c * 1024:(c + 1) * 1024],
                      wv_b.ap()[c * 128:(c + 1) * 128, :])
            for tt in range(NTT):
                for half in range(2):
                    for c in range(DC):
                        p.add("tensor", lambda e, tt=tt, c=c, half=half: e.matmul(
                            ps_big[:, half * 512:(half + 1) * 512],
                            xnb[c][:, tt * 128:(tt + 1) * 128],
                            u16k[:, c * 1024 + half * 512:c * 1024 + (half + 1) * 512],
                            start=(c == 0), stop=(c == DC - 1)))
                    p.add("vector", lambda e, tt=tt, half=half: e.tensor_copy(
                        v_tm[tt][:, half * 512:(half + 1) * 512],
                        ps_big[:, half * 512:(half + 1) * 512]))

            for h in range(NH):
                for g in range(4):
                    g_end = (g + 1) * 512
                    nsc = g_end // 128
                    for ttg in range(4):
                        tt = g * 4 + ttg
                        t_end = (tt + 1) * 128
                        for sb in range((t_end + 511) // 512):
                            w = min(512, t_end - sb * 512)
                            p.add("tensor", lambda e, h=h, tt=tt, sb=sb, w=w: e.matmul(
                                ps_big[:, sb * 512:sb * 512 + w],
                                q_fm[h][:, tt * 128:(tt + 1) * 128],
                                k_fm[h][:, sb * 512:sb * 512 + w],
                                start=True, stop=True))
                        for half in range((t_end + 1023) // 1024):
                            w = min(1024, t_end - half * 1024)
                            p.dma(dec[:, 0:w],
                                  decay_d.ap()[tt * 128:(tt + 1) * 128,
                                               half * 1024:half * 1024 + w])
                            p.add("vector", lambda e, half=half, w=w: e.tensor_tensor(
                                srow[:, half * 1024:half * 1024 + w],
                                ps_big[:, half * 1024:half * 1024 + w],
                                dec[:, 0:w], ALU.mult))
                        p.add("vector", lambda e, tt=tt, t_end=t_end: e.tensor_add(
                            srow[:, tt * 128:t_end], srow[:, tt * 128:t_end], m128[:]))
                        p.add("vector", lambda e, t_end=t_end: e.reduce_max(
                            nm[:], srow[:, 0:t_end], axis=AX.X, negate=True))
                        p.add("scalar", lambda e, t_end=t_end: e.activation(
                            srow[:, 0:t_end], srow[:, 0:t_end], AF.Exp,
                            bias=nm[:], accum_out=zrow[:]))
                        p.add("vector", lambda e: e.reciprocal(rz[:], zrow[:]))
                        p.add("vector", lambda e, t_end=t_end: e.tensor_scalar_mul(
                            p_bf[:, 0:t_end], srow[:, 0:t_end], rz[:]))
                        if t_end < g_end:
                            p.add("vector", lambda e, t_end=t_end, g_end=g_end: e.memset(
                                p_bf[:, t_end:g_end], 0.0))
                        for sc in range(nsc):
                            p.dma(u16k[:, sc * 512 + ttg * 128:sc * 512 + (ttg + 1) * 128],
                                  p_bf[:, sc * 128:(sc + 1) * 128], transpose=True)
                    for sc in range(nsc):
                        p.add("tensor", lambda e, h=h, sc=sc, nsc=nsc: e.matmul(
                            ps_pv[:], v_tm[sc][:, h * 128:(h + 1) * 128],
                            u16k[:, sc * 512:(sc + 1) * 512],
                            start=(sc == 0), stop=(sc == nsc - 1)))
                    p.add("vector", lambda e: e.tensor_copy(a_f32[:], ps_pv[:]))
                    emit_rowstats(p, lb, ones_b, [a_f32], 1, 128, kk, nt=512)
                    p.add("vector", lambda e: e.tensor_tensor(
                        lb.sq[:, 0:512], a_f32[:], lb.mbb[:, 0:512], ALU.subtract))
                    p.add("vector", lambda e: e.tensor_tensor(
                        lb.sq[:, 0:512], lb.sq[:, 0:512], lb.rbb[:, 0:512], ALU.mult))
                    p.add("vector", lambda e, h=h, g=g: e.tensor_scalar(
                        xnb[h][:, g * 512:(g + 1) * 512], lb.sq[:, 0:512],
                        ga_c[:, h:h + 1], be_c[:, h:h + 1], ALU.mult, ALU.add))

            for oc in range(DC):
                p.dma(wbuf[:], wo_b.ap()[oc])
                for tb in range(4):
                    for c in range(DC):
                        p.add("tensor", lambda e, oc=oc, tb=tb, c=c: e.matmul(
                            ps_big[:, 0:512], wbuf[:, c * 128:(c + 1) * 128],
                            xnb[c][:, tb * 512:(tb + 1) * 512],
                            start=(c == 0), stop=(c == DC - 1)))
                    p.add("vector", lambda e, oc=oc: e.tensor_scalar_add(
                        t1[:, 0:512], ps_big[:, 0:512], bo_c[:, oc:oc + 1]))
                    p.dma(xold[:, 0:512], x_d.ap()[oc * 128:(oc + 1) * 128,
                                                   tok0 + tb * 512:tok0 + (tb + 1) * 512])
                    p.add("vector", lambda e: e.tensor_add(t1[:, 0:512], t1[:, 0:512], xold[:, 0:512]))
                    p.dma(xo_d.ap()[oc * 128:(oc + 1) * 128,
                                    tok0 + tb * 512:tok0 + (tb + 1) * 512], t1[:, 0:512])
        scope_cm.__exit__(None, None, None)
        if standalone:
            return _finish()


def attn_a_inputs(sh):
    return {k: sh[k] for k in ("aa_Wqt", "aa_Wkt", "aa_Wv", "aa_Wot", "aa_bo",
                               "aa_theta", "aa_gamma", "aa_beta", "m128")}




def build_mamba(FC=None):
    standalone = FC is None
    with contextlib.ExitStack() as outer:
        if standalone:
            FC, _finish = _standalone_ctx(outer)
        nc, p, kk, ones_b = FC.nc, FC.p, FC.kk, FC.ones_b
        x_d, xo_d = FC.x_d, FC.xo_d
        win_b = FC.prep_wtiled("mb_WinTt", 32, 1024)
        wdt_b = FC.prep_wtiled("mb_WdtTt", 16, 2048)
        wout_b = FC.prep_wtiled("mb_WoutTt", 8, 2048)
        cw_d = FC.dram("conv_w", [2048, 4])
        cb_d = FC.dram("conv_b", [2048])
        alog_d = FC.dram("A_log", [2048, 16])
        bp_d = FC.dram("B_param", [2048, 16])
        cp_d = FC.dram("C_param", [2048, 16])
        dp_d = FC.dram("D_param", [2048])
        bdt_d = FC.dram("b_dt", [2048])
        dtb_d = FC.dram("dt_bias", [2048])

        NI = 16
        LB = 512
        NLB = T // LB
        scope_cm = FC.scope()
        al = scope_cm.__enter__()
        lb_ = LNBufs(al, nc, 512, with_x=False)
        xtb = al.sbl(DC, (128, 512), BF16, "xtb")
        xnb = al.sbl(DC, (128, T), BF16, "xnb")
        xbp = al.sbl(NI, (128, 3 + LB), F32, "xbp")
        u_f = al.sbl(NI, (128, LB), F32, "uf")
        u_b = al.sbl(NI, (128, LB), BF16, "ub")
        y_b = al.sbl(NI, (128, LB), BF16, "yb")
        carry = al.sb((128, NI * 16), F32, "carry")
        cw = al.sb((128, 64), F32, "cw")
        cb_c = al.sb((128, 16), F32, "cbc")
        A_sb = al.sb((128, 256), F32, "Asb")
        B_sb = al.sb((128, 256), F32, "Bsb")
        C_sb = al.sb((128, 256), F32, "Csb")
        D_c = al.sb((128, 16), F32, "Dc")
        bias_c = al.sb((128, 16), F32, "biasc")
        tmp16 = al.sb((128, 16), F32, "tmp16")
        wbuf = al.sb((128, 1024), BF16, "wbuf")
        wdtbuf = al.sb((128, 2048), BF16, "wdtbuf")
        sdt = al.sb((128, LB), F32, "sdt")
        tA = al.sb((128, LB), F32, "tA")
        tL = al.sb((128, LB), F32, "tL")
        dt_t = al.sb((128, LB), F32, "dtt")
        wdu = al.sb((128, LB), F32, "wdu")
        dA_t = al.sb((128, LB), F32, "dAt")
        dBu_t = al.sb((128, LB), F32, "dBut")
        s_t = al.sb((128, LB), F32, "st")
        y0 = al.sb((128, LB), F32, "y0")
        y1 = al.sb((128, LB), F32, "y1")
        g_t = al.sb((128, LB), F32, "gt")
        xold = al.sb((128, LB), F32, "xold")
        t1 = al.sb((128, LB), F32, "t1")
        ps1 = al.ps((128, 512), "ps1")
        ps2 = al.ps((128, 512), "ps2")

        load_scale_cols(p, cb_c, cb_d, 16)
        load_scale_cols(p, D_c, dp_d, 16)
        load_scale_cols(p, bias_c, bdt_d, 16)
        load_scale_cols(p, tmp16, dtb_d, 16)
        p.add("vector", lambda e: e.tensor_add(bias_c[:], bias_c[:], tmp16[:]))
        for c in range(NI):
            p.dma(cw[:, c * 4:(c + 1) * 4], cw_d.ap()[c * 128:(c + 1) * 128, :])
            p.dma(A_sb[:, c * 16:(c + 1) * 16], alog_d.ap()[c * 128:(c + 1) * 128, :])
            p.dma(B_sb[:, c * 16:(c + 1) * 16], bp_d.ap()[c * 128:(c + 1) * 128, :])
            p.dma(C_sb[:, c * 16:(c + 1) * 16], cp_d.ap()[c * 128:(c + 1) * 128, :])
        p.add("scalar", lambda e: e.activation(A_sb[:], A_sb[:], AF.Exp))
        p.add("vector", lambda e: e.tensor_scalar_mul(A_sb[:], A_sb[:], -1.0))

        for b in range(BPC):
            tok0 = b * T
            for tb in range(T // 512):
                for c in range(DC):
                    p.cast_dma(xtb[c][:], x_d.ap()[c * 128:(c + 1) * 128,
                                                   tok0 + tb * 512:tok0 + (tb + 1) * 512])
                emit_ln(p, lb_, ones_b, xnb, tb * 512, kk, xt=xtb, nt=512)

            for li in range(NLB):
                l0 = li * LB
                for oc in range(NI):
                    if li == 0:
                        p.add("vector", lambda e, oc=oc: e.memset(xbp[oc][:, 0:3], 0.0))
                    else:
                        p.add("vector", lambda e, oc=oc: e.tensor_copy(
                            xbp[oc][:, 0:3], xbp[oc][:, LB:LB + 3]))
                for oc in range(NI):
                    p.dma(wbuf[:], win_b.ap()[oc])
                    for c in range(DC):
                        p.add("tensor", lambda e, oc=oc, c=c, l0=l0: e.matmul(
                            ps1[:], wbuf[:, c * 128:(c + 1) * 128],
                            xnb[c][:, l0:l0 + LB], start=(c == 0), stop=(c == DC - 1)))
                    p.add("vector", lambda e, oc=oc: e.tensor_copy(xbp[oc][:, 3:3 + LB], ps1[:]))
                    p.add("vector", lambda e, oc=oc: e.tensor_scalar_mul(
                        y0[:], xbp[oc][:, 0:LB], cw[:, oc * 4:oc * 4 + 1]))
                    for kkk in range(1, 4):
                        p.add("vector", lambda e, oc=oc, kkk=kkk: e.scalar_tensor_tensor(
                            y0[:], xbp[oc][:, kkk:kkk + LB], cw[:, oc * 4 + kkk:oc * 4 + kkk + 1],
                            y0[:], ALU.mult, ALU.add))
                    p.add("scalar", lambda e, oc=oc: e.activation(
                        u_f[oc][:], y0[:], AF.Silu, bias=cb_c[:, oc:oc + 1]))
                    p.add("vector", lambda e, oc=oc: e.tensor_copy(u_b[oc][:], u_f[oc][:]))

                for oc in range(NI):
                    p.dma(wdtbuf[:], wdt_b.ap()[oc])
                    for c in range(NI):
                        p.add("tensor", lambda e, oc=oc, c=c: e.matmul(
                            ps1[:], wdtbuf[:, c * 128:(c + 1) * 128], u_b[c][:],
                            start=(c == 0), stop=(c == NI - 1)))
                    p.add("vector", lambda e, oc=oc: e.tensor_scalar_add(
                        sdt[:], ps1[:], bias_c[:, oc:oc + 1]))
                    k_m1 = kk(-1.0)
                    k_p1 = kk(1.0)
                    p.add("scalar", lambda e: e.activation(tA[:], sdt[:], AF.Abs))
                    p.add("scalar", lambda e, k_m1=k_m1: e.activation(tA[:], tA[:], AF.Exp, scale=k_m1))
                    p.add("scalar", lambda e, k_p1=k_p1: e.activation(tL[:], tA[:], AF.Ln, bias=k_p1))
                    p.add("vector", lambda e: e.tensor_scalar_max(dt_t[:], sdt[:], 0.0))
                    p.add("vector", lambda e: e.tensor_add(dt_t[:], dt_t[:], tL[:]))
                    p.add("vector", lambda e, oc=oc: e.tensor_tensor(
                        wdu[:], dt_t[:], u_f[oc][:], ALU.mult))
                    for n in range(16):
                        cn = oc * 16 + n
                        p.add("scalar", lambda e, cn=cn: e.activation(
                            dA_t[:], dt_t[:], AF.Exp, scale=A_sb[:, cn:cn + 1]))
                        p.add("vector", lambda e, cn=cn: e.tensor_scalar_mul(
                            dBu_t[:], wdu[:], B_sb[:, cn:cn + 1]))
                        if li == 0:
                            p.add("vector", lambda e: e.tensor_tensor_scan(
                                s_t[:], dA_t[:], dBu_t[:], 0.0, ALU.mult, ALU.add))
                        else:
                            p.add("vector", lambda e, cn=cn: e.tensor_tensor_scan(
                                s_t[:], dA_t[:], dBu_t[:], carry[:, cn:cn + 1],
                                ALU.mult, ALU.add))
                        p.add("vector", lambda e, cn=cn: e.tensor_copy(
                            carry[:, cn:cn + 1], s_t[:, LB - 1:LB]))
                        if n == 0:
                            p.add("vector", lambda e, cn=cn: e.tensor_scalar_mul(
                                y0[:], s_t[:], C_sb[:, cn:cn + 1]))
                        else:
                            src_y, dst_y = (y0, y1) if n % 2 == 1 else (y1, y0)
                            p.add("vector", lambda e, cn=cn, s=src_y, d=dst_y: e.scalar_tensor_tensor(
                                d[:], s_t[:], C_sb[:, cn:cn + 1], s[:], ALU.mult, ALU.add))
                    y_fin = y1 if 15 % 2 == 1 else y0
                    p.dma(wbuf[:], win_b.ap()[16 + oc])
                    for c in range(DC):
                        p.add("tensor", lambda e, oc=oc, c=c, l0=l0: e.matmul(
                            ps2[:], wbuf[:, c * 128:(c + 1) * 128],
                            xnb[c][:, l0:l0 + LB], start=(c == 0), stop=(c == DC - 1)))
                    p.add("scalar", lambda e: e.activation(g_t[:], ps2[:], AF.Silu))
                    p.add("vector", lambda e, oc=oc, y_fin=y_fin: e.scalar_tensor_tensor(
                        t1[:], u_f[oc][:], D_c[:, oc:oc + 1], y_fin[:], ALU.mult, ALU.add))
                    p.add("vector", lambda e, oc=oc: e.tensor_tensor(
                        y_b[oc][:], t1[:], g_t[:], ALU.mult))

                for oc in range(DC):
                    p.dma(wdtbuf[:], wout_b.ap()[oc])
                    for c in range(NI):
                        p.add("tensor", lambda e, oc=oc, c=c: e.matmul(
                            ps1[:], wdtbuf[:, c * 128:(c + 1) * 128], y_b[c][:],
                            start=(c == 0), stop=(c == NI - 1)))
                    p.dma(xold[:], x_d.ap()[oc * 128:(oc + 1) * 128, tok0 + l0:tok0 + l0 + LB])
                    p.add("vector", lambda e: e.tensor_tensor(t1[:], ps1[:], xold[:], ALU.add))
                    p.dma(xo_d.ap()[oc * 128:(oc + 1) * 128, tok0 + l0:tok0 + l0 + LB], t1[:])
        scope_cm.__exit__(None, None, None)
        if standalone:
            return _finish()


def mamba_inputs(sh):
    return {
        "mb_WinTt": sh["mb_WinTt"], "mb_WdtTt": sh["mb_WdtTt"], "mb_WoutTt": sh["mb_WoutTt"],
        "conv_w": sh["mb_conv_w"], "conv_b": sh["mb_conv_b"], "A_log": sh["mb_A_log"],
        "B_param": sh["mb_B"], "C_param": sh["mb_C"], "D_param": sh["mb_D"],
        "b_dt": sh["mb_b_dt"], "dt_bias": sh["mb_dt_bias"],
    }




def build_rails(FC=None):
    standalone = FC is None
    with contextlib.ExitStack() as outer:
        if standalone:
            FC, _finish = _standalone_ctx(outer)
        nc, p, kk, ones_b = FC.nc, FC.p, FC.kk, FC.ones_b
        x_d, xo_d = FC.x_d, FC.xo_d
        rm_b = FC.precast("rmBD", [4, 8, 128, 128])
        rup_b = FC.precast("rupBD", [4, 8, 128, 512])
        wc_b = FC.precast("wcT", [16384, 1024])

        NT = 512
        scope_cm = FC.scope()
        al = scope_cm.__enter__()
        lb = LNBufs(al, nc, NT)
        xn = al.sbl(DC, (128, NT), BF16, "xn")
        rm_sb = al.sb((128, 4096), BF16, "rmsb")
        rup_sb = al.sb((128, 16384), BF16, "rupsb")
        y_sb = al.sbl(32, (128, NT), BF16, "ysb")
        z_scr = al.sb((128, NT), BF16, "zscr")
        wc_sb = al.sb((128, 512), BF16, "wcsb")
        t1 = al.sb((128, NT), F32, "t1")
        psz = al.ps((128, 512), "psz")
        ps_o = al.ps((128, 2048), "pso")

        for h in range(4):
            for rb in range(8):
                i = h * 8 + rb
                p.dma(rm_sb[:, i * 128:(i + 1) * 128], rm_b.ap()[h, rb])
                p.dma(rup_sb[:, i * 512:(i + 1) * 512], rup_b.ap()[h, rb])

        for tb in range(NTOK // NT):
            col0 = tb * NT
            xt = emit_ln(p, lb, ones_b, xn, 0, kk, x_d=x_d, tok0=col0)
            for i in range(32):
                rb = i % 8
                p.add("tensor", lambda e, i=i, rb=rb: e.matmul(
                    psz[:], rm_sb[:, i * 128:(i + 1) * 128], xn[rb][:],
                    start=True, stop=True))
                p.add("vector", lambda e, i=i: e.tensor_copy(y_sb[i][:], psz[:]))
            for half in range(2):
                p.dma(wc_sb[:], wc_b.ap()[0:128, half * 512:(half + 1) * 512])
                for ch in range(128):
                    i, zt = ch // 4, ch % 4
                    p.add("tensor", lambda e, i=i, zt=zt: e.matmul(
                        psz[:], rup_sb[:, i * 512 + zt * 128:i * 512 + (zt + 1) * 128],
                        y_sb[i][:], start=True, stop=True))
                    p.add("scalar", lambda e: e.activation(z_scr[:], psz[:], AF.Gelu))
                    if ch > 0:
                        p.dma(wc_sb[:], wc_b.ap()[ch * 128:(ch + 1) * 128,
                                                  half * 512:(half + 1) * 512])
                    for oi in range(4):
                        p.add("tensor", lambda e, ch=ch, oi=oi: e.matmul(
                            ps_o[:, oi * 512:(oi + 1) * 512],
                            wc_sb[:, oi * 128:(oi + 1) * 128], z_scr[:],
                            start=(ch == 0), stop=(ch == 127)))
                for oi in range(4):
                    oc = half * 4 + oi
                    p.dma(lb.sq[:, 0:NT], x_d.ap()[oc * 128:(oc + 1) * 128, col0:col0 + NT])
                    p.add("vector", lambda e, oi=oi: e.tensor_tensor(
                        t1[:], ps_o[:, oi * 512:(oi + 1) * 512], lb.sq[:, 0:NT], ALU.add))
                    p.dma(xo_d.ap()[oc * 128:(oc + 1) * 128, col0:col0 + NT], t1[:])
        scope_cm.__exit__(None, None, None)
        if standalone:
            return _finish()


def rails_inputs(sh):
    return {"rmBD": sh["rf_rmBD"], "rupBD": sh["rf_rupBD"], "wcT": sh["rf_wcT"]}




def build_attn_a1(FC=None):
    standalone = FC is None
    with contextlib.ExitStack() as outer:
        if standalone:
            FC, _finish = _standalone_ctx(outer)
        nc, p, kk, ones_b = FC.nc, FC.p, FC.kk, FC.ones_b
        x_d, xo_d = FC.x_d, FC.xo_d
        wq_b2 = FC.precast("ma_Wq", [D, D])
        bq_d = FC.dram("ma_bq", [1, D])
        wd_b = FC.prep_wtiled("ma_Wdt", 2, 1024)
        bd_d = FC.dram("ma_bdown", [256])
        wu_b2 = FC.precast("ma_Wup", [256, 2048])
        bu_d = FC.dram("ma_bup", [1, 2048])
        wo_b2 = FC.prep_wtiled("ma_Wot", 8, 1024)
        bo_d = FC.dram("ma_bout", [D])
        cos_d = FC.dram("ma_cos", [T, 32])
        sin_d = FC.dram("ma_sin", [T, 32])

        scope_cm = FC.scope()
        al = scope_cm.__enter__()
        lb = LNBufs(al, nc, 512)
        xn = al.sbl(DC, (128, NTOK), BF16, "xn")
        lat = al.sbl(2, (128, NTOK), BF16, "lat")
        wq_sb = al.sbl(DC, (128, 1024), BF16, "wqsb")
        wu_sb = al.sbl(2, (128, 2048), BF16, "wusb")
        wbuf = al.sb((128, 1024), BF16, "wbuf")
        bq_b = al.sb((128, 1024), F32, "bqb")
        bu_b = al.sb((128, 2048), F32, "bub")
        bd_c = al.sb((128, 2), F32, "bdc")
        bo_c = al.sb((128, 8), F32, "boc")
        cos_t = al.sb((128, 32), F32, "cost")
        sin_t = al.sb((128, 32), F32, "sint")
        q_t = al.sb((128, 1024), F32, "qt")
        kv_t = al.sb((128, 2048), F32, "kvt")
        q_ro = al.sb((128, 1024), F32, "qro")
        k_ro = al.sb((128, 1024), F32, "kro")
        v_c = al.sb((128, 1024), F32, "vc")
        tr1 = al.sb((128, 512), F32, "tr1")
        tr2 = al.sb((128, 512), F32, "tr2")
        S1 = al.sb((128, 256), F32, "S1")
        S2 = al.sb((128, 256), F32, "S2")
        nmx = al.sb((128, 16), F32, "nmx")
        zs = al.sb((128, 16), F32, "zs")
        rzz = al.sb((128, 16), F32, "rzz")
        o_acc = al.sb((128, 1024), F32, "oacc")
        tmps = al.sb((128, 1024), F32, "tmps")
        o_bf = al.sb((128, 1024), BF16, "obf")
        a_tr = al.sb((128, 1024), BF16, "atr")
        xold = al.sb((128, 512), F32, "xold")
        t1 = al.sb((128, 512), F32, "t1")
        ps1 = al.ps((128, 512), "ps1")

        p.dma(bq_b[:], bq_d.ap().broadcast_to([128, 1024]))
        p.dma(bu_b[:], bu_d.ap().broadcast_to([128, 2048]))
        load_scale_cols(p, bd_c, bd_d, 2)
        load_scale_cols(p, bo_c, bo_d, 8)
        for c in range(DC):
            p.dma(wq_sb[c][:], wq_b2.ap()[c * 128:(c + 1) * 128, :])
        for c in range(2):
            p.dma(wu_sb[c][:], wu_b2.ap()[c * 128:(c + 1) * 128, :])

        for tb in range(NTOK // 512):
            emit_ln(p, lb, ones_b, xn, tb * 512, kk, x_d=x_d, tok0=tb * 512)
        for oc in range(2):
            p.dma(wbuf[:], wd_b.ap()[oc])
            for tb in range(NTOK // 512):
                for c in range(DC):
                    p.add("tensor", lambda e, oc=oc, tb=tb, c=c: e.matmul(
                        ps1[:], wbuf[:, c * 128:(c + 1) * 128],
                        xn[c][:, tb * 512:(tb + 1) * 512],
                        start=(c == 0), stop=(c == DC - 1)))
                p.add("vector", lambda e, oc=oc, tb=tb: e.tensor_scalar_add(
                    lat[oc][:, tb * 512:(tb + 1) * 512], ps1[:], bd_c[:, oc:oc + 1]))

        H = 16
        for tt in range(NTOK // 128):
            pos0 = (tt % 16) * 128
            for half in range(2):
                for c in range(DC):
                    p.add("tensor", lambda e, tt=tt, c=c, half=half: e.matmul(
                        ps1[:], xn[c][:, tt * 128:(tt + 1) * 128],
                        wq_sb[c][:, half * 512:(half + 1) * 512],
                        start=(c == 0), stop=(c == DC - 1)))
                p.add("vector", lambda e, half=half: e.tensor_tensor(
                    q_t[:, half * 512:(half + 1) * 512], ps1[:],
                    bq_b[:, half * 512:(half + 1) * 512], ALU.add))
            for half in range(4):
                for c in range(2):
                    p.add("tensor", lambda e, tt=tt, c=c, half=half: e.matmul(
                        ps1[:], lat[c][:, tt * 128:(tt + 1) * 128],
                        wu_sb[c][:, half * 512:(half + 1) * 512],
                        start=(c == 0), stop=(c == 1)))
                p.add("vector", lambda e, half=half: e.tensor_tensor(
                    kv_t[:, half * 512:(half + 1) * 512], ps1[:],
                    bu_b[:, half * 512:(half + 1) * 512], ALU.add))
            p.dma(cos_t[:], cos_d.ap()[pos0:pos0 + 128, :])
            p.dma(sin_t[:], sin_d.ap()[pos0:pos0 + 128, :])
            cosb = cos_t[:].unsqueeze(1).broadcast_to([128, 16, 32])
            sinb = sin_t[:].unsqueeze(1).broadcast_to([128, 16, 32])
            q1 = bass.AP(q_t, 0, [[1024, 128], [64, 16], [1, 32]])
            q2 = bass.AP(q_t, 32, [[1024, 128], [64, 16], [1, 32]])
            qoA = bass.AP(q_ro, 0, [[1024, 128], [64, 16], [1, 32]])
            qoB = bass.AP(q_ro, 32, [[1024, 128], [64, 16], [1, 32]])
            tv = tr1[:].rearrange("p (h d) -> p h d", h=16)
            tv2 = tr2[:].rearrange("p (h d) -> p h d", h=16)
            k1 = bass.AP(kv_t, 0, [[2048, 128], [128, 16], [1, 32]])
            k2 = bass.AP(kv_t, 32, [[2048, 128], [128, 16], [1, 32]])
            koA = bass.AP(k_ro, 0, [[1024, 128], [64, 16], [1, 32]])
            koB = bass.AP(k_ro, 32, [[1024, 128], [64, 16], [1, 32]])
            for (a1, a2, oA, oB) in ((q1, q2, qoA, qoB), (k1, k2, koA, koB)):
                p.add("vector", lambda e, a1=a1, cb=cosb: e.tensor_tensor(tv, a1, cb, ALU.mult))
                p.add("vector", lambda e, a2=a2, sb=sinb: e.tensor_tensor(tv2, a2, sb, ALU.mult))
                p.add("vector", lambda e, oA=oA: e.tensor_tensor(oA, tv, tv2, ALU.subtract))
                p.add("vector", lambda e, a1=a1, sb=sinb: e.tensor_tensor(tv, a1, sb, ALU.mult))
                p.add("vector", lambda e, a2=a2, cb=cosb: e.tensor_tensor(tv2, a2, cb, ALU.mult))
                p.add("vector", lambda e, oB=oB: e.tensor_tensor(oB, tv, tv2, ALU.add))
            vsrc = bass.AP(kv_t, 64, [[2048, 128], [128, 16], [1, 64]])
            p.add("vector", lambda e: e.tensor_copy(
                v_c[:].rearrange("p (h d) -> p h d", h=16), vsrc))
            qv = q_ro[:].rearrange("p (h d) -> p h d", h=16)
            for g in range(16):
                kg = k_ro[:, g * 64:(g + 1) * 64].unsqueeze(1).broadcast_to([128, 16, 64])
                p.add("vector", lambda e, kg=kg: e.tensor_tensor(
                    tmps[:].rearrange("p (h d) -> p h d", h=16), qv, kg, ALU.mult))
                sout = bass.AP(S1, g, [[256, 128], [16, 16]])
                p.add("vector", lambda e, sout=sout: e.tensor_reduce(
                    sout, tmps[:].rearrange("p (h d) -> p h d", h=16),
                    axis=AX.X, op=ALU.add))
            sv = S1[:].rearrange("p (h g) -> p h g", h=16)
            p.add("vector", lambda e: e.reduce_max(nmx[:], sv, axis=AX.X, negate=True))
            p.add("vector", lambda e: e.tensor_tensor(
                S2[:].rearrange("p (h g) -> p h g", h=16), sv,
                nmx[:].unsqueeze(2).broadcast_to([128, 16, 16]), ALU.add))
            k_s = kk(0.125)
            p.add("scalar", lambda e, k_s=k_s: e.activation(S2[:], S2[:], AF.Exp, scale=k_s))
            p.add("vector", lambda e: e.tensor_reduce(
                zs[:], S2[:].rearrange("p (h g) -> p h g", h=16), axis=AX.X, op=ALU.add))
            p.add("vector", lambda e: e.reciprocal(rzz[:], zs[:]))
            p.add("vector", lambda e: e.tensor_tensor(
                S2[:].rearrange("p (h g) -> p h g", h=16),
                S2[:].rearrange("p (h g) -> p h g", h=16),
                rzz[:].unsqueeze(2).broadcast_to([128, 16, 16]), ALU.mult))
            for g in range(16):
                pg = bass.AP(S2, g, [[256, 128], [16, 16], [0, 64]])
                vg = v_c[:, g * 64:(g + 1) * 64].unsqueeze(1).broadcast_to([128, 16, 64])
                if g == 0:
                    p.add("vector", lambda e, pg=pg, vg=vg: e.tensor_tensor(
                        o_acc[:].rearrange("p (h d) -> p h d", h=16), pg, vg, ALU.mult))
                else:
                    p.add("vector", lambda e, pg=pg, vg=vg: e.tensor_tensor(
                        tmps[:].rearrange("p (h d) -> p h d", h=16), pg, vg, ALU.mult))
                    p.add("vector", lambda e: e.tensor_add(o_acc[:], o_acc[:], tmps[:]))
            p.add("vector", lambda e: e.tensor_copy(o_bf[:], o_acc[:]))
            for c in range(DC):
                p.dma(a_tr[:, c * 128:(c + 1) * 128],
                      o_bf[:, c * 128:(c + 1) * 128], transpose=True)
            for c in range(DC):
                p.add("vector", lambda e, c=c, tt=tt: e.tensor_copy(
                    xn[c][:, tt * 128:(tt + 1) * 128], a_tr[:, c * 128:(c + 1) * 128]))

        for oc in range(DC):
            p.dma(wbuf[:], wo_b2.ap()[oc])
            for tb in range(NTOK // 512):
                for c in range(DC):
                    p.add("tensor", lambda e, oc=oc, tb=tb, c=c: e.matmul(
                        ps1[:], wbuf[:, c * 128:(c + 1) * 128],
                        xn[c][:, tb * 512:(tb + 1) * 512],
                        start=(c == 0), stop=(c == DC - 1)))
                p.add("vector", lambda e, oc=oc: e.tensor_scalar_add(
                    t1[:], ps1[:], bo_c[:, oc:oc + 1]))
                p.dma(xold[:], x_d.ap()[oc * 128:(oc + 1) * 128, tb * 512:(tb + 1) * 512])
                p.add("vector", lambda e: e.tensor_add(t1[:], t1[:], xold[:]))
                p.dma(xo_d.ap()[oc * 128:(oc + 1) * 128, tb * 512:(tb + 1) * 512], t1[:])
        scope_cm.__exit__(None, None, None)
        if standalone:
            return _finish()


def attn_a1_inputs(sh):
    return {k: sh[k] for k in ("ma_Wq", "ma_bq", "ma_Wdt", "ma_bdown", "ma_Wup",
                               "ma_bup", "ma_Wot", "ma_bout", "ma_cos", "ma_sin")}




def build_attn_b(FC=None):
    standalone = FC is None
    with contextlib.ExitStack() as outer:
        if standalone:
            FC, _finish = _standalone_ctx(outer)
        nc, p, kk, ones_b = FC.nc, FC.p, FC.kk, FC.ones_b
        x_d, xo_d = FC.x_d, FC.xo_d
        qw_d = FC.dram("ab_qw_sl", [128, 1024])
        kw_b = FC.precast("ab_k_w", [D, D])
        vw_b = FC.precast("ab_v_w", [D, D])
        ow_b = FC.prep_wtiled("ab_owt", 8, 1024)
        qb_d = FC.dram("ab_qb_sl", [1, 128])
        kb_d = FC.dram("ab_k_b", [1, D])
        vb_d = FC.dram("ab_v_b", [1, D])
        ob_d = FC.dram("ab_out_b", [D])
        bk_d = FC.dram("ab_bias_k", [1, D])
        bv_d = FC.dram("ab_bias_v", [1, D])
        gw_d = FC.dram("ab_grep_w", [64, 8])
        gb_d = FC.dram("ab_grep_b", [1, 8])
        ga_d = FC.dram("ab_grep_a", [1, 16])
        pb_d = FC.dram("ab_pb", [2, 272])

        cc_in = {}
        cc_out = {}
        for nm in ("k", "v"):
            cc_in[nm] = nc.dram_tensor(f"ccin_{nm}", [NTOK, D], BF16)
            cc_out[nm] = nc.dram_tensor(f"ccout_{nm}", [B * T, D], BF16)
        cc_in["xn"] = nc.dram_tensor("ccin_xn", [D, NTOK], BF16)
        cc_out["xn"] = nc.dram_tensor("ccout_xn", [NCORES * D, NTOK], BF16)
        scr_d = nc.dram_tensor("scr1", [1, 64], F32)

        scope_cm = FC.scope()
        al = scope_cm.__enter__()
        lb = LNBufs(al, nc, 512)
        xn = al.sbl(DC, (128, NTOK), BF16, "xn")
        wbuf = al.sb((128, 1024), BF16, "wbuf")
        biasrow = al.sb((128, 1024), F32, "biasrow")
        qtmp = al.sb((128, 1024), BF16, "qtmp")
        kf = al.sb((128, 17 * 1024), BF16, "kf")
        vf = al.sb((128, 17 * 1024), BF16, "vf")
        wsb = [kf[:, c * 1024:(c + 1) * 1024] for c in range(DC)]
        q_nn = al.sb((128, 2048), BF16, "qnn")
        qwsl = al.sb((128, 1024), BF16, "qwsl")
        qxt = al.sb((128, 1024), BF16, "qxt")
        qb_b = al.sb((128, 128), F32, "qbb")
        gv_b = al.sb((128, 64), F32, "gvb")
        ga_b = al.sb((128, 16), F32, "gab")
        gb_b = al.sb((128, 1), F32, "gbb")
        pb_b = al.sbl(2, (128, 272), F32, "pbb")
        gw_sb = al.sb((64, 8), F32, "gwsb")
        gv_col = al.sb((64, 1), F32, "gvcol")
        gb_row = al.sb((1, 8), F32, "gbrow")
        gb_one = al.sb((1, 1), F32, "gbone")
        S = al.sb((128, 272), F32, "S")
        tmp272 = al.sb((128, 272), F32, "tmp272")
        P_bf = al.sb((128, 272), BF16, "Pbf")
        nmx = al.sb((128, 16), F32, "nmx16")
        zs = al.sb((128, 16), F32, "zs16")
        rzz = al.sb((128, 16), F32, "rzz16")
        gdot = al.sb((128, 16), F32, "gdot")
        g0 = al.sb((128, 16), F32, "g0")
        tmps = al.sb((128, 1088), F32, "tmps")
        o_acc = al.sb((128, 1024), F32, "oacc")
        o_bf = al.sb((128, 1024), BF16, "obf")
        a_tr = al.sb((128, 1024), BF16, "atr")
        ob_c = al.sb((128, 8), F32, "obc")
        xold = al.sb((128, 512), F32, "xold")
        t1 = al.sb((128, 512), F32, "t1")
        ps1 = al.ps((128, 512), "ps1")
        ps2 = al.ps((128, 512), "ps2")

        load_scale_cols(p, ob_c, ob_d, 8)
        p.dma(gw_sb[:], gw_d.ap())
        p.add("vector", lambda e: e.tensor_reduce(gv_col[:], gw_sb[:], axis=AX.X, op=ALU.add))
        p.dma(gb_row[:], gb_d.ap())
        p.add("vector", lambda e: e.tensor_reduce(gb_one[:], gb_row[:], axis=AX.X, op=ALU.add))
        p.dma(scr_d.ap()[0:1, 0:1], gb_one[:])
        p.dma(scr_d.ap()[0:1, 1:2], gv_col[:].rearrange("p o -> o p")
              if False else gv_col[0:1, :])
        p.dma(gb_b[:], scr_d.ap()[0:1, 0:1].broadcast_to([128, 1]))
        gv_d = nc.dram_tensor("gvscr", [64, 1], F32)
        p.dma(gv_d.ap(), gv_col[:])
        p.dma(gv_b[:], bass.AP(gv_d, 0, [[0, 128], [1, 64]]))
        p.dma(ga_b[:], ga_d.ap().broadcast_to([128, 16]))
        for nl in range(2):
            p.dma(pb_b[nl][:], pb_d.ap()[nl:nl + 1, :].broadcast_to([128, 272]))
        p.cast_dma(qwsl[:], qw_d.ap())
        p.dma(qb_b[:], qb_d.ap().broadcast_to([128, 128]))

        for tb in range(NTOK // 512):
            emit_ln(p, lb, ones_b, xn, tb * 512, kk, x_d=x_d, tok0=tb * 512)

        for c in range(DC):
            p.dma(cc_in["xn"].ap()[c * 128:(c + 1) * 128, :], xn[c][:])
        p.collective("AllGather", ALU.bypass, [list(range(NCORES))],
                     [cc_in["xn"].ap()], [cc_out["xn"].ap()])
        for nm, w_b, b_d in (("k", kw_b, kb_d), ("v", vw_b, vb_d)):
            for c in range(DC):
                p.dma(wsb[c], w_b.ap()[c * 128:(c + 1) * 128, :])
            p.dma(biasrow[:], b_d.ap().broadcast_to([128, 1024]))
            for tt in range(NTOK // 128):
                for half in range(2):
                    for c in range(DC):
                        p.add("tensor", lambda e, tt=tt, c=c, half=half: e.matmul(
                            ps1[:], xn[c][:, tt * 128:(tt + 1) * 128],
                            wsb[c][:, half * 512:(half + 1) * 512],
                            start=(c == 0), stop=(c == DC - 1)))
                    p.add("vector", lambda e, half=half: e.tensor_tensor(
                        qtmp[:, half * 512:(half + 1) * 512], ps1[:],
                        biasrow[:, half * 512:(half + 1) * 512], ALU.add))
                p.dma(cc_in[nm].ap()[tt * 128:(tt + 1) * 128, :], qtmp[:])
            p.collective("AllGather", ALU.bypass, [list(range(NCORES))],
                         [cc_in[nm].ap()], [cc_out[nm].ap()])

        p.cast_dma(kf[:, 16 * 1024:].rearrange("p (o d) -> p o d", o=1),
                   bk_d.ap().unsqueeze(0).broadcast_to([128, 1, 1024]))
        p.cast_dma(vf[:, 16 * 1024:].rearrange("p (o d) -> p o d", o=1),
                   bv_d.ap().unsqueeze(0).broadcast_to([128, 1, 1024]))

        if DBG:
            dbg_kf = nc.dram_tensor("dbg_kf", [128, 2048], BF16, kind="ExternalOutput")
            dbg_qnn = nc.dram_tensor("dbg_qnn", [128, 2048], BF16, kind="ExternalOutput")
            dbg_S = nc.dram_tensor("dbg_S", [128, 272], F32, kind="ExternalOutput")
            dbg_gv = nc.dram_tensor("dbg_gv", [128, 64], F32, kind="ExternalOutput")
            dbg_gb = nc.dram_tensor("dbg_gb", [128, 1], F32, kind="ExternalOutput")
            dbg_g0 = nc.dram_tensor("dbg_g0", [128, 16], F32, kind="ExternalOutput")
            dbg_gd = nc.dram_tensor("dbg_gd", [128, 16], F32, kind="ExternalOutput")
            dbg_Sr = nc.dram_tensor("dbg_Sr", [128, 272], F32, kind="ExternalOutput")
            dbg_pb = nc.dram_tensor("dbg_pb", [128, 272], F32, kind="ExternalOutput")
            dbg_t272 = nc.dram_tensor("dbg_t272", [128, 272], F32, kind="ExternalOutput")
            p.dma(dbg_gv.ap(), gv_b[:])
            p.dma(dbg_gb.ap(), gb_b[:])
        for bt in range(16):
            for s2 in range(16):
                p.dma(kf[:, s2 * 1024:(s2 + 1) * 1024],
                      cc_out["k"].ap()[s2 * T + bt * 128:s2 * T + (bt + 1) * 128, :])
                p.dma(vf[:, s2 * 1024:(s2 + 1) * 1024],
                      cc_out["v"].ap()[s2 * T + bt * 128:s2 * T + (bt + 1) * 128, :])
            if DBG and bt == 0:
                p.dma(dbg_kf.ap(), kf[:, 0:2048])
            for tq in range(16):
                rbase = (tq // 2) * D
                cbase = (tq % 2) * T + bt * 128
                for c in range(DC):
                    p.dma(qxt[:, c * 128:(c + 1) * 128],
                          cc_out["xn"].ap()[rbase + c * 128:rbase + (c + 1) * 128,
                                            cbase:cbase + 128])
                for c in range(DC):
                    p.add("tensor", lambda e, c=c: e.matmul(
                        ps1[:, 0:128], qxt[:, c * 128:(c + 1) * 128],
                        qwsl[:, c * 128:(c + 1) * 128],
                        start=(c == 0), stop=(c == DC - 1)))
                p.add("vector", lambda e, tq=tq: e.tensor_tensor(
                    q_nn[:, tq * 128:(tq + 1) * 128], ps1[:, 0:128], qb_b[:], ALU.add))
            if DBG and bt == 0:
                p.dma(dbg_qnn.ap(), q_nn[:])
            for nl in range(2):
                for tq in range(16):
                    kwin = kf[:].rearrange("p (r h) -> p r h", h=64)[:, tq * 17:tq * 17 + 17, :]
                    qs = q_nn[:, tq * 128 + nl * 64:tq * 128 + nl * 64 + 64]\
                        .unsqueeze(1).broadcast_to([128, 17, 64])
                    p.add("vector", lambda e, qs=qs, kwin=kwin: e.tensor_tensor(
                        tmps[:].rearrange("p (r h) -> p r h", h=64), qs, kwin, ALU.mult))
                    p.add("vector", lambda e, tq=tq: e.tensor_reduce(
                        S[:, tq * 17:tq * 17 + 17],
                        tmps[:].rearrange("p (r h) -> p r h", h=64), axis=AX.X, op=ALU.add))
                qg = bass.AP(q_nn, nl * 64, [[2048, 128], [128, 16], [1, 64]])
                p.add("vector", lambda e, qg=qg: e.tensor_tensor(
                    tmps[:, 0:1024].rearrange("p (t d) -> p t d", t=16), qg,
                    gv_b[:].unsqueeze(1).broadcast_to([128, 16, 64]), ALU.mult))
                p.add("vector", lambda e: e.tensor_reduce(
                    gdot[:], tmps[:, 0:1024].rearrange("p (t d) -> p t d", t=16),
                    axis=AX.X, op=ALU.add))
                p.add("scalar", lambda e: e.activation(g0[:], gdot[:], AF.Sigmoid, bias=gb_b[:]))
                p.add("vector", lambda e: e.tensor_tensor(g0[:], g0[:], ga_b[:], ALU.mult))
                if DBG and bt == 0 and nl == 0:
                    p.dma(dbg_g0.ap(), g0[:])
                    p.dma(dbg_gd.ap(), gdot[:])
                    p.dma(dbg_Sr.ap(), S[:])
                    p.dma(dbg_pb.ap(), pb_b[0][:])
                p.add("vector", lambda e, nl=nl: e.tensor_tensor(
                    tmp272[:].rearrange("p (t s) -> p t s", t=16),
                    g0[:].unsqueeze(2).broadcast_to([128, 16, 17]),
                    pb_b[nl][:].rearrange("p (t s) -> p t s", t=16), ALU.mult))
                p.add("vector", lambda e: e.scalar_tensor_tensor(
                    S[:], S[:], 1.0 / 256.0, tmp272[:], ALU.mult, ALU.add))
                if DBG and bt == 0 and nl == 0:
                    p.dma(dbg_S.ap(), S[:])
                    p.dma(dbg_t272.ap(), tmp272[:])
                sv = S[:].rearrange("p (t s) -> p t s", t=16)
                p.add("vector", lambda e, sv=sv: e.reduce_max(nmx[:], sv, axis=AX.X, negate=True))
                p.add("vector", lambda e, sv=sv: e.tensor_tensor(
                    sv, sv, nmx[:].unsqueeze(2).broadcast_to([128, 16, 17]), ALU.add))
                p.add("scalar", lambda e: e.activation(S[:], S[:], AF.Exp))
                p.add("vector", lambda e, sv=sv: e.tensor_reduce(zs[:], sv, axis=AX.X, op=ALU.add))
                p.add("vector", lambda e: e.reciprocal(rzz[:], zs[:]))
                p.add("vector", lambda e, sv=sv: e.tensor_tensor(
                    sv, sv, rzz[:].unsqueeze(2).broadcast_to([128, 16, 17]), ALU.mult))
                p.add("vector", lambda e: e.tensor_copy(P_bf[:], S[:]))
                for tq in range(16):
                    pg = P_bf[:, tq * 17:tq * 17 + 17].unsqueeze(2).broadcast_to([128, 17, 64])
                    vwin = vf[:].rearrange("p (r h) -> p r h", h=64)[:, tq * 17:tq * 17 + 17, :]
                    oT = bass.AP(tmps, 0, [[1088, 128], [1, 17], [17, 64]])
                    p.add("vector", lambda e, pg=pg, vwin=vwin, oT=oT: e.tensor_tensor(
                        oT, pg, vwin, ALU.mult))
                    p.add("vector", lambda e, tq=tq: e.tensor_reduce(
                        o_acc[:, tq * 64:(tq + 1) * 64],
                        tmps[:, 0:1088].rearrange("p (d s) -> p d s", d=64),
                        axis=AX.X, op=ALU.add))
                p.add("vector", lambda e: e.tensor_copy(o_bf[:], o_acc[:]))
                for c in range(DC):
                    p.dma(a_tr[:, c * 128:(c + 1) * 128],
                          o_bf[:, c * 128:(c + 1) * 128], transpose=True)
                for c in range(DC):
                    p.add("vector", lambda e, c=c, nl=nl, bt=bt: e.tensor_copy(
                        xn[c][:, nl * T + bt * 128:nl * T + (bt + 1) * 128],
                        a_tr[:, c * 128:(c + 1) * 128]))

        for oc in range(DC):
            p.dma(wbuf[:], ow_b.ap()[oc])
            for tb in range(NTOK // 512):
                for c in range(DC):
                    p.add("tensor", lambda e, oc=oc, tb=tb, c=c: e.matmul(
                        ps2[:], wbuf[:, c * 128:(c + 1) * 128],
                        xn[c][:, tb * 512:(tb + 1) * 512],
                        start=(c == 0), stop=(c == DC - 1)))
                p.add("vector", lambda e, oc=oc: e.tensor_scalar_add(
                    t1[:], ps2[:], ob_c[:, oc:oc + 1]))
                p.dma(xold[:], x_d.ap()[oc * 128:(oc + 1) * 128, tb * 512:(tb + 1) * 512])
                p.add("vector", lambda e: e.tensor_add(t1[:], t1[:], xold[:]))
                p.dma(xo_d.ap()[oc * 128:(oc + 1) * 128, tb * 512:(tb + 1) * 512], t1[:])
        for nm2 in ("k", "v", "xn"):
            p.dma(qtmp[:], cc_out[nm2].ap()[0:128, 0:1024])
        scope_cm.__exit__(None, None, None)
        if standalone:
            return _finish()


def attn_b_inputs(sh, core):
    d = {k: sh[k] for k in ("ab_k_w", "ab_v_w", "ab_owt", "ab_k_b", "ab_v_b",
                            "ab_out_b", "ab_bias_k", "ab_bias_v", "ab_grep_w",
                            "ab_grep_b", "ab_grep_a")}
    d["ab_qw_sl"] = sh["ab_qw_sl"][core]
    d["ab_qb_sl"] = sh["ab_qb_sl"][core]
    d["ab_pb"] = sh["ab_pb"][core]
    return d




def _t(a):
    return np.ascontiguousarray(np.asarray(a).T)


def _tiled(w_in_out, noc):
    w = np.asarray(w_in_out, np.float32)
    nin, nout = w.shape
    ncc = nin // 128
    assert noc * 128 == nout
    t = w.reshape(ncc, 128, noc, 128).transpose(2, 1, 0, 3).reshape(noc, 128, ncc * 128)
    return np.ascontiguousarray(t)


def host_prep(inputs):
    sh = {}
    sh["w1Tt"] = _tiled(_t(inputs["L0_ffn_w1_w"]), 32)
    sh["w2Tt"] = _tiled(_t(inputs["L0_ffn_w2_w"]), 32)
    sh["h1Tt"] = _tiled(_t(inputs["L0_ffn_h1_w"]), 32)
    sh["h2Tt"] = _tiled(_t(inputs["L0_ffn_h2_w"]), 32)
    sh["w3Tt"] = _tiled(_t(inputs["L0_ffn_w3_w"]), 8)
    sh["w1s"] = np.asarray(inputs["L0_ffn_w1_s"])
    sh["w2s"] = np.asarray(inputs["L0_ffn_w2_s"])
    sh["h1s"] = np.asarray(inputs["L0_ffn_h1_s"])
    sh["h2s"] = np.asarray(inputs["L0_ffn_h2_s"])
    sh["w3s"] = np.asarray(inputs["L0_ffn_w3_s"])

    sh["aa_Wqt"] = _tiled(np.asarray(inputs["L0_attn_a_Wq"]), 8)
    sh["aa_Wkt"] = _tiled(np.asarray(inputs["L0_attn_a_Wk"]), 8)
    sh["aa_Wv"] = np.asarray(inputs["L0_attn_a_Wv"])
    sh["aa_Wot"] = _tiled(np.asarray(inputs["L0_attn_a_Wo"]), 8)
    sh["aa_bo"] = np.asarray(inputs["L0_attn_a_bo"])
    sh["aa_theta"] = np.ascontiguousarray(np.asarray(inputs["L0_attn_a_theta"]).reshape(1, 8))
    sh["aa_gamma"] = np.asarray(inputs["L0_attn_a_gamma"])
    sh["aa_beta"] = np.asarray(inputs["L0_attn_a_beta"])
    ii = np.arange(128)
    sh["m128"] = np.where(ii[:, None] >= ii[None, :], 0.0, -1e30).astype(np.float32)

    sh["mb_WinTt"] = _tiled(_t(inputs["L1_attn_b_W_in"]), 32)
    sh["mb_WdtTt"] = _tiled(_t(inputs["L1_attn_b_W_dt"]), 16)
    sh["mb_WoutTt"] = _tiled(_t(inputs["L1_attn_b_W_out"]), 8)
    sh["mb_conv_w"] = np.asarray(inputs["L1_attn_b_conv_w"])
    sh["mb_conv_b"] = np.asarray(inputs["L1_attn_b_conv_b"])
    sh["mb_A_log"] = np.asarray(inputs["L1_attn_b_A_log"])
    sh["mb_B"] = np.asarray(inputs["L1_attn_b_B_param"])
    sh["mb_C"] = np.asarray(inputs["L1_attn_b_C_param"])
    sh["mb_D"] = np.asarray(inputs["L1_attn_b_D_param"])
    sh["mb_b_dt"] = np.asarray(inputs["L1_attn_b_b_dt"])
    sh["mb_dt_bias"] = np.asarray(inputs["L1_attn_b_dt_bias"])

    rm = np.asarray(inputs["L1_ffn_rail_maker"])
    rup = np.asarray(inputs["L1_ffn_rail_up_proj"])
    rmBD = np.zeros((4, 8, 128, 128), np.float32)
    rupBD = np.zeros((4, 8, 128, 512), np.float32)
    for h in range(4):
        for rb in range(8):
            for rr in range(8):
                r = rb * 8 + rr
                rmBD[h, rb, rr * 16:(rr + 1) * 16, rr * 16:(rr + 1) * 16] = rm[h, r]
                rupBD[h, rb, rr * 16:(rr + 1) * 16, rr * 64:(rr + 1) * 64] = rup[h, r].T
    sh["rf_rmBD"] = rmBD
    sh["rf_rupBD"] = rupBD
    sh["rf_wcT"] = _t(inputs["L1_ffn_w_cproj"])

    sh["ma_Wq"] = np.asarray(inputs["L1_attn_a_Wq"])
    sh["ma_bq"] = np.ascontiguousarray(np.asarray(inputs["L1_attn_a_bq"]).reshape(1, -1))
    sh["ma_Wdt"] = _tiled(np.asarray(inputs["L1_attn_a_Wdown"]), 2)
    sh["ma_bdown"] = np.asarray(inputs["L1_attn_a_bdown"])
    sh["ma_Wup"] = np.asarray(inputs["L1_attn_a_Wup"])
    sh["ma_bup"] = np.ascontiguousarray(np.asarray(inputs["L1_attn_a_bup"]).reshape(1, -1))
    sh["ma_Wot"] = _tiled(np.asarray(inputs["L1_attn_a_Wout"]), 8)
    sh["ma_bout"] = np.asarray(inputs["L1_attn_a_bout"])
    inv = (1.0 / 10000.0 ** (np.arange(0, 64, 2, dtype=np.float32) / np.float32(64.0))).astype(np.float32)
    fr = (np.arange(T, dtype=np.float32)[:, None] * inv[None, :]).astype(np.float32)
    sh["ma_cos"] = np.cos(fr).astype(np.float32)
    sh["ma_sin"] = np.sin(fr).astype(np.float32)

    qw = np.asarray(inputs["L0_attn_b_q_w"])
    qb = np.asarray(inputs["L0_attn_b_q_b"])
    sh["ab_qw_sl"] = [np.ascontiguousarray(
        qw[:, c * 128:(c + 1) * 128].reshape(8, 128, 128).transpose(1, 0, 2).reshape(128, 1024))
        for c in range(NCORES)]
    sh["ab_qb_sl"] = [np.ascontiguousarray(qb[c * 128:(c + 1) * 128].reshape(1, 128))
                      for c in range(NCORES)]
    sh["ab_k_w"] = np.asarray(inputs["L0_attn_b_k_w"])
    sh["ab_v_w"] = np.asarray(inputs["L0_attn_b_v_w"])
    sh["ab_owt"] = _tiled(np.asarray(inputs["L0_attn_b_out_w"]), 8)
    sh["ab_k_b"] = np.ascontiguousarray(np.asarray(inputs["L0_attn_b_k_b"]).reshape(1, -1))
    sh["ab_v_b"] = np.ascontiguousarray(np.asarray(inputs["L0_attn_b_v_b"]).reshape(1, -1))
    sh["ab_out_b"] = np.asarray(inputs["L0_attn_b_out_b"])
    sh["ab_bias_k"] = np.ascontiguousarray(np.asarray(inputs["L0_attn_b_bias_k"]).reshape(1, -1))
    sh["ab_bias_v"] = np.ascontiguousarray(np.asarray(inputs["L0_attn_b_bias_v"]).reshape(1, -1))
    sh["ab_grep_w"] = np.asarray(inputs["L0_attn_b_grep_w"])
    sh["ab_grep_b"] = np.ascontiguousarray(np.asarray(inputs["L0_attn_b_grep_b"]).reshape(1, -1))
    sh["ab_grep_a"] = np.ascontiguousarray(np.asarray(inputs["L0_attn_b_grep_a"]).reshape(1, -1))
    rel_pos = np.arange(17)[None, :] - np.arange(16)[:, None]
    nb = 16
    buckets = (rel_pos > 0).astype(np.int32) * nb
    rp = np.abs(rel_pos)
    max_exact = 8
    is_small = rp < max_exact
    large = max_exact + (np.log(rp.astype(np.float32) / max_exact + 1e-6)
                         / np.log(128.0 / 8) * (nb - max_exact)).astype(np.int32)
    large = np.minimum(large, nb - 1)
    buckets = buckets + np.where(is_small, rp, large)
    rel_emb = np.asarray(inputs["L0_attn_b_rel_emb"])
    pb_all = []
    for c in range(NCORES):
        m = np.zeros((2, 272), np.float32)
        for nl in range(2):
            n = 2 * c + nl
            for tq in range(16):
                for s in range(17):
                    m[nl, tq * 17 + s] = rel_emb[buckets[n, s], tq]
        pb_all.append(m)
    sh["ab_pb"] = pb_all

    x = np.asarray(inputs["L0_attn_a_x"])
    xs = []
    for c in range(NCORES):
        xc = x[BPC * c:BPC * (c + 1)].reshape(NTOK, D)
        xs.append(np.ascontiguousarray(xc.T))
    return sh, xs


def run_launch(build_fn, per_core_maps):
    nc = build_fn()
    res = run_bass_kernel_spmd(nc, per_core_maps, core_ids=list(range(NCORES)))
    return res


FUSED = True


def build_fused():
    nc = bass.Bass("TRN2", target_bir_lowering=False)
    x0_d = nc.dram_tensor("x", [D, NTOK], F32, kind="ExternalInput")
    xo_d = nc.dram_tensor("xo", [D, NTOK], F32, kind="ExternalOutput")
    stages_x = [x0_d] + [nc.dram_tensor(f"xstage{i}", [D, NTOK], F32)
                         for i in range(1, 8)] + [xo_d]
    with contextlib.ExitStack() as stack:
        al = Alloc(nc, stack)
        p = Prog(nc)
        sems = {k: al.sem(f"ch_{k}") for k in ("c", "ds", "dg")}
        kk = Konst(al, p)
        ones_b = al.sb((128, 1), BF16, "onesb")
        p.add("vector", lambda e: e.memset(ones_b[:], 1.0))
        FC = FuseCtx(nc, al, p, kk, ones_b)
        seq = [build_attn_a, build_ffn0, build_attn_b, build_ffn0,
               build_attn_a1, build_rails, build_mamba, build_rails]
        for i, fn in enumerate(seq):
            FC.x_d = stages_x[i]
            FC.xo_d = stages_x[i + 1]
            fn(FC)
        p.emit(sems, al)
    return nc


def fused_inputs(sh, core):
    d = {}
    d.update(attn_a_inputs(sh))
    d.update(ffn0_inputs(sh))
    d.update(attn_b_inputs(sh, core))
    d.update(attn_a1_inputs(sh))
    d.update(rails_inputs(sh))
    d.update(mamba_inputs(sh))
    return d


def kernel(**inputs):
    sh, xs = host_prep(inputs)
    if FUSED:
        nc = build_fused()
        maps = [dict(fused_inputs(sh, c), x=xs[c]) for c in range(NCORES)]
        res = run_bass_kernel_spmd(nc, maps, core_ids=list(range(NCORES)))
        xs = [np.asarray(res.results[c]["xo"]) for c in range(NCORES)]
    else:
        stages = [
            (build_attn_a, lambda c: attn_a_inputs(sh)),
            (build_ffn0, lambda c: ffn0_inputs(sh)),
            (build_attn_b, lambda c: attn_b_inputs(sh, c)),
            (build_ffn0, lambda c: ffn0_inputs(sh)),
            (build_attn_a1, lambda c: attn_a1_inputs(sh)),
            (build_rails, lambda c: rails_inputs(sh)),
            (build_mamba, lambda c: mamba_inputs(sh)),
            (build_rails, lambda c: rails_inputs(sh)),
        ]
        built = {}
        for build_fn, mapfn in stages:
            if build_fn not in built:
                built[build_fn] = build_fn()
            nc = built[build_fn]
            maps = [dict(mapfn(c), x=xs[c]) for c in range(NCORES)]
            res = run_bass_kernel_spmd(nc, maps, core_ids=list(range(NCORES)))
            xs = [np.asarray(res.results[c]["xo"]) for c in range(NCORES)]
    out = np.stack([x.T.reshape(BPC, T, D) for x in xs]).reshape(B, T, D)
    return np.ascontiguousarray(out)


# version 08
# speedup vs baseline: 1.1734x; 1.0322x over previous
import contextlib

import numpy as np
import concourse.bass as bass
import concourse.mybir as mybir
from concourse.bass_utils import run_bass_kernel_spmd

F32 = mybir.dt.float32
BF16 = mybir.dt.bfloat16
AF = mybir.ActivationFunctionType
ALU = mybir.AluOpType
AX = mybir.AxisListType

NCORES = 8
B, T, D = 16, 2048, 1024
BPC = B // NCORES
NTOK = BPC * T
DC = D // 128


LAST_TOTAL = None
LAST_NSTEPS = None
NOCLEAR = ("dg",)
OWN_SKIP = ("tensor",)


class Prog:

    def __init__(self, nc):
        self.nc = nc
        self.steps = []
        self.n_cc = 0

    def mark(self):
        return len(self.steps)

    def add(self, eng, fn, inc=1, key=None, after=None):
        if key is None:
            key = "tensor" if eng == "tensor" else "c"
        self.steps.append((eng, fn, inc, key, after))

    def dma(self, out, in_, q="sync", after=None, **kw):
        key = "dg" if q == "gpsimd" else "ds"
        self.add(q, lambda e, o=out, i=in_, kw=kw: e.dma_start(o, i, **kw), 16, key,
                 after=after)

    def cast_dma(self, out, in_, **kw):
        self.add("gpsimd", lambda e, o=out, i=in_, kw=kw: e.dma_start(o, i, **kw), 16, "dg")

    def collective(self, kind, op, rg, ins, outs):
        key = f"cc{self.n_cc}"
        self.n_cc += 1
        self.add("gpsimd", lambda e, kind=kind, op=op, rg=rg, ins=ins, outs=outs:
                 e.collective_compute(kind, op, replica_groups=rg, ins=ins, outs=outs),
                 1, key)

    def emit(self, sems, al=None):
        global LAST_TOTAL, LAST_NSTEPS
        engines_used = [en for en in ["sync", "tensor", "vector", "scalar", "gpsimd"]
                        if any(s[0] == en for s in self.steps)]
        comp_keys = [k for k in ("tensor", "c")
                     if any(s[3] == k for s in self.steps)]
        keys = comp_keys + ["ds", "dg"] + [f"cc{i}" for i in range(self.n_cc)]
        sems = dict(sems)
        for k in comp_keys:
            if k not in sems:
                sems[k] = al.sem(f"eng_{k}")
        for i in range(self.n_cc):
            sems[f"cc{i}"] = al.sem(f"ccsem{i}")
        pre = []
        cum = {k: 0 for k in keys}
        for (_, _, inc, key, _aft) in self.steps:
            pre.append(dict(cum))
            cum[key] += inc
        self.total = dict(cum)
        LAST_TOTAL = dict(cum)
        LAST_NSTEPS = len(self.steps)
        done = al.sem("prog_done") if al is not None else None
        n_other = len([e for e in engines_used if e != "gpsimd"])
        with self.nc.Block() as block:
            for en in ["sync", "tensor", "vector", "scalar", "gpsimd"]:
                sub = [(i, self.steps[i]) for i in range(len(self.steps))
                       if self.steps[i][0] == en]
                if not sub and en != "gpsimd":
                    continue

                def mk(sub, en):
                    def body(eng):
                        last = {k: 0 for k in keys}
                        for i, (_, fn, inc, key, aft) in sub:
                            for k in keys:
                                if k == "tensor" and en in OWN_SKIP:
                                    continue
                                v = pre[i][k]
                                if aft is not None and k in comp_keys:
                                    v = pre[aft][k]
                                if v > last[k]:
                                    eng.wait_ge(sems[k], v)
                                    last[k] = v
                            fn(eng).then_inc(sems[key], inc)
                        if done is None:
                            return
                        if en != "gpsimd":
                            eng.nop().then_inc(done, 1)
                        else:
                            for k in keys:
                                if cum[k] > last[k] and k != "gpsimd":
                                    eng.wait_ge(sems[k], cum[k])
                            if n_other:
                                eng.wait_ge(done, n_other)
                            for k in keys:
                                if k in NOCLEAR:
                                    continue
                                eng.sem_clear(sems[k])
                            eng.sem_clear(done)
                    return body

                getattr(block, en)(mk(sub, en))


class Alloc:
    def __init__(self, nc, stack, ctr=None):
        self.nc = nc
        self.stack = stack
        self.ctr = ctr if ctr is not None else [0]

    @property
    def n(self):
        return self.ctr[0]

    def _id(self):
        self.ctr[0] += 1
        return self.ctr[0]

    def sb(self, shape, dt=F32, name=None):
        i = self._id()
        name = f"{name}_{i}" if name else f"sb{i}"
        return self.stack.enter_context(self.nc.sbuf_tensor(name, list(shape), dt))

    def sbl(self, k, shape, dt=F32, name=None):
        return [self.sb(shape, dt, name and f"{name}{i}") for i in range(k)]

    def ps(self, shape, name=None, dt=F32):
        i = self._id()
        name = f"{name}_{i}" if name else f"ps{i}"
        return self.stack.enter_context(self.nc.psum_tensor(name, list(shape), dt))

    def sem(self, name=None):
        i = self._id()
        return self.stack.enter_context(self.nc.semaphore(f"{name or 'sem'}_{i}"))


class FuseCtx:

    def __init__(self, nc, al, p, kk, ones_b):
        self.nc = nc
        self.al = al
        self.p = p
        self.kk = kk
        self.ones_b = ones_b
        self.x_d = None
        self.xo_d = None
        self._drams = {}

    def dram(self, name, shape, kind="ExternalInput"):
        if name not in self._drams:
            self._drams[name] = self.nc.dram_tensor(name, list(shape), F32, kind=kind)
        return self._drams[name]

    def prep_wtiled(self, name, noc, free):
        key = name + "_bf"
        if key not in self._drams:
            src = self.dram(name, [noc, 128, free])
            dst = self.nc.dram_tensor(key, [noc, 128, free], BF16)
            self._drams[key] = dst
            for oc in range(noc):
                self.p.cast_dma(dst.ap()[oc], src.ap()[oc])
        return self._drams[key]

    def precast(self, name, shape):
        key = name + "_bf"
        if key not in self._drams:
            src = self.dram(name, shape)
            dst = self.nc.dram_tensor(key, list(shape), BF16)
            self._drams[key] = dst
            flat_r = 1
            for s in shape[:-1]:
                flat_r *= s
            last = shape[-1]
            for r0 in range(0, flat_r, 128):
                rr = min(128, flat_r - r0)
                sv = bass.AP(src, r0 * last, [[last, rr], [1, last]])
                dv = bass.AP(dst, r0 * last, [[last, rr], [1, last]])
                self.p.cast_dma(dv, sv)
        return self._drams[key]

    @contextlib.contextmanager
    def scope(self):
        with contextlib.ExitStack() as stack:
            yield Alloc(self.nc, stack, self.al.ctr)


def _standalone_ctx(stack):
    nc = bass.Bass("TRN2", target_bir_lowering=False)
    x_d = nc.dram_tensor("x", [D, NTOK], F32, kind="ExternalInput")
    xo_d = nc.dram_tensor("xo", [D, NTOK], F32, kind="ExternalOutput")
    al = Alloc(nc, stack)
    p = Prog(nc)
    sems = {k: al.sem(f"ch_{k}") for k in ("c", "ds", "dg")}
    kk = Konst(al, p)
    ones_b = al.sb((128, 1), BF16, "onesb")
    p.add("vector", lambda e: e.memset(ones_b[:], 1.0))
    FC = FuseCtx(nc, al, p, kk, ones_b)
    FC.x_d = x_d
    FC.xo_d = xo_d

    def finish():
        p.emit(sems, al)
        return nc

    return FC, finish


class Konst:

    def __init__(self, al, p, n=24):
        self.tile = al.sb((128, n), F32, "konst")
        self.p = p
        self.map = {}
        self.n = 0

    def __call__(self, v):
        v = float(v)
        if v not in self.map:
            i = self.n
            self.n += 1
            self.map[v] = i
            self.p.add("vector", lambda e, i=i, v=v: e.memset(self.tile[:, i:i + 1], v))
        i = self.map[v]
        return self.tile[:, i:i + 1]


def load_scale_cols(p, dst, src_d, n):
    for c in range(n):
        p.dma(dst[:, c:c + 1], src_d.ap()[c * 128:(c + 1) * 128].unsqueeze(1))


def wcol_load(p, wbuf, w_d, oc, cchunks, rows=None):
    src = w_d.ap()[:, oc * 128:(oc + 1) * 128].rearrange("(c p) o -> p c o", p=128)
    p.cast_dma(wbuf[:].rearrange("p (c o) -> p c o", c=cchunks), src)




class LNBufs:

    def __init__(self, al, nc, nt, tag="", with_x=True):
        self.nt = nt
        self.xt = al.sbl(DC, (128, nt), F32, f"lnx{tag}") if with_x else None
        self.xb = al.sb((128, nt), BF16, f"lnxb{tag}")
        self.sqb = al.sb((128, nt), BF16, f"lnsqb{tag}")
        self.mean = al.sb((1, nt), F32, f"lnmean{tag}")
        self.msq = al.sb((1, nt), F32, f"lnmsq{tag}")
        self.var = al.sb((1, nt), F32, f"lnvar{tag}")
        self.rstd = al.sb((1, nt), F32, f"lnrstd{tag}")
        self.mbb = al.sb((128, nt), F32, f"lnmbb{tag}")
        self.rbb = al.sb((128, nt), F32, f"lnrbb{tag}")
        self.sq = al.sb((128, nt), F32, f"lnsq{tag}")
        self.psA = al.ps((1, nt), f"lnpsA{tag}")
        self.psB = al.ps((1, nt), f"lnpsB{tag}")
        self.rows_d = nc.dram_tensor(f"lnrows{tag}_{al._id()}", [2, nt], F32)


def emit_rowstats(p, lb, ones_b, tiles, nch, ndim, kk, nt=None, eps=1e-5):
    nt = nt or lb.nt
    for c in range(nch):
        p.add("vector", lambda e, c=c: e.tensor_copy(lb.xb[:, 0:nt], tiles[c][:, 0:nt]))
        p.add("tensor", lambda e, c=c: e.matmul(
            lb.psA[0:1, 0:nt], ones_b[:], lb.xb[:, 0:nt],
            start=(c == 0), stop=(c == nch - 1)))
    for c in range(nch):
        p.add("scalar", lambda e, c=c: e.activation(lb.sqb[:, 0:nt], tiles[c][:, 0:nt], AF.Square))
        p.add("tensor", lambda e, c=c: e.matmul(
            lb.psB[0:1, 0:nt], ones_b[:], lb.sqb[:, 0:nt],
            start=(c == 0), stop=(c == nch - 1)))
    p.add("vector", lambda e: e.tensor_scalar_mul(lb.mean[:, 0:nt], lb.psA[0:1, 0:nt], 1.0 / ndim))
    p.add("vector", lambda e: e.tensor_scalar_mul(lb.msq[:, 0:nt], lb.psB[0:1, 0:nt], 1.0 / ndim))
    p.add("vector", lambda e: e.tensor_tensor(lb.var[:, 0:nt], lb.mean[:, 0:nt], lb.mean[:, 0:nt], ALU.mult))
    p.add("vector", lambda e: e.tensor_tensor(lb.var[:, 0:nt], lb.msq[:, 0:nt], lb.var[:, 0:nt], ALU.subtract))
    p.add("vector", lambda e, eps=eps: e.tensor_scalar_add(lb.var[:, 0:nt], lb.var[:, 0:nt], float(eps)))
    p.add("scalar", lambda e: e.activation(lb.rstd[:, 0:nt], lb.var[:, 0:nt], AF.Ln))
    p.add("vector", lambda e: e.tensor_scalar_mul(lb.rstd[:, 0:nt], lb.rstd[:, 0:nt], -0.5))
    p.add("scalar", lambda e: e.activation(lb.rstd[:, 0:nt], lb.rstd[:, 0:nt], AF.Exp))
    p.dma(lb.rows_d.ap()[0:1, 0:nt], lb.mean[:, 0:nt])
    p.dma(lb.rows_d.ap()[1:2, 0:nt], lb.rstd[:, 0:nt])
    p.dma(lb.mbb[:, 0:nt], lb.rows_d.ap()[0:1, 0:nt].broadcast_to([128, nt]))
    p.dma(lb.rbb[:, 0:nt], lb.rows_d.ap()[1:2, 0:nt].broadcast_to([128, nt]))


def emit_ln(p, lb, ones_b, xn_tiles, xn_col0, kk, x_d=None, tok0=0,
            xt=None, nt=None):
    nt = nt or lb.nt
    if xt is None:
        xt = lb.xt
        for c in range(DC):
            p.dma(xt[c][:, 0:nt], x_d.ap()[c * 128:(c + 1) * 128, tok0:tok0 + nt])
    emit_rowstats(p, lb, ones_b, xt, DC, D, kk, nt=nt)
    for c in range(DC):
        p.add("vector", lambda e, c=c: e.tensor_tensor(lb.sq[:, 0:nt], xt[c][:, 0:nt], lb.mbb[:, 0:nt], ALU.subtract))
        p.add("vector", lambda e, c=c: e.tensor_tensor(
            xn_tiles[c][:, xn_col0:xn_col0 + nt], lb.sq[:, 0:nt], lb.rbb[:, 0:nt], ALU.mult))
    return xt



DBG = False


def _tap(p, nc, name, src_ap, shape, q="sync"):
    if not DBG:
        return
    d = nc.dram_tensor(f"dbg_{name}", list(shape), F32, kind="ExternalOutput")
    if src_ap.dtype == F32:
        p.dma(d.ap(), src_ap, q=q)
    else:
        p.cast_dma(d.ap(), src_ap)


def build_ffn0(FC=None):
    standalone = FC is None
    with contextlib.ExitStack() as outer:
        if standalone:
            FC, _finish = _standalone_ctx(outer)
        nc, p, kk, ones_b = FC.nc, FC.p, FC.kk, FC.ones_b
        x_d, xo_d = FC.x_d, FC.xo_d
        w1_b = FC.prep_wtiled("w1Tt", 32, 1024)
        w2_b = FC.prep_wtiled("w2Tt", 32, 1024)
        h1_b = FC.prep_wtiled("h1Tt", 32, 4096)
        h2_b = FC.prep_wtiled("h2Tt", 32, 4096)
        w3_b = FC.prep_wtiled("w3Tt", 8, 4096)
        s1_d = FC.dram("w1s", [4096])
        s2_d = FC.dram("w2s", [4096])
        sh1_d = FC.dram("h1s", [4096])
        sh2_d = FC.dram("h2s", [4096])
        s3_d = FC.dram("w3s", [1024])

        NT = 512
        NB = NTOK // NT
        scope_cm = FC.scope()
        al = scope_cm.__enter__()
        lb = LNBufs(al, nc, NT)
        xn = al.sbl(DC, (128, NT), BF16, "xn")
        h_a = al.sbl(32, (128, NT), BF16, "ha")
        h_b = al.sbl(32, (128, NT), BF16, "hb")
        silu_t = al.sb((128, NT), F32, "silu")
        yo = al.sb((128, NT), F32, "yo")
        wbufs = al.sbl(2, (128, 1024), BF16, "wbuf")
        wbufs2 = al.sbl(2, (128, 1024), BF16, "wbuf2")
        whbufs = al.sbl(2, (128, 4096), BF16, "whbuf")
        s1 = al.sb((128, 32), F32, "s1")
        s2 = al.sb((128, 32), F32, "s2")
        sh1 = al.sb((128, 32), F32, "sh1")
        sh2 = al.sb((128, 32), F32, "sh2")
        s3 = al.sb((128, 8), F32, "s3")
        psr = [al.ps((128, NT), f"psr{j}") for j in range(4)]

        load_scale_cols(p, s1, s1_d, 32)
        load_scale_cols(p, s2, s2_d, 32)
        load_scale_cols(p, sh1, sh1_d, 32)
        load_scale_cols(p, sh2, sh2_d, 32)
        load_scale_cols(p, s3, s3_d, 8)

        for tb in range(NB):
            col0 = tb * NT
            xt = emit_ln(p, lb, ones_b, xn, 0, kk, x_d=x_d, tok0=col0)
            if tb == 0:
                _tap(p, nc, "mean", lb.mean[:], (1, NT))
                _tap(p, nc, "rstd", lb.rstd[:], (1, NT))
                _tap(p, nc, "xn0", xn[0][:], (128, NT))
                _tap(p, nc, "s1col", s1[:], (128, 32))
            mks = []
            for oc in range(32):
                aft = mks[oc - 2] if oc >= 2 else None
                wb, wb2 = wbufs[oc % 2], wbufs2[oc % 2]
                pa, pb = psr[(2 * oc) % 4], psr[(2 * oc + 1) % 4]
                p.dma(wb[:], w1_b.ap()[oc], after=aft)
                for c in range(DC):
                    p.add("tensor", lambda e, c=c, wb=wb, pa=pa: e.matmul(
                        pa[:], wb[:, c * 128:(c + 1) * 128], xn[c][:],
                        start=(c == 0), stop=(c == DC - 1)), after=aft)
                p.add("scalar", lambda e, oc=oc, pa=pa: e.activation(
                    silu_t[:], pa[:], AF.Silu, scale=s1[:, oc:oc + 1]))
                p.dma(wb2[:], w2_b.ap()[oc], after=aft)
                for c in range(DC):
                    p.add("tensor", lambda e, c=c, wb2=wb2, pb=pb: e.matmul(
                        pb[:], wb2[:, c * 128:(c + 1) * 128], xn[c][:],
                        start=(c == 0), stop=(c == DC - 1)), after=aft)
                p.add("vector", lambda e, oc=oc, pb=pb: e.scalar_tensor_tensor(
                    h_a[oc][:], pb[:], s2[:, oc:oc + 1], silu_t[:], ALU.mult, ALU.mult))
                mks.append(p.mark())
            mks = []
            for oc in range(32):
                aft = mks[oc - 2] if oc >= 2 else None
                wh, pa = whbufs[oc % 2], psr[oc % 4]
                p.dma(wh[:], h1_b.ap()[oc], after=aft)
                for c in range(32):
                    p.add("tensor", lambda e, c=c, wh=wh, pa=pa: e.matmul(
                        pa[:], wh[:, c * 128:(c + 1) * 128], h_a[c][:],
                        start=(c == 0), stop=(c == 31)), after=aft)
                p.add("scalar", lambda e, oc=oc, pa=pa: e.activation(
                    h_b[oc][:], pa[:], AF.Gelu, scale=sh1[:, oc:oc + 1]))
                mks.append(p.mark())
            mks = []
            for oc in range(32):
                aft = mks[oc - 2] if oc >= 2 else None
                wh, pa = whbufs[oc % 2], psr[oc % 4]
                p.dma(wh[:], h2_b.ap()[oc], after=aft)
                for c in range(32):
                    p.add("tensor", lambda e, c=c, wh=wh, pa=pa: e.matmul(
                        pa[:], wh[:, c * 128:(c + 1) * 128], h_b[c][:],
                        start=(c == 0), stop=(c == 31)), after=aft)
                p.add("scalar", lambda e, oc=oc, pa=pa: e.activation(
                    h_a[oc][:], pa[:], AF.Gelu, scale=sh2[:, oc:oc + 1]))
                mks.append(p.mark())
            mks = []
            for oc in range(DC):
                aft = mks[oc - 2] if oc >= 2 else None
                wh, pa = whbufs[oc % 2], psr[oc % 4]
                p.dma(wh[:], w3_b.ap()[oc], after=aft)
                for c in range(32):
                    p.add("tensor", lambda e, c=c, wh=wh, pa=pa: e.matmul(
                        pa[:], wh[:, c * 128:(c + 1) * 128], h_a[c][:],
                        start=(c == 0), stop=(c == 31)), after=aft)
                p.add("vector", lambda e, oc=oc, pa=pa: e.scalar_tensor_tensor(
                    yo[:], pa[:], s3[:, oc:oc + 1], xt[oc][:], ALU.mult, ALU.add))
                p.dma(xo_d.ap()[oc * 128:(oc + 1) * 128, col0:col0 + NT], yo[:])
                mks.append(p.mark())

        scope_cm.__exit__(None, None, None)
        if standalone:
            return _finish()


def ffn0_inputs(shared):
    return {k: shared[k] for k in ("w1Tt", "w2Tt", "h1Tt", "h2Tt", "w3Tt",
                                   "w1s", "w2s", "h1s", "h2s", "w3s")}




def build_attn_a(FC=None):
    standalone = FC is None
    with contextlib.ExitStack() as outer:
        if standalone:
            FC, _finish = _standalone_ctx(outer)
        nc, p, kk, ones_b = FC.nc, FC.p, FC.kk, FC.ones_b
        x_d, xo_d = FC.x_d, FC.xo_d
        wq_b = FC.prep_wtiled("aa_Wqt", 8, 1024)
        wk_b = FC.prep_wtiled("aa_Wkt", 8, 1024)
        wv_b = FC.precast("aa_Wv", [D, D])
        wo_b = FC.prep_wtiled("aa_Wot", 8, 1024)
        bo_d = FC.dram("aa_bo", [D])
        th_d = FC.dram("aa_theta", [1, 8])
        ga_d = FC.dram("aa_gamma", [D])
        be_d = FC.dram("aa_beta", [D])
        m128_d = FC.dram("m128", [128, 128])
        decay_d = nc.dram_tensor("decay", [T, T], F32)

        NH, HD = 8, 128
        NTT = T // 128
        scope_cm = FC.scope()
        al = scope_cm.__enter__()
        lb = LNBufs(al, nc, 512, with_x=False)
        xnb = al.sbl(DC, (128, T), BF16, "xnb")
        xtb = al.sbl(DC, (128, 512), BF16, "xtb")
        q_fm = al.sbl(NH, (128, T), BF16, "qfm")
        k_fm = al.sbl(NH, (128, T), BF16, "kfm")
        v_tm = al.sbl(NTT, (128, D), BF16, "vtm")
        u16k = al.sb((128, 8192), BF16, "u16k")
        wbuf = al.sb((128, 1024), BF16, "wbuf")
        dec = al.sb((128, 1024), F32, "dec")
        srow = al.sb((128, T), F32, "srow")
        p_bf = al.sb((128, T), BF16, "pbf")
        a_f32 = al.sb((128, 512), F32, "af32")
        e_t = al.sb((128, 512), F32, "et")
        acc_t = al.sb((128, 512), F32, "acct")
        n_i = al.sb((128, 512), mybir.dt.int32, "ni")
        nf = al.sb((128, 512), F32, "nf")
        ngt = al.sb((128, 8), F32, "ngt")
        m128 = al.sb((128, 128), F32, "m128s")
        nm = al.sb((128, 1), F32, "nm")
        zrow = al.sb((128, 1), F32, "zrow")
        rz = al.sb((128, 1), F32, "rz")
        bo_c = al.sb((128, 8), F32, "boc")
        ga_c = al.sb((128, 8), F32, "gac")
        be_c = al.sb((128, 8), F32, "bec")
        xold = al.sb((128, 1024), F32, "xold")
        t1 = al.sb((128, 1024), F32, "t1")
        ps_big = al.ps((128, 2048), "psbig")
        ps_pv = al.ps((128, 512), "pspv")

        p.dma(m128[:], m128_d.ap())
        p.dma(ngt[:], th_d.ap().broadcast_to([128, 8]))
        p.add("vector", lambda e: e.tensor_scalar_mul(ngt[:], ngt[:], -1.0))
        load_scale_cols(p, bo_c, bo_d, 8)
        load_scale_cols(p, ga_c, ga_d, 8)
        load_scale_cols(p, be_c, be_d, 8)

        for tt in range(NTT):
            t_end = (tt + 1) * 128
            for sb in range((t_end + 511) // 512):
                w = min(512, t_end - sb * 512)
                p.add("gpsimd", lambda e, tt=tt, sb=sb, w=w: e.iota(
                    n_i[:, 0:w], [[-1, w]], base=tt * 128 - sb * 512,
                    channel_multiplier=1))
                p.add("vector", lambda e, w=w: e.tensor_copy(nf[:, 0:w], n_i[:, 0:w]))
                p.add("vector", lambda e, w=w: e.tensor_scalar_max(nf[:, 0:w], nf[:, 0:w], 0.0))
                for h in range(NH):
                    if h == 0:
                        p.add("scalar", lambda e, w=w, h=h: e.activation(
                            acc_t[:, 0:w], nf[:, 0:w], AF.Exp, scale=ngt[:, h:h + 1]))
                    else:
                        p.add("scalar", lambda e, w=w, h=h: e.activation(
                            e_t[:, 0:w], nf[:, 0:w], AF.Exp, scale=ngt[:, h:h + 1]))
                        p.add("vector", lambda e, w=w: e.tensor_add(
                            acc_t[:, 0:w], acc_t[:, 0:w], e_t[:, 0:w]))
                p.add("vector", lambda e, w=w: e.tensor_scalar_mul(
                    acc_t[:, 0:w], acc_t[:, 0:w], 1.0 / (NH * float(np.sqrt(HD)))))
                p.dma(decay_d.ap()[tt * 128:(tt + 1) * 128, sb * 512:sb * 512 + w],
                      acc_t[:, 0:w])

        for b in range(BPC):
            tok0 = b * T
            for tb in range(T // 512):
                for c in range(DC):
                    p.cast_dma(xtb[c][:], x_d.ap()[c * 128:(c + 1) * 128,
                                                   tok0 + tb * 512:tok0 + (tb + 1) * 512])
                emit_ln(p, lb, ones_b, xnb, tb * 512, kk, xt=xtb, nt=512)
            for w_b, dst in ((wq_b, q_fm), (wk_b, k_fm)):
                for oc in range(NH):
                    p.dma(wbuf[:], w_b.ap()[oc])
                    for tb in range(4):
                        for c in range(DC):
                            p.add("tensor", lambda e, oc=oc, tb=tb, c=c, dst=dst: e.matmul(
                                ps_big[:, 0:512], wbuf[:, c * 128:(c + 1) * 128],
                                xnb[c][:, tb * 512:(tb + 1) * 512],
                                start=(c == 0), stop=(c == DC - 1)))
                        p.add("vector", lambda e, oc=oc, tb=tb, dst=dst: e.tensor_copy(
                            dst[oc][:, tb * 512:(tb + 1) * 512], ps_big[:, 0:512]))
            for c in range(DC):
                p.dma(u16k[:, c * 1024:(c + 1) * 1024],
                      wv_b.ap()[c * 128:(c + 1) * 128, :])
            for tt in range(NTT):
                for half in range(2):
                    for c in range(DC):
                        p.add("tensor", lambda e, tt=tt, c=c, half=half: e.matmul(
                            ps_big[:, half * 512:(half + 1) * 512],
                            xnb[c][:, tt * 128:(tt + 1) * 128],
                            u16k[:, c * 1024 + half * 512:c * 1024 + (half + 1) * 512],
                            start=(c == 0), stop=(c == DC - 1)))
                    p.add("vector", lambda e, tt=tt, half=half: e.tensor_copy(
                        v_tm[tt][:, half * 512:(half + 1) * 512],
                        ps_big[:, half * 512:(half + 1) * 512]))

            for h in range(NH):
                for g in range(4):
                    g_end = (g + 1) * 512
                    nsc = g_end // 128
                    for ttg in range(4):
                        tt = g * 4 + ttg
                        t_end = (tt + 1) * 128
                        for sb in range((t_end + 511) // 512):
                            w = min(512, t_end - sb * 512)
                            p.add("tensor", lambda e, h=h, tt=tt, sb=sb, w=w: e.matmul(
                                ps_big[:, sb * 512:sb * 512 + w],
                                q_fm[h][:, tt * 128:(tt + 1) * 128],
                                k_fm[h][:, sb * 512:sb * 512 + w],
                                start=True, stop=True))
                        for half in range((t_end + 1023) // 1024):
                            w = min(1024, t_end - half * 1024)
                            p.dma(dec[:, 0:w],
                                  decay_d.ap()[tt * 128:(tt + 1) * 128,
                                               half * 1024:half * 1024 + w])
                            p.add("vector", lambda e, half=half, w=w: e.tensor_tensor(
                                srow[:, half * 1024:half * 1024 + w],
                                ps_big[:, half * 1024:half * 1024 + w],
                                dec[:, 0:w], ALU.mult))
                        p.add("vector", lambda e, tt=tt, t_end=t_end: e.tensor_add(
                            srow[:, tt * 128:t_end], srow[:, tt * 128:t_end], m128[:]))
                        p.add("vector", lambda e, t_end=t_end: e.reduce_max(
                            nm[:], srow[:, 0:t_end], axis=AX.X, negate=True))
                        p.add("scalar", lambda e, t_end=t_end: e.activation(
                            srow[:, 0:t_end], srow[:, 0:t_end], AF.Exp,
                            bias=nm[:], accum_out=zrow[:]))
                        p.add("vector", lambda e: e.reciprocal(rz[:], zrow[:]))
                        p.add("vector", lambda e, t_end=t_end: e.tensor_scalar_mul(
                            p_bf[:, 0:t_end], srow[:, 0:t_end], rz[:]))
                        if t_end < g_end:
                            p.add("vector", lambda e, t_end=t_end, g_end=g_end: e.memset(
                                p_bf[:, t_end:g_end], 0.0))
                        for sc in range(nsc):
                            p.dma(u16k[:, sc * 512 + ttg * 128:sc * 512 + (ttg + 1) * 128],
                                  p_bf[:, sc * 128:(sc + 1) * 128], transpose=True)
                    for sc in range(nsc):
                        p.add("tensor", lambda e, h=h, sc=sc, nsc=nsc: e.matmul(
                            ps_pv[:], v_tm[sc][:, h * 128:(h + 1) * 128],
                            u16k[:, sc * 512:(sc + 1) * 512],
                            start=(sc == 0), stop=(sc == nsc - 1)))
                    p.add("vector", lambda e: e.tensor_copy(a_f32[:], ps_pv[:]))
                    emit_rowstats(p, lb, ones_b, [a_f32], 1, 128, kk, nt=512)
                    p.add("vector", lambda e: e.tensor_tensor(
                        lb.sq[:, 0:512], a_f32[:], lb.mbb[:, 0:512], ALU.subtract))
                    p.add("vector", lambda e: e.tensor_tensor(
                        lb.sq[:, 0:512], lb.sq[:, 0:512], lb.rbb[:, 0:512], ALU.mult))
                    p.add("vector", lambda e, h=h, g=g: e.tensor_scalar(
                        xnb[h][:, g * 512:(g + 1) * 512], lb.sq[:, 0:512],
                        ga_c[:, h:h + 1], be_c[:, h:h + 1], ALU.mult, ALU.add))

            for oc in range(DC):
                p.dma(wbuf[:], wo_b.ap()[oc])
                for tb in range(4):
                    for c in range(DC):
                        p.add("tensor", lambda e, oc=oc, tb=tb, c=c: e.matmul(
                            ps_big[:, 0:512], wbuf[:, c * 128:(c + 1) * 128],
                            xnb[c][:, tb * 512:(tb + 1) * 512],
                            start=(c == 0), stop=(c == DC - 1)))
                    p.add("vector", lambda e, oc=oc: e.tensor_scalar_add(
                        t1[:, 0:512], ps_big[:, 0:512], bo_c[:, oc:oc + 1]))
                    p.dma(xold[:, 0:512], x_d.ap()[oc * 128:(oc + 1) * 128,
                                                   tok0 + tb * 512:tok0 + (tb + 1) * 512])
                    p.add("vector", lambda e: e.tensor_add(t1[:, 0:512], t1[:, 0:512], xold[:, 0:512]))
                    p.dma(xo_d.ap()[oc * 128:(oc + 1) * 128,
                                    tok0 + tb * 512:tok0 + (tb + 1) * 512], t1[:, 0:512])
        scope_cm.__exit__(None, None, None)
        if standalone:
            return _finish()


def attn_a_inputs(sh):
    return {k: sh[k] for k in ("aa_Wqt", "aa_Wkt", "aa_Wv", "aa_Wot", "aa_bo",
                               "aa_theta", "aa_gamma", "aa_beta", "m128")}




def build_mamba(FC=None):
    standalone = FC is None
    with contextlib.ExitStack() as outer:
        if standalone:
            FC, _finish = _standalone_ctx(outer)
        nc, p, kk, ones_b = FC.nc, FC.p, FC.kk, FC.ones_b
        x_d, xo_d = FC.x_d, FC.xo_d
        win_b = FC.prep_wtiled("mb_WinTt", 32, 1024)
        wdt_b = FC.prep_wtiled("mb_WdtTt", 16, 2048)
        wout_b = FC.prep_wtiled("mb_WoutTt", 8, 2048)
        cw_d = FC.dram("conv_w", [2048, 4])
        cb_d = FC.dram("conv_b", [2048])
        alog_d = FC.dram("A_log", [2048, 16])
        bp_d = FC.dram("B_param", [2048, 16])
        cp_d = FC.dram("C_param", [2048, 16])
        dp_d = FC.dram("D_param", [2048])
        bdt_d = FC.dram("b_dt", [2048])
        dtb_d = FC.dram("dt_bias", [2048])

        NI = 16
        LB = 512
        NLB = T // LB
        scope_cm = FC.scope()
        al = scope_cm.__enter__()
        lb_ = LNBufs(al, nc, 512, with_x=False)
        xtb = al.sbl(DC, (128, 512), BF16, "xtb")
        xnb = al.sbl(DC, (128, T), BF16, "xnb")
        xbp = al.sbl(NI, (128, 3 + LB), F32, "xbp")
        u_f = al.sbl(NI, (128, LB), F32, "uf")
        u_b = al.sbl(NI, (128, LB), BF16, "ub")
        y_b = al.sbl(NI, (128, LB), BF16, "yb")
        carry = al.sb((128, NI * 16), F32, "carry")
        cw = al.sb((128, 64), F32, "cw")
        cb_c = al.sb((128, 16), F32, "cbc")
        A_sb = al.sb((128, 256), F32, "Asb")
        B_sb = al.sb((128, 256), F32, "Bsb")
        C_sb = al.sb((128, 256), F32, "Csb")
        D_c = al.sb((128, 16), F32, "Dc")
        bias_c = al.sb((128, 16), F32, "biasc")
        tmp16 = al.sb((128, 16), F32, "tmp16")
        wbuf = al.sb((128, 1024), BF16, "wbuf")
        wdtbuf = al.sb((128, 2048), BF16, "wdtbuf")
        sdt = al.sb((128, LB), F32, "sdt")
        tA = al.sb((128, LB), F32, "tA")
        tL = al.sb((128, LB), F32, "tL")
        dt_t = al.sb((128, LB), F32, "dtt")
        wdu = al.sb((128, LB), F32, "wdu")
        dA_t = al.sb((128, LB), F32, "dAt")
        dBu_t = al.sb((128, LB), F32, "dBut")
        s_t = al.sb((128, LB), F32, "st")
        y0 = al.sb((128, LB), F32, "y0")
        y1 = al.sb((128, LB), F32, "y1")
        g_t = al.sb((128, LB), F32, "gt")
        xold = al.sb((128, LB), F32, "xold")
        t1 = al.sb((128, LB), F32, "t1")
        ps1 = al.ps((128, 512), "ps1")
        ps2 = al.ps((128, 512), "ps2")

        load_scale_cols(p, cb_c, cb_d, 16)
        load_scale_cols(p, D_c, dp_d, 16)
        load_scale_cols(p, bias_c, bdt_d, 16)
        load_scale_cols(p, tmp16, dtb_d, 16)
        p.add("vector", lambda e: e.tensor_add(bias_c[:], bias_c[:], tmp16[:]))
        for c in range(NI):
            p.dma(cw[:, c * 4:(c + 1) * 4], cw_d.ap()[c * 128:(c + 1) * 128, :])
            p.dma(A_sb[:, c * 16:(c + 1) * 16], alog_d.ap()[c * 128:(c + 1) * 128, :])
            p.dma(B_sb[:, c * 16:(c + 1) * 16], bp_d.ap()[c * 128:(c + 1) * 128, :])
            p.dma(C_sb[:, c * 16:(c + 1) * 16], cp_d.ap()[c * 128:(c + 1) * 128, :])
        p.add("scalar", lambda e: e.activation(A_sb[:], A_sb[:], AF.Exp))
        p.add("vector", lambda e: e.tensor_scalar_mul(A_sb[:], A_sb[:], -1.0))

        for b in range(BPC):
            tok0 = b * T
            for tb in range(T // 512):
                for c in range(DC):
                    p.cast_dma(xtb[c][:], x_d.ap()[c * 128:(c + 1) * 128,
                                                   tok0 + tb * 512:tok0 + (tb + 1) * 512])
                emit_ln(p, lb_, ones_b, xnb, tb * 512, kk, xt=xtb, nt=512)

            for li in range(NLB):
                l0 = li * LB
                for oc in range(NI):
                    if li == 0:
                        p.add("vector", lambda e, oc=oc: e.memset(xbp[oc][:, 0:3], 0.0))
                    else:
                        p.add("vector", lambda e, oc=oc: e.tensor_copy(
                            xbp[oc][:, 0:3], xbp[oc][:, LB:LB + 3]))
                for oc in range(NI):
                    p.dma(wbuf[:], win_b.ap()[oc])
                    for c in range(DC):
                        p.add("tensor", lambda e, oc=oc, c=c, l0=l0: e.matmul(
                            ps1[:], wbuf[:, c * 128:(c + 1) * 128],
                            xnb[c][:, l0:l0 + LB], start=(c == 0), stop=(c == DC - 1)))
                    p.add("vector", lambda e, oc=oc: e.tensor_copy(xbp[oc][:, 3:3 + LB], ps1[:]))
                    p.add("vector", lambda e, oc=oc: e.tensor_scalar_mul(
                        y0[:], xbp[oc][:, 0:LB], cw[:, oc * 4:oc * 4 + 1]))
                    for kkk in range(1, 4):
                        p.add("vector", lambda e, oc=oc, kkk=kkk: e.scalar_tensor_tensor(
                            y0[:], xbp[oc][:, kkk:kkk + LB], cw[:, oc * 4 + kkk:oc * 4 + kkk + 1],
                            y0[:], ALU.mult, ALU.add))
                    p.add("scalar", lambda e, oc=oc: e.activation(
                        u_f[oc][:], y0[:], AF.Silu, bias=cb_c[:, oc:oc + 1]))
                    p.add("vector", lambda e, oc=oc: e.tensor_copy(u_b[oc][:], u_f[oc][:]))

                for oc in range(NI):
                    p.dma(wdtbuf[:], wdt_b.ap()[oc])
                    for c in range(NI):
                        p.add("tensor", lambda e, oc=oc, c=c: e.matmul(
                            ps1[:], wdtbuf[:, c * 128:(c + 1) * 128], u_b[c][:],
                            start=(c == 0), stop=(c == NI - 1)))
                    p.add("vector", lambda e, oc=oc: e.tensor_scalar_add(
                        sdt[:], ps1[:], bias_c[:, oc:oc + 1]))
                    k_m1 = kk(-1.0)
                    k_p1 = kk(1.0)
                    p.add("scalar", lambda e: e.activation(tA[:], sdt[:], AF.Abs))
                    p.add("scalar", lambda e, k_m1=k_m1: e.activation(tA[:], tA[:], AF.Exp, scale=k_m1))
                    p.add("scalar", lambda e, k_p1=k_p1: e.activation(tL[:], tA[:], AF.Ln, bias=k_p1))
                    p.add("vector", lambda e: e.tensor_scalar_max(dt_t[:], sdt[:], 0.0))
                    p.add("vector", lambda e: e.tensor_add(dt_t[:], dt_t[:], tL[:]))
                    p.add("vector", lambda e, oc=oc: e.tensor_tensor(
                        wdu[:], dt_t[:], u_f[oc][:], ALU.mult))
                    for n in range(16):
                        cn = oc * 16 + n
                        p.add("scalar", lambda e, cn=cn: e.activation(
                            dA_t[:], dt_t[:], AF.Exp, scale=A_sb[:, cn:cn + 1]))
                        p.add("vector", lambda e, cn=cn: e.tensor_scalar_mul(
                            dBu_t[:], wdu[:], B_sb[:, cn:cn + 1]))
                        if li == 0:
                            p.add("vector", lambda e: e.tensor_tensor_scan(
                                s_t[:], dA_t[:], dBu_t[:], 0.0, ALU.mult, ALU.add))
                        else:
                            p.add("vector", lambda e, cn=cn: e.tensor_tensor_scan(
                                s_t[:], dA_t[:], dBu_t[:], carry[:, cn:cn + 1],
                                ALU.mult, ALU.add))
                        p.add("vector", lambda e, cn=cn: e.tensor_copy(
                            carry[:, cn:cn + 1], s_t[:, LB - 1:LB]))
                        if n == 0:
                            p.add("vector", lambda e, cn=cn: e.tensor_scalar_mul(
                                y0[:], s_t[:], C_sb[:, cn:cn + 1]))
                        else:
                            src_y, dst_y = (y0, y1) if n % 2 == 1 else (y1, y0)
                            p.add("vector", lambda e, cn=cn, s=src_y, d=dst_y: e.scalar_tensor_tensor(
                                d[:], s_t[:], C_sb[:, cn:cn + 1], s[:], ALU.mult, ALU.add))
                    y_fin = y1 if 15 % 2 == 1 else y0
                    p.dma(wbuf[:], win_b.ap()[16 + oc])
                    for c in range(DC):
                        p.add("tensor", lambda e, oc=oc, c=c, l0=l0: e.matmul(
                            ps2[:], wbuf[:, c * 128:(c + 1) * 128],
                            xnb[c][:, l0:l0 + LB], start=(c == 0), stop=(c == DC - 1)))
                    p.add("scalar", lambda e: e.activation(g_t[:], ps2[:], AF.Silu))
                    p.add("vector", lambda e, oc=oc, y_fin=y_fin: e.scalar_tensor_tensor(
                        t1[:], u_f[oc][:], D_c[:, oc:oc + 1], y_fin[:], ALU.mult, ALU.add))
                    p.add("vector", lambda e, oc=oc: e.tensor_tensor(
                        y_b[oc][:], t1[:], g_t[:], ALU.mult))

                for oc in range(DC):
                    p.dma(wdtbuf[:], wout_b.ap()[oc])
                    for c in range(NI):
                        p.add("tensor", lambda e, oc=oc, c=c: e.matmul(
                            ps1[:], wdtbuf[:, c * 128:(c + 1) * 128], y_b[c][:],
                            start=(c == 0), stop=(c == NI - 1)))
                    p.dma(xold[:], x_d.ap()[oc * 128:(oc + 1) * 128, tok0 + l0:tok0 + l0 + LB])
                    p.add("vector", lambda e: e.tensor_tensor(t1[:], ps1[:], xold[:], ALU.add))
                    p.dma(xo_d.ap()[oc * 128:(oc + 1) * 128, tok0 + l0:tok0 + l0 + LB], t1[:])
        scope_cm.__exit__(None, None, None)
        if standalone:
            return _finish()


def mamba_inputs(sh):
    return {
        "mb_WinTt": sh["mb_WinTt"], "mb_WdtTt": sh["mb_WdtTt"], "mb_WoutTt": sh["mb_WoutTt"],
        "conv_w": sh["mb_conv_w"], "conv_b": sh["mb_conv_b"], "A_log": sh["mb_A_log"],
        "B_param": sh["mb_B"], "C_param": sh["mb_C"], "D_param": sh["mb_D"],
        "b_dt": sh["mb_b_dt"], "dt_bias": sh["mb_dt_bias"],
    }




def build_rails(FC=None):
    standalone = FC is None
    with contextlib.ExitStack() as outer:
        if standalone:
            FC, _finish = _standalone_ctx(outer)
        nc, p, kk, ones_b = FC.nc, FC.p, FC.kk, FC.ones_b
        x_d, xo_d = FC.x_d, FC.xo_d
        rm_b = FC.precast("rmBD", [4, 8, 128, 128])
        rup_b = FC.precast("rupBD", [4, 8, 128, 512])
        wc_b = FC.precast("wcT", [16384, 1024])

        NT = 512
        scope_cm = FC.scope()
        al = scope_cm.__enter__()
        lb = LNBufs(al, nc, NT)
        xn = al.sbl(DC, (128, NT), BF16, "xn")
        rm_sb = al.sb((128, 4096), BF16, "rmsb")
        rup_sb = al.sb((128, 16384), BF16, "rupsb")
        y_sb = al.sbl(32, (128, NT), BF16, "ysb")
        z_scrs = al.sbl(2, (128, NT), BF16, "zscr")
        wc_sbs = al.sbl(2, (128, 512), BF16, "wcsb")
        t1 = al.sb((128, NT), F32, "t1")
        pszs = [al.ps((128, 512), f"psz{j}") for j in range(2)]
        ps_o = al.ps((128, 2048), "pso")

        for h in range(4):
            for rb in range(8):
                i = h * 8 + rb
                p.dma(rm_sb[:, i * 128:(i + 1) * 128], rm_b.ap()[h, rb])
                p.dma(rup_sb[:, i * 512:(i + 1) * 512], rup_b.ap()[h, rb])

        for tb in range(NTOK // NT):
            col0 = tb * NT
            xt = emit_ln(p, lb, ones_b, xn, 0, kk, x_d=x_d, tok0=col0)
            ymk = []
            for i in range(32):
                rb = i % 8
                aft = ymk[i - 2] if i >= 2 else None
                pz = pszs[i % 2]
                p.add("tensor", lambda e, i=i, rb=rb, pz=pz: e.matmul(
                    pz[:], rm_sb[:, i * 128:(i + 1) * 128], xn[rb][:],
                    start=True, stop=True), after=aft)
                p.add("vector", lambda e, i=i, pz=pz: e.tensor_copy(y_sb[i][:], pz[:]))
                ymk.append(p.mark())
            for half in range(2):
                h0 = p.mark()
                am = {}
                p.dma(wc_sbs[0][:], wc_b.ap()[0:128, half * 512:(half + 1) * 512])
                for ch in range(129):
                    if ch < 128:
                        i, zt = ch // 4, ch % 4
                        pz, zs = pszs[ch % 2], z_scrs[ch % 2]
                        aft = am.get(ch - 2, h0)
                        p.add("tensor", lambda e, i=i, zt=zt, pz=pz: e.matmul(
                            pz[:], rup_sb[:, i * 512 + zt * 128:i * 512 + (zt + 1) * 128],
                            y_sb[i][:], start=True, stop=True), after=aft)
                        p.add("scalar", lambda e, pz=pz, zs=zs: e.activation(
                            zs[:], pz[:], AF.Gelu))
                        am[ch] = p.mark()
                    if ch >= 1:
                        cz = ch - 1
                        wcb, zs = wc_sbs[cz % 2], z_scrs[cz % 2]
                        for oi in range(4):
                            p.add("tensor", lambda e, cz=cz, oi=oi, wcb=wcb, zs=zs: e.matmul(
                                ps_o[:, oi * 512:(oi + 1) * 512],
                                wcb[:, oi * 128:(oi + 1) * 128], zs[:],
                                start=(cz == 0), stop=(cz == 127)), after=am[cz])
                    if ch < 127:
                        p.dma(wc_sbs[(ch + 1) % 2][:],
                              wc_b.ap()[(ch + 1) * 128:(ch + 2) * 128,
                                        half * 512:(half + 1) * 512])
                for oi in range(4):
                    oc = half * 4 + oi
                    p.dma(lb.sq[:, 0:NT], x_d.ap()[oc * 128:(oc + 1) * 128, col0:col0 + NT])
                    p.add("vector", lambda e, oi=oi: e.tensor_tensor(
                        t1[:], ps_o[:, oi * 512:(oi + 1) * 512], lb.sq[:, 0:NT], ALU.add))
                    p.dma(xo_d.ap()[oc * 128:(oc + 1) * 128, col0:col0 + NT], t1[:])
        scope_cm.__exit__(None, None, None)
        if standalone:
            return _finish()


def rails_inputs(sh):
    return {"rmBD": sh["rf_rmBD"], "rupBD": sh["rf_rupBD"], "wcT": sh["rf_wcT"]}




def build_attn_a1(FC=None):
    standalone = FC is None
    with contextlib.ExitStack() as outer:
        if standalone:
            FC, _finish = _standalone_ctx(outer)
        nc, p, kk, ones_b = FC.nc, FC.p, FC.kk, FC.ones_b
        x_d, xo_d = FC.x_d, FC.xo_d
        wq_b2 = FC.precast("ma_Wq", [D, D])
        bq_d = FC.dram("ma_bq", [1, D])
        wd_b = FC.prep_wtiled("ma_Wdt", 2, 1024)
        bd_d = FC.dram("ma_bdown", [256])
        wu_b2 = FC.precast("ma_Wup", [256, 2048])
        bu_d = FC.dram("ma_bup", [1, 2048])
        wo_b2 = FC.prep_wtiled("ma_Wot", 8, 1024)
        bo_d = FC.dram("ma_bout", [D])
        cos_d = FC.dram("ma_cos", [T, 32])
        sin_d = FC.dram("ma_sin", [T, 32])

        scope_cm = FC.scope()
        al = scope_cm.__enter__()
        lb = LNBufs(al, nc, 512)
        xn = al.sbl(DC, (128, NTOK), BF16, "xn")
        lat = al.sbl(2, (128, NTOK), BF16, "lat")
        wq_sb = al.sbl(DC, (128, 1024), BF16, "wqsb")
        wu_sb = al.sbl(2, (128, 2048), BF16, "wusb")
        wbuf = al.sb((128, 1024), BF16, "wbuf")
        bq_b = al.sb((128, 1024), F32, "bqb")
        bu_b = al.sb((128, 2048), F32, "bub")
        bd_c = al.sb((128, 2), F32, "bdc")
        bo_c = al.sb((128, 8), F32, "boc")
        cos_t = al.sb((128, 32), F32, "cost")
        sin_t = al.sb((128, 32), F32, "sint")
        q_t = al.sb((128, 1024), F32, "qt")
        kv_t = al.sb((128, 2048), F32, "kvt")
        q_ro = al.sb((128, 1024), F32, "qro")
        k_ro = al.sb((128, 1024), F32, "kro")
        v_c = al.sb((128, 1024), F32, "vc")
        tr1 = al.sb((128, 512), F32, "tr1")
        tr2 = al.sb((128, 512), F32, "tr2")
        S1 = al.sb((128, 256), F32, "S1")
        S2 = al.sb((128, 256), F32, "S2")
        nmx = al.sb((128, 16), F32, "nmx")
        zs = al.sb((128, 16), F32, "zs")
        rzz = al.sb((128, 16), F32, "rzz")
        o_acc = al.sb((128, 1024), F32, "oacc")
        tmps = al.sb((128, 1024), F32, "tmps")
        o_bf = al.sb((128, 1024), BF16, "obf")
        a_tr = al.sb((128, 1024), BF16, "atr")
        xold = al.sb((128, 512), F32, "xold")
        t1 = al.sb((128, 512), F32, "t1")
        ps1 = al.ps((128, 512), "ps1")

        p.dma(bq_b[:], bq_d.ap().broadcast_to([128, 1024]))
        p.dma(bu_b[:], bu_d.ap().broadcast_to([128, 2048]))
        load_scale_cols(p, bd_c, bd_d, 2)
        load_scale_cols(p, bo_c, bo_d, 8)
        for c in range(DC):
            p.dma(wq_sb[c][:], wq_b2.ap()[c * 128:(c + 1) * 128, :])
        for c in range(2):
            p.dma(wu_sb[c][:], wu_b2.ap()[c * 128:(c + 1) * 128, :])

        for tb in range(NTOK // 512):
            emit_ln(p, lb, ones_b, xn, tb * 512, kk, x_d=x_d, tok0=tb * 512)
        for oc in range(2):
            p.dma(wbuf[:], wd_b.ap()[oc])
            for tb in range(NTOK // 512):
                for c in range(DC):
                    p.add("tensor", lambda e, oc=oc, tb=tb, c=c: e.matmul(
                        ps1[:], wbuf[:, c * 128:(c + 1) * 128],
                        xn[c][:, tb * 512:(tb + 1) * 512],
                        start=(c == 0), stop=(c == DC - 1)))
                p.add("vector", lambda e, oc=oc, tb=tb: e.tensor_scalar_add(
                    lat[oc][:, tb * 512:(tb + 1) * 512], ps1[:], bd_c[:, oc:oc + 1]))

        H = 16
        for tt in range(NTOK // 128):
            pos0 = (tt % 16) * 128
            for half in range(2):
                for c in range(DC):
                    p.add("tensor", lambda e, tt=tt, c=c, half=half: e.matmul(
                        ps1[:], xn[c][:, tt * 128:(tt + 1) * 128],
                        wq_sb[c][:, half * 512:(half + 1) * 512],
                        start=(c == 0), stop=(c == DC - 1)))
                p.add("vector", lambda e, half=half: e.tensor_tensor(
                    q_t[:, half * 512:(half + 1) * 512], ps1[:],
                    bq_b[:, half * 512:(half + 1) * 512], ALU.add))
            for half in range(4):
                for c in range(2):
                    p.add("tensor", lambda e, tt=tt, c=c, half=half: e.matmul(
                        ps1[:], lat[c][:, tt * 128:(tt + 1) * 128],
                        wu_sb[c][:, half * 512:(half + 1) * 512],
                        start=(c == 0), stop=(c == 1)))
                p.add("vector", lambda e, half=half: e.tensor_tensor(
                    kv_t[:, half * 512:(half + 1) * 512], ps1[:],
                    bu_b[:, half * 512:(half + 1) * 512], ALU.add))
            p.dma(cos_t[:], cos_d.ap()[pos0:pos0 + 128, :])
            p.dma(sin_t[:], sin_d.ap()[pos0:pos0 + 128, :])
            cosb = cos_t[:].unsqueeze(1).broadcast_to([128, 16, 32])
            sinb = sin_t[:].unsqueeze(1).broadcast_to([128, 16, 32])
            q1 = bass.AP(q_t, 0, [[1024, 128], [64, 16], [1, 32]])
            q2 = bass.AP(q_t, 32, [[1024, 128], [64, 16], [1, 32]])
            qoA = bass.AP(q_ro, 0, [[1024, 128], [64, 16], [1, 32]])
            qoB = bass.AP(q_ro, 32, [[1024, 128], [64, 16], [1, 32]])
            tv = tr1[:].rearrange("p (h d) -> p h d", h=16)
            tv2 = tr2[:].rearrange("p (h d) -> p h d", h=16)
            k1 = bass.AP(kv_t, 0, [[2048, 128], [128, 16], [1, 32]])
            k2 = bass.AP(kv_t, 32, [[2048, 128], [128, 16], [1, 32]])
            koA = bass.AP(k_ro, 0, [[1024, 128], [64, 16], [1, 32]])
            koB = bass.AP(k_ro, 32, [[1024, 128], [64, 16], [1, 32]])
            for (a1, a2, oA, oB) in ((q1, q2, qoA, qoB), (k1, k2, koA, koB)):
                p.add("vector", lambda e, a1=a1, cb=cosb: e.tensor_tensor(tv, a1, cb, ALU.mult))
                p.add("vector", lambda e, a2=a2, sb=sinb: e.tensor_tensor(tv2, a2, sb, ALU.mult))
                p.add("vector", lambda e, oA=oA: e.tensor_tensor(oA, tv, tv2, ALU.subtract))
                p.add("vector", lambda e, a1=a1, sb=sinb: e.tensor_tensor(tv, a1, sb, ALU.mult))
                p.add("vector", lambda e, a2=a2, cb=cosb: e.tensor_tensor(tv2, a2, cb, ALU.mult))
                p.add("vector", lambda e, oB=oB: e.tensor_tensor(oB, tv, tv2, ALU.add))
            vsrc = bass.AP(kv_t, 64, [[2048, 128], [128, 16], [1, 64]])
            p.add("vector", lambda e: e.tensor_copy(
                v_c[:].rearrange("p (h d) -> p h d", h=16), vsrc))
            qv = q_ro[:].rearrange("p (h d) -> p h d", h=16)
            for g in range(16):
                kg = k_ro[:, g * 64:(g + 1) * 64].unsqueeze(1).broadcast_to([128, 16, 64])
                p.add("vector", lambda e, kg=kg: e.tensor_tensor(
                    tmps[:].rearrange("p (h d) -> p h d", h=16), qv, kg, ALU.mult))
                sout = bass.AP(S1, g, [[256, 128], [16, 16]])
                p.add("vector", lambda e, sout=sout: e.tensor_reduce(
                    sout, tmps[:].rearrange("p (h d) -> p h d", h=16),
                    axis=AX.X, op=ALU.add))
            sv = S1[:].rearrange("p (h g) -> p h g", h=16)
            p.add("vector", lambda e: e.reduce_max(nmx[:], sv, axis=AX.X, negate=True))
            p.add("vector", lambda e: e.tensor_tensor(
                S2[:].rearrange("p (h g) -> p h g", h=16), sv,
                nmx[:].unsqueeze(2).broadcast_to([128, 16, 16]), ALU.add))
            k_s = kk(0.125)
            p.add("scalar", lambda e, k_s=k_s: e.activation(S2[:], S2[:], AF.Exp, scale=k_s))
            p.add("vector", lambda e: e.tensor_reduce(
                zs[:], S2[:].rearrange("p (h g) -> p h g", h=16), axis=AX.X, op=ALU.add))
            p.add("vector", lambda e: e.reciprocal(rzz[:], zs[:]))
            p.add("vector", lambda e: e.tensor_tensor(
                S2[:].rearrange("p (h g) -> p h g", h=16),
                S2[:].rearrange("p (h g) -> p h g", h=16),
                rzz[:].unsqueeze(2).broadcast_to([128, 16, 16]), ALU.mult))
            for g in range(16):
                pg = bass.AP(S2, g, [[256, 128], [16, 16], [0, 64]])
                vg = v_c[:, g * 64:(g + 1) * 64].unsqueeze(1).broadcast_to([128, 16, 64])
                if g == 0:
                    p.add("vector", lambda e, pg=pg, vg=vg: e.tensor_tensor(
                        o_acc[:].rearrange("p (h d) -> p h d", h=16), pg, vg, ALU.mult))
                else:
                    p.add("vector", lambda e, pg=pg, vg=vg: e.tensor_tensor(
                        tmps[:].rearrange("p (h d) -> p h d", h=16), pg, vg, ALU.mult))
                    p.add("vector", lambda e: e.tensor_add(o_acc[:], o_acc[:], tmps[:]))
            p.add("vector", lambda e: e.tensor_copy(o_bf[:], o_acc[:]))
            for c in range(DC):
                p.dma(a_tr[:, c * 128:(c + 1) * 128],
                      o_bf[:, c * 128:(c + 1) * 128], transpose=True)
            for c in range(DC):
                p.add("vector", lambda e, c=c, tt=tt: e.tensor_copy(
                    xn[c][:, tt * 128:(tt + 1) * 128], a_tr[:, c * 128:(c + 1) * 128]))

        for oc in range(DC):
            p.dma(wbuf[:], wo_b2.ap()[oc])
            for tb in range(NTOK // 512):
                for c in range(DC):
                    p.add("tensor", lambda e, oc=oc, tb=tb, c=c: e.matmul(
                        ps1[:], wbuf[:, c * 128:(c + 1) * 128],
                        xn[c][:, tb * 512:(tb + 1) * 512],
                        start=(c == 0), stop=(c == DC - 1)))
                p.add("vector", lambda e, oc=oc: e.tensor_scalar_add(
                    t1[:], ps1[:], bo_c[:, oc:oc + 1]))
                p.dma(xold[:], x_d.ap()[oc * 128:(oc + 1) * 128, tb * 512:(tb + 1) * 512])
                p.add("vector", lambda e: e.tensor_add(t1[:], t1[:], xold[:]))
                p.dma(xo_d.ap()[oc * 128:(oc + 1) * 128, tb * 512:(tb + 1) * 512], t1[:])
        scope_cm.__exit__(None, None, None)
        if standalone:
            return _finish()


def attn_a1_inputs(sh):
    return {k: sh[k] for k in ("ma_Wq", "ma_bq", "ma_Wdt", "ma_bdown", "ma_Wup",
                               "ma_bup", "ma_Wot", "ma_bout", "ma_cos", "ma_sin")}




def build_attn_b(FC=None):
    standalone = FC is None
    with contextlib.ExitStack() as outer:
        if standalone:
            FC, _finish = _standalone_ctx(outer)
        nc, p, kk, ones_b = FC.nc, FC.p, FC.kk, FC.ones_b
        x_d, xo_d = FC.x_d, FC.xo_d
        qw_d = FC.dram("ab_qw_sl", [128, 1024])
        kw_b = FC.precast("ab_k_w", [D, D])
        vw_b = FC.precast("ab_v_w", [D, D])
        ow_b = FC.prep_wtiled("ab_owt", 8, 1024)
        qb_d = FC.dram("ab_qb_sl", [1, 128])
        kb_d = FC.dram("ab_k_b", [1, D])
        vb_d = FC.dram("ab_v_b", [1, D])
        ob_d = FC.dram("ab_out_b", [D])
        bk_d = FC.dram("ab_bias_k", [1, D])
        bv_d = FC.dram("ab_bias_v", [1, D])
        gw_d = FC.dram("ab_grep_w", [64, 8])
        gb_d = FC.dram("ab_grep_b", [1, 8])
        ga_d = FC.dram("ab_grep_a", [1, 16])
        pb_d = FC.dram("ab_pb", [2, 272])

        cc_in = {}
        cc_out = {}
        for nm in ("k", "v"):
            cc_in[nm] = nc.dram_tensor(f"ccin_{nm}", [NTOK, D], BF16)
            cc_out[nm] = nc.dram_tensor(f"ccout_{nm}", [B * T, D], BF16)
        cc_in["xn"] = nc.dram_tensor("ccin_xn", [D, NTOK], BF16)
        cc_out["xn"] = nc.dram_tensor("ccout_xn", [NCORES * D, NTOK], BF16)
        scr_d = nc.dram_tensor("scr1", [1, 64], F32)

        scope_cm = FC.scope()
        al = scope_cm.__enter__()
        lb = LNBufs(al, nc, 512)
        xn = al.sbl(DC, (128, NTOK), BF16, "xn")
        wbuf = al.sb((128, 1024), BF16, "wbuf")
        biasrow = al.sb((128, 1024), F32, "biasrow")
        qtmp = al.sb((128, 1024), BF16, "qtmp")
        kf = al.sb((128, 17 * 1024), BF16, "kf")
        vf = al.sb((128, 17 * 1024), BF16, "vf")
        wsb = [kf[:, c * 1024:(c + 1) * 1024] for c in range(DC)]
        q_nn = al.sb((128, 2048), BF16, "qnn")
        qwsl = al.sb((128, 1024), BF16, "qwsl")
        qxt = al.sb((128, 1024), BF16, "qxt")
        qb_b = al.sb((128, 128), F32, "qbb")
        gv_b = al.sb((128, 64), F32, "gvb")
        ga_b = al.sb((128, 16), F32, "gab")
        gb_b = al.sb((128, 1), F32, "gbb")
        pb_b = al.sbl(2, (128, 272), F32, "pbb")
        gw_sb = al.sb((64, 8), F32, "gwsb")
        gv_col = al.sb((64, 1), F32, "gvcol")
        gb_row = al.sb((1, 8), F32, "gbrow")
        gb_one = al.sb((1, 1), F32, "gbone")
        S = al.sb((128, 272), F32, "S")
        tmp272 = al.sb((128, 272), F32, "tmp272")
        P_bf = al.sb((128, 272), BF16, "Pbf")
        nmx = al.sb((128, 16), F32, "nmx16")
        zs = al.sb((128, 16), F32, "zs16")
        rzz = al.sb((128, 16), F32, "rzz16")
        gdot = al.sb((128, 16), F32, "gdot")
        g0 = al.sb((128, 16), F32, "g0")
        tmps = al.sb((128, 1088), F32, "tmps")
        o_acc = al.sb((128, 1024), F32, "oacc")
        o_bf = al.sb((128, 1024), BF16, "obf")
        a_tr = al.sb((128, 1024), BF16, "atr")
        ob_c = al.sb((128, 8), F32, "obc")
        xold = al.sb((128, 512), F32, "xold")
        t1 = al.sb((128, 512), F32, "t1")
        ps1 = al.ps((128, 512), "ps1")
        ps2 = al.ps((128, 512), "ps2")

        load_scale_cols(p, ob_c, ob_d, 8)
        p.dma(gw_sb[:], gw_d.ap())
        p.add("vector", lambda e: e.tensor_reduce(gv_col[:], gw_sb[:], axis=AX.X, op=ALU.add))
        p.dma(gb_row[:], gb_d.ap())
        p.add("vector", lambda e: e.tensor_reduce(gb_one[:], gb_row[:], axis=AX.X, op=ALU.add))
        p.dma(scr_d.ap()[0:1, 0:1], gb_one[:])
        p.dma(scr_d.ap()[0:1, 1:2], gv_col[:].rearrange("p o -> o p")
              if False else gv_col[0:1, :])
        p.dma(gb_b[:], scr_d.ap()[0:1, 0:1].broadcast_to([128, 1]))
        gv_d = nc.dram_tensor("gvscr", [64, 1], F32)
        p.dma(gv_d.ap(), gv_col[:])
        p.dma(gv_b[:], bass.AP(gv_d, 0, [[0, 128], [1, 64]]))
        p.dma(ga_b[:], ga_d.ap().broadcast_to([128, 16]))
        for nl in range(2):
            p.dma(pb_b[nl][:], pb_d.ap()[nl:nl + 1, :].broadcast_to([128, 272]))
        p.cast_dma(qwsl[:], qw_d.ap())
        p.dma(qb_b[:], qb_d.ap().broadcast_to([128, 128]))

        for tb in range(NTOK // 512):
            emit_ln(p, lb, ones_b, xn, tb * 512, kk, x_d=x_d, tok0=tb * 512)

        for c in range(DC):
            p.dma(cc_in["xn"].ap()[c * 128:(c + 1) * 128, :], xn[c][:])
        p.collective("AllGather", ALU.bypass, [list(range(NCORES))],
                     [cc_in["xn"].ap()], [cc_out["xn"].ap()])
        for nm, w_b, b_d in (("k", kw_b, kb_d), ("v", vw_b, vb_d)):
            for c in range(DC):
                p.dma(wsb[c], w_b.ap()[c * 128:(c + 1) * 128, :])
            p.dma(biasrow[:], b_d.ap().broadcast_to([128, 1024]))
            for tt in range(NTOK // 128):
                for half in range(2):
                    for c in range(DC):
                        p.add("tensor", lambda e, tt=tt, c=c, half=half: e.matmul(
                            ps1[:], xn[c][:, tt * 128:(tt + 1) * 128],
                            wsb[c][:, half * 512:(half + 1) * 512],
                            start=(c == 0), stop=(c == DC - 1)))
                    p.add("vector", lambda e, half=half: e.tensor_tensor(
                        qtmp[:, half * 512:(half + 1) * 512], ps1[:],
                        biasrow[:, half * 512:(half + 1) * 512], ALU.add))
                p.dma(cc_in[nm].ap()[tt * 128:(tt + 1) * 128, :], qtmp[:])
            p.collective("AllGather", ALU.bypass, [list(range(NCORES))],
                         [cc_in[nm].ap()], [cc_out[nm].ap()])

        p.cast_dma(kf[:, 16 * 1024:].rearrange("p (o d) -> p o d", o=1),
                   bk_d.ap().unsqueeze(0).broadcast_to([128, 1, 1024]))
        p.cast_dma(vf[:, 16 * 1024:].rearrange("p (o d) -> p o d", o=1),
                   bv_d.ap().unsqueeze(0).broadcast_to([128, 1, 1024]))

        if DBG:
            dbg_kf = nc.dram_tensor("dbg_kf", [128, 2048], BF16, kind="ExternalOutput")
            dbg_qnn = nc.dram_tensor("dbg_qnn", [128, 2048], BF16, kind="ExternalOutput")
            dbg_S = nc.dram_tensor("dbg_S", [128, 272], F32, kind="ExternalOutput")
            dbg_gv = nc.dram_tensor("dbg_gv", [128, 64], F32, kind="ExternalOutput")
            dbg_gb = nc.dram_tensor("dbg_gb", [128, 1], F32, kind="ExternalOutput")
            dbg_g0 = nc.dram_tensor("dbg_g0", [128, 16], F32, kind="ExternalOutput")
            dbg_gd = nc.dram_tensor("dbg_gd", [128, 16], F32, kind="ExternalOutput")
            dbg_Sr = nc.dram_tensor("dbg_Sr", [128, 272], F32, kind="ExternalOutput")
            dbg_pb = nc.dram_tensor("dbg_pb", [128, 272], F32, kind="ExternalOutput")
            dbg_t272 = nc.dram_tensor("dbg_t272", [128, 272], F32, kind="ExternalOutput")
            p.dma(dbg_gv.ap(), gv_b[:])
            p.dma(dbg_gb.ap(), gb_b[:])
        for bt in range(16):
            for s2 in range(16):
                p.dma(kf[:, s2 * 1024:(s2 + 1) * 1024],
                      cc_out["k"].ap()[s2 * T + bt * 128:s2 * T + (bt + 1) * 128, :])
                p.dma(vf[:, s2 * 1024:(s2 + 1) * 1024],
                      cc_out["v"].ap()[s2 * T + bt * 128:s2 * T + (bt + 1) * 128, :])
            if DBG and bt == 0:
                p.dma(dbg_kf.ap(), kf[:, 0:2048])
            for tq in range(16):
                rbase = (tq // 2) * D
                cbase = (tq % 2) * T + bt * 128
                for c in range(DC):
                    p.dma(qxt[:, c * 128:(c + 1) * 128],
                          cc_out["xn"].ap()[rbase + c * 128:rbase + (c + 1) * 128,
                                            cbase:cbase + 128])
                for c in range(DC):
                    p.add("tensor", lambda e, c=c: e.matmul(
                        ps1[:, 0:128], qxt[:, c * 128:(c + 1) * 128],
                        qwsl[:, c * 128:(c + 1) * 128],
                        start=(c == 0), stop=(c == DC - 1)))
                p.add("vector", lambda e, tq=tq: e.tensor_tensor(
                    q_nn[:, tq * 128:(tq + 1) * 128], ps1[:, 0:128], qb_b[:], ALU.add))
            if DBG and bt == 0:
                p.dma(dbg_qnn.ap(), q_nn[:])
            for nl in range(2):
                for tq in range(16):
                    kwin = kf[:].rearrange("p (r h) -> p r h", h=64)[:, tq * 17:tq * 17 + 17, :]
                    qs = q_nn[:, tq * 128 + nl * 64:tq * 128 + nl * 64 + 64]\
                        .unsqueeze(1).broadcast_to([128, 17, 64])
                    p.add("vector", lambda e, qs=qs, kwin=kwin: e.tensor_tensor(
                        tmps[:].rearrange("p (r h) -> p r h", h=64), qs, kwin, ALU.mult))
                    p.add("vector", lambda e, tq=tq: e.tensor_reduce(
                        S[:, tq * 17:tq * 17 + 17],
                        tmps[:].rearrange("p (r h) -> p r h", h=64), axis=AX.X, op=ALU.add))
                qg = bass.AP(q_nn, nl * 64, [[2048, 128], [128, 16], [1, 64]])
                p.add("vector", lambda e, qg=qg: e.tensor_tensor(
                    tmps[:, 0:1024].rearrange("p (t d) -> p t d", t=16), qg,
                    gv_b[:].unsqueeze(1).broadcast_to([128, 16, 64]), ALU.mult))
                p.add("vector", lambda e: e.tensor_reduce(
                    gdot[:], tmps[:, 0:1024].rearrange("p (t d) -> p t d", t=16),
                    axis=AX.X, op=ALU.add))
                p.add("scalar", lambda e: e.activation(g0[:], gdot[:], AF.Sigmoid, bias=gb_b[:]))
                p.add("vector", lambda e: e.tensor_tensor(g0[:], g0[:], ga_b[:], ALU.mult))
                if DBG and bt == 0 and nl == 0:
                    p.dma(dbg_g0.ap(), g0[:])
                    p.dma(dbg_gd.ap(), gdot[:])
                    p.dma(dbg_Sr.ap(), S[:])
                    p.dma(dbg_pb.ap(), pb_b[0][:])
                p.add("vector", lambda e, nl=nl: e.tensor_tensor(
                    tmp272[:].rearrange("p (t s) -> p t s", t=16),
                    g0[:].unsqueeze(2).broadcast_to([128, 16, 17]),
                    pb_b[nl][:].rearrange("p (t s) -> p t s", t=16), ALU.mult))
                p.add("vector", lambda e: e.scalar_tensor_tensor(
                    S[:], S[:], 1.0 / 256.0, tmp272[:], ALU.mult, ALU.add))
                if DBG and bt == 0 and nl == 0:
                    p.dma(dbg_S.ap(), S[:])
                    p.dma(dbg_t272.ap(), tmp272[:])
                sv = S[:].rearrange("p (t s) -> p t s", t=16)
                p.add("vector", lambda e, sv=sv: e.reduce_max(nmx[:], sv, axis=AX.X, negate=True))
                p.add("vector", lambda e, sv=sv: e.tensor_tensor(
                    sv, sv, nmx[:].unsqueeze(2).broadcast_to([128, 16, 17]), ALU.add))
                p.add("scalar", lambda e: e.activation(S[:], S[:], AF.Exp))
                p.add("vector", lambda e, sv=sv: e.tensor_reduce(zs[:], sv, axis=AX.X, op=ALU.add))
                p.add("vector", lambda e: e.reciprocal(rzz[:], zs[:]))
                p.add("vector", lambda e, sv=sv: e.tensor_tensor(
                    sv, sv, rzz[:].unsqueeze(2).broadcast_to([128, 16, 17]), ALU.mult))
                p.add("vector", lambda e: e.tensor_copy(P_bf[:], S[:]))
                for tq in range(16):
                    pg = P_bf[:, tq * 17:tq * 17 + 17].unsqueeze(2).broadcast_to([128, 17, 64])
                    vwin = vf[:].rearrange("p (r h) -> p r h", h=64)[:, tq * 17:tq * 17 + 17, :]
                    oT = bass.AP(tmps, 0, [[1088, 128], [1, 17], [17, 64]])
                    p.add("vector", lambda e, pg=pg, vwin=vwin, oT=oT: e.tensor_tensor(
                        oT, pg, vwin, ALU.mult))
                    p.add("vector", lambda e, tq=tq: e.tensor_reduce(
                        o_acc[:, tq * 64:(tq + 1) * 64],
                        tmps[:, 0:1088].rearrange("p (d s) -> p d s", d=64),
                        axis=AX.X, op=ALU.add))
                p.add("vector", lambda e: e.tensor_copy(o_bf[:], o_acc[:]))
                for c in range(DC):
                    p.dma(a_tr[:, c * 128:(c + 1) * 128],
                          o_bf[:, c * 128:(c + 1) * 128], transpose=True)
                for c in range(DC):
                    p.add("vector", lambda e, c=c, nl=nl, bt=bt: e.tensor_copy(
                        xn[c][:, nl * T + bt * 128:nl * T + (bt + 1) * 128],
                        a_tr[:, c * 128:(c + 1) * 128]))

        for oc in range(DC):
            p.dma(wbuf[:], ow_b.ap()[oc])
            for tb in range(NTOK // 512):
                for c in range(DC):
                    p.add("tensor", lambda e, oc=oc, tb=tb, c=c: e.matmul(
                        ps2[:], wbuf[:, c * 128:(c + 1) * 128],
                        xn[c][:, tb * 512:(tb + 1) * 512],
                        start=(c == 0), stop=(c == DC - 1)))
                p.add("vector", lambda e, oc=oc: e.tensor_scalar_add(
                    t1[:], ps2[:], ob_c[:, oc:oc + 1]))
                p.dma(xold[:], x_d.ap()[oc * 128:(oc + 1) * 128, tb * 512:(tb + 1) * 512])
                p.add("vector", lambda e: e.tensor_add(t1[:], t1[:], xold[:]))
                p.dma(xo_d.ap()[oc * 128:(oc + 1) * 128, tb * 512:(tb + 1) * 512], t1[:])
        for nm2 in ("k", "v", "xn"):
            p.dma(qtmp[:], cc_out[nm2].ap()[0:128, 0:1024])
        scope_cm.__exit__(None, None, None)
        if standalone:
            return _finish()


def attn_b_inputs(sh, core):
    d = {k: sh[k] for k in ("ab_k_w", "ab_v_w", "ab_owt", "ab_k_b", "ab_v_b",
                            "ab_out_b", "ab_bias_k", "ab_bias_v", "ab_grep_w",
                            "ab_grep_b", "ab_grep_a")}
    d["ab_qw_sl"] = sh["ab_qw_sl"][core]
    d["ab_qb_sl"] = sh["ab_qb_sl"][core]
    d["ab_pb"] = sh["ab_pb"][core]
    return d




def _t(a):
    return np.ascontiguousarray(np.asarray(a).T)


def _tiled(w_in_out, noc):
    w = np.asarray(w_in_out, np.float32)
    nin, nout = w.shape
    ncc = nin // 128
    assert noc * 128 == nout
    t = w.reshape(ncc, 128, noc, 128).transpose(2, 1, 0, 3).reshape(noc, 128, ncc * 128)
    return np.ascontiguousarray(t)


def host_prep(inputs):
    sh = {}
    sh["w1Tt"] = _tiled(_t(inputs["L0_ffn_w1_w"]), 32)
    sh["w2Tt"] = _tiled(_t(inputs["L0_ffn_w2_w"]), 32)
    sh["h1Tt"] = _tiled(_t(inputs["L0_ffn_h1_w"]), 32)
    sh["h2Tt"] = _tiled(_t(inputs["L0_ffn_h2_w"]), 32)
    sh["w3Tt"] = _tiled(_t(inputs["L0_ffn_w3_w"]), 8)
    sh["w1s"] = np.asarray(inputs["L0_ffn_w1_s"])
    sh["w2s"] = np.asarray(inputs["L0_ffn_w2_s"])
    sh["h1s"] = np.asarray(inputs["L0_ffn_h1_s"])
    sh["h2s"] = np.asarray(inputs["L0_ffn_h2_s"])
    sh["w3s"] = np.asarray(inputs["L0_ffn_w3_s"])

    sh["aa_Wqt"] = _tiled(np.asarray(inputs["L0_attn_a_Wq"]), 8)
    sh["aa_Wkt"] = _tiled(np.asarray(inputs["L0_attn_a_Wk"]), 8)
    sh["aa_Wv"] = np.asarray(inputs["L0_attn_a_Wv"])
    sh["aa_Wot"] = _tiled(np.asarray(inputs["L0_attn_a_Wo"]), 8)
    sh["aa_bo"] = np.asarray(inputs["L0_attn_a_bo"])
    sh["aa_theta"] = np.ascontiguousarray(np.asarray(inputs["L0_attn_a_theta"]).reshape(1, 8))
    sh["aa_gamma"] = np.asarray(inputs["L0_attn_a_gamma"])
    sh["aa_beta"] = np.asarray(inputs["L0_attn_a_beta"])
    ii = np.arange(128)
    sh["m128"] = np.where(ii[:, None] >= ii[None, :], 0.0, -1e30).astype(np.float32)

    sh["mb_WinTt"] = _tiled(_t(inputs["L1_attn_b_W_in"]), 32)
    sh["mb_WdtTt"] = _tiled(_t(inputs["L1_attn_b_W_dt"]), 16)
    sh["mb_WoutTt"] = _tiled(_t(inputs["L1_attn_b_W_out"]), 8)
    sh["mb_conv_w"] = np.asarray(inputs["L1_attn_b_conv_w"])
    sh["mb_conv_b"] = np.asarray(inputs["L1_attn_b_conv_b"])
    sh["mb_A_log"] = np.asarray(inputs["L1_attn_b_A_log"])
    sh["mb_B"] = np.asarray(inputs["L1_attn_b_B_param"])
    sh["mb_C"] = np.asarray(inputs["L1_attn_b_C_param"])
    sh["mb_D"] = np.asarray(inputs["L1_attn_b_D_param"])
    sh["mb_b_dt"] = np.asarray(inputs["L1_attn_b_b_dt"])
    sh["mb_dt_bias"] = np.asarray(inputs["L1_attn_b_dt_bias"])

    rm = np.asarray(inputs["L1_ffn_rail_maker"])
    rup = np.asarray(inputs["L1_ffn_rail_up_proj"])
    rmBD = np.zeros((4, 8, 128, 128), np.float32)
    rupBD = np.zeros((4, 8, 128, 512), np.float32)
    for h in range(4):
        for rb in range(8):
            for rr in range(8):
                r = rb * 8 + rr
                rmBD[h, rb, rr * 16:(rr + 1) * 16, rr * 16:(rr + 1) * 16] = rm[h, r]
                rupBD[h, rb, rr * 16:(rr + 1) * 16, rr * 64:(rr + 1) * 64] = rup[h, r].T
    sh["rf_rmBD"] = rmBD
    sh["rf_rupBD"] = rupBD
    sh["rf_wcT"] = _t(inputs["L1_ffn_w_cproj"])

    sh["ma_Wq"] = np.asarray(inputs["L1_attn_a_Wq"])
    sh["ma_bq"] = np.ascontiguousarray(np.asarray(inputs["L1_attn_a_bq"]).reshape(1, -1))
    sh["ma_Wdt"] = _tiled(np.asarray(inputs["L1_attn_a_Wdown"]), 2)
    sh["ma_bdown"] = np.asarray(inputs["L1_attn_a_bdown"])
    sh["ma_Wup"] = np.asarray(inputs["L1_attn_a_Wup"])
    sh["ma_bup"] = np.ascontiguousarray(np.asarray(inputs["L1_attn_a_bup"]).reshape(1, -1))
    sh["ma_Wot"] = _tiled(np.asarray(inputs["L1_attn_a_Wout"]), 8)
    sh["ma_bout"] = np.asarray(inputs["L1_attn_a_bout"])
    inv = (1.0 / 10000.0 ** (np.arange(0, 64, 2, dtype=np.float32) / np.float32(64.0))).astype(np.float32)
    fr = (np.arange(T, dtype=np.float32)[:, None] * inv[None, :]).astype(np.float32)
    sh["ma_cos"] = np.cos(fr).astype(np.float32)
    sh["ma_sin"] = np.sin(fr).astype(np.float32)

    qw = np.asarray(inputs["L0_attn_b_q_w"])
    qb = np.asarray(inputs["L0_attn_b_q_b"])
    sh["ab_qw_sl"] = [np.ascontiguousarray(
        qw[:, c * 128:(c + 1) * 128].reshape(8, 128, 128).transpose(1, 0, 2).reshape(128, 1024))
        for c in range(NCORES)]
    sh["ab_qb_sl"] = [np.ascontiguousarray(qb[c * 128:(c + 1) * 128].reshape(1, 128))
                      for c in range(NCORES)]
    sh["ab_k_w"] = np.asarray(inputs["L0_attn_b_k_w"])
    sh["ab_v_w"] = np.asarray(inputs["L0_attn_b_v_w"])
    sh["ab_owt"] = _tiled(np.asarray(inputs["L0_attn_b_out_w"]), 8)
    sh["ab_k_b"] = np.ascontiguousarray(np.asarray(inputs["L0_attn_b_k_b"]).reshape(1, -1))
    sh["ab_v_b"] = np.ascontiguousarray(np.asarray(inputs["L0_attn_b_v_b"]).reshape(1, -1))
    sh["ab_out_b"] = np.asarray(inputs["L0_attn_b_out_b"])
    sh["ab_bias_k"] = np.ascontiguousarray(np.asarray(inputs["L0_attn_b_bias_k"]).reshape(1, -1))
    sh["ab_bias_v"] = np.ascontiguousarray(np.asarray(inputs["L0_attn_b_bias_v"]).reshape(1, -1))
    sh["ab_grep_w"] = np.asarray(inputs["L0_attn_b_grep_w"])
    sh["ab_grep_b"] = np.ascontiguousarray(np.asarray(inputs["L0_attn_b_grep_b"]).reshape(1, -1))
    sh["ab_grep_a"] = np.ascontiguousarray(np.asarray(inputs["L0_attn_b_grep_a"]).reshape(1, -1))
    rel_pos = np.arange(17)[None, :] - np.arange(16)[:, None]
    nb = 16
    buckets = (rel_pos > 0).astype(np.int32) * nb
    rp = np.abs(rel_pos)
    max_exact = 8
    is_small = rp < max_exact
    large = max_exact + (np.log(rp.astype(np.float32) / max_exact + 1e-6)
                         / np.log(128.0 / 8) * (nb - max_exact)).astype(np.int32)
    large = np.minimum(large, nb - 1)
    buckets = buckets + np.where(is_small, rp, large)
    rel_emb = np.asarray(inputs["L0_attn_b_rel_emb"])
    pb_all = []
    for c in range(NCORES):
        m = np.zeros((2, 272), np.float32)
        for nl in range(2):
            n = 2 * c + nl
            for tq in range(16):
                for s in range(17):
                    m[nl, tq * 17 + s] = rel_emb[buckets[n, s], tq]
        pb_all.append(m)
    sh["ab_pb"] = pb_all

    x = np.asarray(inputs["L0_attn_a_x"])
    xs = []
    for c in range(NCORES):
        xc = x[BPC * c:BPC * (c + 1)].reshape(NTOK, D)
        xs.append(np.ascontiguousarray(xc.T))
    return sh, xs


def run_launch(build_fn, per_core_maps):
    nc = build_fn()
    res = run_bass_kernel_spmd(nc, per_core_maps, core_ids=list(range(NCORES)))
    return res


FUSED = True


def build_fused():
    nc = bass.Bass("TRN2", target_bir_lowering=False)
    x0_d = nc.dram_tensor("x", [D, NTOK], F32, kind="ExternalInput")
    xo_d = nc.dram_tensor("xo", [D, NTOK], F32, kind="ExternalOutput")
    stages_x = [x0_d] + [nc.dram_tensor(f"xstage{i}", [D, NTOK], F32)
                         for i in range(1, 8)] + [xo_d]
    with contextlib.ExitStack() as stack:
        al = Alloc(nc, stack)
        p = Prog(nc)
        sems = {k: al.sem(f"ch_{k}") for k in ("c", "ds", "dg")}
        kk = Konst(al, p)
        ones_b = al.sb((128, 1), BF16, "onesb")
        p.add("vector", lambda e: e.memset(ones_b[:], 1.0))
        FC = FuseCtx(nc, al, p, kk, ones_b)
        seq = [build_attn_a, build_ffn0, build_attn_b, build_ffn0,
               build_attn_a1, build_rails, build_mamba, build_rails]
        for i, fn in enumerate(seq):
            FC.x_d = stages_x[i]
            FC.xo_d = stages_x[i + 1]
            fn(FC)
        p.emit(sems, al)
    return nc


def fused_inputs(sh, core):
    d = {}
    d.update(attn_a_inputs(sh))
    d.update(ffn0_inputs(sh))
    d.update(attn_b_inputs(sh, core))
    d.update(attn_a1_inputs(sh))
    d.update(rails_inputs(sh))
    d.update(mamba_inputs(sh))
    return d


def kernel(**inputs):
    sh, xs = host_prep(inputs)
    if FUSED:
        nc = build_fused()
        maps = [dict(fused_inputs(sh, c), x=xs[c]) for c in range(NCORES)]
        res = run_bass_kernel_spmd(nc, maps, core_ids=list(range(NCORES)))
        xs = [np.asarray(res.results[c]["xo"]) for c in range(NCORES)]
    else:
        stages = [
            (build_attn_a, lambda c: attn_a_inputs(sh)),
            (build_ffn0, lambda c: ffn0_inputs(sh)),
            (build_attn_b, lambda c: attn_b_inputs(sh, c)),
            (build_ffn0, lambda c: ffn0_inputs(sh)),
            (build_attn_a1, lambda c: attn_a1_inputs(sh)),
            (build_rails, lambda c: rails_inputs(sh)),
            (build_mamba, lambda c: mamba_inputs(sh)),
            (build_rails, lambda c: rails_inputs(sh)),
        ]
        built = {}
        for build_fn, mapfn in stages:
            if build_fn not in built:
                built[build_fn] = build_fn()
            nc = built[build_fn]
            maps = [dict(mapfn(c), x=xs[c]) for c in range(NCORES)]
            res = run_bass_kernel_spmd(nc, maps, core_ids=list(range(NCORES)))
            xs = [np.asarray(res.results[c]["xo"]) for c in range(NCORES)]
    out = np.stack([x.T.reshape(BPC, T, D) for x in xs]).reshape(B, T, D)
    return np.ascontiguousarray(out)


# version 09 (final)
# speedup vs baseline: 1.2608x; 1.0745x over previous
import contextlib

import numpy as np
import concourse.bass as bass
import concourse.mybir as mybir
from concourse.bass_utils import run_bass_kernel_spmd

F32 = mybir.dt.float32
BF16 = mybir.dt.bfloat16
AF = mybir.ActivationFunctionType
ALU = mybir.AluOpType
AX = mybir.AxisListType

NCORES = 8
B, T, D = 16, 2048, 1024
BPC = B // NCORES
NTOK = BPC * T
DC = D // 128


LAST_TOTAL = None
LAST_NSTEPS = None
NOCLEAR = ("dg",)
OWN_SKIP = ("tensor",)


class Prog:

    def __init__(self, nc):
        self.nc = nc
        self.steps = []
        self.n_cc = 0

    def mark(self):
        return len(self.steps)

    def add(self, eng, fn, inc=1, key=None, after=None, after_ds=False):
        if key is None:
            key = "tensor" if eng == "tensor" else "c"
        self.steps.append((eng, fn, inc, key, after, after_ds))

    def dma(self, out, in_, q="sync", after=None, **kw):
        key = "dg" if q == "gpsimd" else "ds"
        self.add(q, lambda e, o=out, i=in_, kw=kw: e.dma_start(o, i, **kw), 16, key,
                 after=after)

    def cast_dma(self, out, in_, **kw):
        self.add("gpsimd", lambda e, o=out, i=in_, kw=kw: e.dma_start(o, i, **kw), 16, "dg")

    def collective(self, kind, op, rg, ins, outs):
        key = f"cc{self.n_cc}"
        self.n_cc += 1
        self.add("gpsimd", lambda e, kind=kind, op=op, rg=rg, ins=ins, outs=outs:
                 e.collective_compute(kind, op, replica_groups=rg, ins=ins, outs=outs),
                 1, key)

    def emit(self, sems, al=None):
        global LAST_TOTAL, LAST_NSTEPS
        engines_used = [en for en in ["sync", "tensor", "vector", "scalar", "gpsimd"]
                        if any(s[0] == en for s in self.steps)]
        comp_keys = [k for k in ("tensor", "c")
                     if any(s[3] == k for s in self.steps)]
        keys = comp_keys + ["ds", "dg"] + [f"cc{i}" for i in range(self.n_cc)]
        sems = dict(sems)
        for k in comp_keys:
            if k not in sems:
                sems[k] = al.sem(f"eng_{k}")
        for i in range(self.n_cc):
            sems[f"cc{i}"] = al.sem(f"ccsem{i}")
        pre = []
        cum = {k: 0 for k in keys}
        for (_, _, inc, key, _aft, _ads) in self.steps:
            pre.append(dict(cum))
            cum[key] += inc
        self.total = dict(cum)
        LAST_TOTAL = dict(cum)
        LAST_NSTEPS = len(self.steps)
        done = al.sem("prog_done") if al is not None else None
        n_other = len([e for e in engines_used if e != "gpsimd"])
        with self.nc.Block() as block:
            for en in ["sync", "tensor", "vector", "scalar", "gpsimd"]:
                sub = [(i, self.steps[i]) for i in range(len(self.steps))
                       if self.steps[i][0] == en]
                if not sub and en != "gpsimd":
                    continue

                def mk(sub, en):
                    def body(eng):
                        last = {k: 0 for k in keys}
                        for i, (_, fn, inc, key, aft, ads) in sub:
                            for k in keys:
                                if k == "tensor" and en in OWN_SKIP:
                                    continue
                                v = pre[i][k]
                                if aft is not None and (k in comp_keys or
                                                        (ads and k in ("ds", "dg"))):
                                    v = pre[aft][k]
                                if v > last[k]:
                                    eng.wait_ge(sems[k], v)
                                    last[k] = v
                            fn(eng).then_inc(sems[key], inc)
                        if done is None:
                            return
                        if en != "gpsimd":
                            eng.nop().then_inc(done, 1)
                        else:
                            for k in keys:
                                if cum[k] > last[k] and k != "gpsimd":
                                    eng.wait_ge(sems[k], cum[k])
                            if n_other:
                                eng.wait_ge(done, n_other)
                            for k in keys:
                                if k in NOCLEAR:
                                    continue
                                eng.sem_clear(sems[k])
                            eng.sem_clear(done)
                    return body

                getattr(block, en)(mk(sub, en))


class Alloc:
    def __init__(self, nc, stack, ctr=None):
        self.nc = nc
        self.stack = stack
        self.ctr = ctr if ctr is not None else [0]

    @property
    def n(self):
        return self.ctr[0]

    def _id(self):
        self.ctr[0] += 1
        return self.ctr[0]

    def sb(self, shape, dt=F32, name=None):
        i = self._id()
        name = f"{name}_{i}" if name else f"sb{i}"
        return self.stack.enter_context(self.nc.sbuf_tensor(name, list(shape), dt))

    def sbl(self, k, shape, dt=F32, name=None):
        return [self.sb(shape, dt, name and f"{name}{i}") for i in range(k)]

    def ps(self, shape, name=None, dt=F32):
        i = self._id()
        name = f"{name}_{i}" if name else f"ps{i}"
        return self.stack.enter_context(self.nc.psum_tensor(name, list(shape), dt))

    def sem(self, name=None):
        i = self._id()
        return self.stack.enter_context(self.nc.semaphore(f"{name or 'sem'}_{i}"))


class FuseCtx:

    def __init__(self, nc, al, p, kk, ones_b):
        self.nc = nc
        self.al = al
        self.p = p
        self.kk = kk
        self.ones_b = ones_b
        self.x_d = None
        self.xo_d = None
        self._drams = {}

    def dram(self, name, shape, kind="ExternalInput"):
        if name not in self._drams:
            self._drams[name] = self.nc.dram_tensor(name, list(shape), F32, kind=kind)
        return self._drams[name]

    def prep_wtiled(self, name, noc, free):
        key = name + "_bf"
        if key not in self._drams:
            src = self.dram(name, [noc, 128, free])
            dst = self.nc.dram_tensor(key, [noc, 128, free], BF16)
            self._drams[key] = dst
            for oc in range(noc):
                self.p.cast_dma(dst.ap()[oc], src.ap()[oc])
        return self._drams[key]

    def precast(self, name, shape):
        key = name + "_bf"
        if key not in self._drams:
            src = self.dram(name, shape)
            dst = self.nc.dram_tensor(key, list(shape), BF16)
            self._drams[key] = dst
            flat_r = 1
            for s in shape[:-1]:
                flat_r *= s
            last = shape[-1]
            for r0 in range(0, flat_r, 128):
                rr = min(128, flat_r - r0)
                sv = bass.AP(src, r0 * last, [[last, rr], [1, last]])
                dv = bass.AP(dst, r0 * last, [[last, rr], [1, last]])
                self.p.cast_dma(dv, sv)
        return self._drams[key]

    @contextlib.contextmanager
    def scope(self):
        with contextlib.ExitStack() as stack:
            yield Alloc(self.nc, stack, self.al.ctr)


def _standalone_ctx(stack):
    nc = bass.Bass("TRN2", target_bir_lowering=False)
    x_d = nc.dram_tensor("x", [D, NTOK], F32, kind="ExternalInput")
    xo_d = nc.dram_tensor("xo", [D, NTOK], F32, kind="ExternalOutput")
    al = Alloc(nc, stack)
    p = Prog(nc)
    sems = {k: al.sem(f"ch_{k}") for k in ("c", "ds", "dg")}
    kk = Konst(al, p)
    ones_b = al.sb((128, 1), BF16, "onesb")
    p.add("vector", lambda e: e.memset(ones_b[:], 1.0))
    FC = FuseCtx(nc, al, p, kk, ones_b)
    FC.x_d = x_d
    FC.xo_d = xo_d

    def finish():
        p.emit(sems, al)
        return nc

    return FC, finish


class Konst:

    def __init__(self, al, p, n=24):
        self.tile = al.sb((128, n), F32, "konst")
        self.p = p
        self.map = {}
        self.n = 0

    def __call__(self, v):
        v = float(v)
        if v not in self.map:
            i = self.n
            self.n += 1
            self.map[v] = i
            self.p.add("vector", lambda e, i=i, v=v: e.memset(self.tile[:, i:i + 1], v))
        i = self.map[v]
        return self.tile[:, i:i + 1]


def load_scale_cols(p, dst, src_d, n):
    for c in range(n):
        p.dma(dst[:, c:c + 1], src_d.ap()[c * 128:(c + 1) * 128].unsqueeze(1))


def wcol_load(p, wbuf, w_d, oc, cchunks, rows=None):
    src = w_d.ap()[:, oc * 128:(oc + 1) * 128].rearrange("(c p) o -> p c o", p=128)
    p.cast_dma(wbuf[:].rearrange("p (c o) -> p c o", c=cchunks), src)




class LNBufs:

    def __init__(self, al, nc, nt, tag="", with_x=True):
        self.nt = nt
        self.xt = al.sbl(DC, (128, nt), F32, f"lnx{tag}") if with_x else None
        self.xb = al.sb((128, nt), BF16, f"lnxb{tag}")
        self.sqb = al.sb((128, nt), BF16, f"lnsqb{tag}")
        self.mean = al.sb((1, nt), F32, f"lnmean{tag}")
        self.msq = al.sb((1, nt), F32, f"lnmsq{tag}")
        self.var = al.sb((1, nt), F32, f"lnvar{tag}")
        self.rstd = al.sb((1, nt), F32, f"lnrstd{tag}")
        self.mbb = al.sb((128, nt), F32, f"lnmbb{tag}")
        self.rbb = al.sb((128, nt), F32, f"lnrbb{tag}")
        self.sq = al.sb((128, nt), F32, f"lnsq{tag}")
        self.psA = al.ps((1, nt), f"lnpsA{tag}")
        self.psB = al.ps((1, nt), f"lnpsB{tag}")
        self.rows_d = nc.dram_tensor(f"lnrows{tag}_{al._id()}", [2, nt], F32)


def emit_rowstats(p, lb, ones_b, tiles, nch, ndim, kk, nt=None, eps=1e-5):
    nt = nt or lb.nt
    for c in range(nch):
        p.add("vector", lambda e, c=c: e.tensor_copy(lb.xb[:, 0:nt], tiles[c][:, 0:nt]))
        p.add("tensor", lambda e, c=c: e.matmul(
            lb.psA[0:1, 0:nt], ones_b[:], lb.xb[:, 0:nt],
            start=(c == 0), stop=(c == nch - 1)))
    for c in range(nch):
        p.add("scalar", lambda e, c=c: e.activation(lb.sqb[:, 0:nt], tiles[c][:, 0:nt], AF.Square))
        p.add("tensor", lambda e, c=c: e.matmul(
            lb.psB[0:1, 0:nt], ones_b[:], lb.sqb[:, 0:nt],
            start=(c == 0), stop=(c == nch - 1)))
    p.add("vector", lambda e: e.tensor_scalar_mul(lb.mean[:, 0:nt], lb.psA[0:1, 0:nt], 1.0 / ndim))
    p.add("vector", lambda e: e.tensor_scalar_mul(lb.msq[:, 0:nt], lb.psB[0:1, 0:nt], 1.0 / ndim))
    p.add("vector", lambda e: e.tensor_tensor(lb.var[:, 0:nt], lb.mean[:, 0:nt], lb.mean[:, 0:nt], ALU.mult))
    p.add("vector", lambda e: e.tensor_tensor(lb.var[:, 0:nt], lb.msq[:, 0:nt], lb.var[:, 0:nt], ALU.subtract))
    p.add("vector", lambda e, eps=eps: e.tensor_scalar_add(lb.var[:, 0:nt], lb.var[:, 0:nt], float(eps)))
    p.add("scalar", lambda e: e.activation(lb.rstd[:, 0:nt], lb.var[:, 0:nt], AF.Ln))
    p.add("vector", lambda e: e.tensor_scalar_mul(lb.rstd[:, 0:nt], lb.rstd[:, 0:nt], -0.5))
    p.add("scalar", lambda e: e.activation(lb.rstd[:, 0:nt], lb.rstd[:, 0:nt], AF.Exp))
    p.dma(lb.rows_d.ap()[0:1, 0:nt], lb.mean[:, 0:nt])
    p.dma(lb.rows_d.ap()[1:2, 0:nt], lb.rstd[:, 0:nt])
    p.dma(lb.mbb[:, 0:nt], lb.rows_d.ap()[0:1, 0:nt].broadcast_to([128, nt]))
    p.dma(lb.rbb[:, 0:nt], lb.rows_d.ap()[1:2, 0:nt].broadcast_to([128, nt]))


def emit_ln(p, lb, ones_b, xn_tiles, xn_col0, kk, x_d=None, tok0=0,
            xt=None, nt=None):
    nt = nt or lb.nt
    if xt is None:
        xt = lb.xt
        for c in range(DC):
            p.dma(xt[c][:, 0:nt], x_d.ap()[c * 128:(c + 1) * 128, tok0:tok0 + nt])
    emit_rowstats(p, lb, ones_b, xt, DC, D, kk, nt=nt)
    for c in range(DC):
        p.add("vector", lambda e, c=c: e.tensor_tensor(lb.sq[:, 0:nt], xt[c][:, 0:nt], lb.mbb[:, 0:nt], ALU.subtract))
        p.add("vector", lambda e, c=c: e.tensor_tensor(
            xn_tiles[c][:, xn_col0:xn_col0 + nt], lb.sq[:, 0:nt], lb.rbb[:, 0:nt], ALU.mult))
    return xt



DBG = False


def _tap(p, nc, name, src_ap, shape, q="sync"):
    if not DBG:
        return
    d = nc.dram_tensor(f"dbg_{name}", list(shape), F32, kind="ExternalOutput")
    if src_ap.dtype == F32:
        p.dma(d.ap(), src_ap, q=q)
    else:
        p.cast_dma(d.ap(), src_ap)


def build_ffn0(FC=None):
    standalone = FC is None
    with contextlib.ExitStack() as outer:
        if standalone:
            FC, _finish = _standalone_ctx(outer)
        nc, p, kk, ones_b = FC.nc, FC.p, FC.kk, FC.ones_b
        x_d, xo_d = FC.x_d, FC.xo_d
        w1_b = FC.prep_wtiled("w1Tt", 32, 1024)
        w2_b = FC.prep_wtiled("w2Tt", 32, 1024)
        h1_b = FC.prep_wtiled("h1Tt", 32, 4096)
        h2_b = FC.prep_wtiled("h2Tt", 32, 4096)
        w3_b = FC.prep_wtiled("w3Tt", 8, 4096)
        s1_d = FC.dram("w1s", [4096])
        s2_d = FC.dram("w2s", [4096])
        sh1_d = FC.dram("h1s", [4096])
        sh2_d = FC.dram("h2s", [4096])
        s3_d = FC.dram("w3s", [1024])

        NT = 512
        NB = NTOK // NT
        scope_cm = FC.scope()
        al = scope_cm.__enter__()
        lb = LNBufs(al, nc, NT)
        xn = al.sbl(DC, (128, NT), BF16, "xn")
        h_a = al.sbl(32, (128, NT), BF16, "ha")
        h_b = al.sbl(32, (128, NT), BF16, "hb")
        silu_t = al.sb((128, NT), F32, "silu")
        yo = al.sb((128, NT), F32, "yo")
        wbufs = al.sbl(2, (128, 1024), BF16, "wbuf")
        wbufs2 = al.sbl(2, (128, 1024), BF16, "wbuf2")
        whbufs = al.sbl(2, (128, 4096), BF16, "whbuf")
        s1 = al.sb((128, 32), F32, "s1")
        s2 = al.sb((128, 32), F32, "s2")
        sh1 = al.sb((128, 32), F32, "sh1")
        sh2 = al.sb((128, 32), F32, "sh2")
        s3 = al.sb((128, 8), F32, "s3")
        psr = [al.ps((128, NT), f"psr{j}") for j in range(4)]

        load_scale_cols(p, s1, s1_d, 32)
        load_scale_cols(p, s2, s2_d, 32)
        load_scale_cols(p, sh1, sh1_d, 32)
        load_scale_cols(p, sh2, sh2_d, 32)
        load_scale_cols(p, s3, s3_d, 8)

        for tb in range(NB):
            col0 = tb * NT
            xt = emit_ln(p, lb, ones_b, xn, 0, kk, x_d=x_d, tok0=col0)
            if tb == 0:
                _tap(p, nc, "mean", lb.mean[:], (1, NT))
                _tap(p, nc, "rstd", lb.rstd[:], (1, NT))
                _tap(p, nc, "xn0", xn[0][:], (128, NT))
                _tap(p, nc, "s1col", s1[:], (128, 32))
            mks = []
            for oc in range(32):
                aft = mks[oc - 2] if oc >= 2 else None
                wb, wb2 = wbufs[oc % 2], wbufs2[oc % 2]
                pa, pb = psr[(2 * oc) % 4], psr[(2 * oc + 1) % 4]
                p.dma(wb[:], w1_b.ap()[oc], after=aft)
                for c in range(DC):
                    p.add("tensor", lambda e, c=c, wb=wb, pa=pa: e.matmul(
                        pa[:], wb[:, c * 128:(c + 1) * 128], xn[c][:],
                        start=(c == 0), stop=(c == DC - 1)), after=aft)
                p.add("scalar", lambda e, oc=oc, pa=pa: e.activation(
                    silu_t[:], pa[:], AF.Silu, scale=s1[:, oc:oc + 1]))
                p.dma(wb2[:], w2_b.ap()[oc], after=aft)
                for c in range(DC):
                    p.add("tensor", lambda e, c=c, wb2=wb2, pb=pb: e.matmul(
                        pb[:], wb2[:, c * 128:(c + 1) * 128], xn[c][:],
                        start=(c == 0), stop=(c == DC - 1)), after=aft)
                p.add("vector", lambda e, oc=oc, pb=pb: e.scalar_tensor_tensor(
                    h_a[oc][:], pb[:], s2[:, oc:oc + 1], silu_t[:], ALU.mult, ALU.mult))
                mks.append(p.mark())
            mks = []
            for oc in range(32):
                aft = mks[oc - 2] if oc >= 2 else None
                wh, pa = whbufs[oc % 2], psr[oc % 4]
                p.dma(wh[:], h1_b.ap()[oc], after=aft)
                for c in range(32):
                    p.add("tensor", lambda e, c=c, wh=wh, pa=pa: e.matmul(
                        pa[:], wh[:, c * 128:(c + 1) * 128], h_a[c][:],
                        start=(c == 0), stop=(c == 31)), after=aft)
                p.add("scalar", lambda e, oc=oc, pa=pa: e.activation(
                    h_b[oc][:], pa[:], AF.Gelu, scale=sh1[:, oc:oc + 1]))
                mks.append(p.mark())
            mks = []
            for oc in range(32):
                aft = mks[oc - 2] if oc >= 2 else None
                wh, pa = whbufs[oc % 2], psr[oc % 4]
                p.dma(wh[:], h2_b.ap()[oc], after=aft)
                for c in range(32):
                    p.add("tensor", lambda e, c=c, wh=wh, pa=pa: e.matmul(
                        pa[:], wh[:, c * 128:(c + 1) * 128], h_b[c][:],
                        start=(c == 0), stop=(c == 31)), after=aft)
                p.add("scalar", lambda e, oc=oc, pa=pa: e.activation(
                    h_a[oc][:], pa[:], AF.Gelu, scale=sh2[:, oc:oc + 1]))
                mks.append(p.mark())
            mks = []
            for oc in range(DC):
                aft = mks[oc - 2] if oc >= 2 else None
                wh, pa = whbufs[oc % 2], psr[oc % 4]
                p.dma(wh[:], w3_b.ap()[oc], after=aft)
                for c in range(32):
                    p.add("tensor", lambda e, c=c, wh=wh, pa=pa: e.matmul(
                        pa[:], wh[:, c * 128:(c + 1) * 128], h_a[c][:],
                        start=(c == 0), stop=(c == 31)), after=aft)
                p.add("vector", lambda e, oc=oc, pa=pa: e.scalar_tensor_tensor(
                    yo[:], pa[:], s3[:, oc:oc + 1], xt[oc][:], ALU.mult, ALU.add))
                p.dma(xo_d.ap()[oc * 128:(oc + 1) * 128, col0:col0 + NT], yo[:])
                mks.append(p.mark())

        scope_cm.__exit__(None, None, None)
        if standalone:
            return _finish()


def ffn0_inputs(shared):
    return {k: shared[k] for k in ("w1Tt", "w2Tt", "h1Tt", "h2Tt", "w3Tt",
                                   "w1s", "w2s", "h1s", "h2s", "w3s")}




def build_attn_a(FC=None):
    standalone = FC is None
    with contextlib.ExitStack() as outer:
        if standalone:
            FC, _finish = _standalone_ctx(outer)
        nc, p, kk, ones_b = FC.nc, FC.p, FC.kk, FC.ones_b
        x_d, xo_d = FC.x_d, FC.xo_d
        wq_b = FC.prep_wtiled("aa_Wqt", 8, 1024)
        wk_b = FC.prep_wtiled("aa_Wkt", 8, 1024)
        wv_b = FC.precast("aa_Wv", [D, D])
        wo_b = FC.prep_wtiled("aa_Wot", 8, 1024)
        bo_d = FC.dram("aa_bo", [D])
        th_d = FC.dram("aa_theta", [1, 8])
        ga_d = FC.dram("aa_gamma", [D])
        be_d = FC.dram("aa_beta", [D])
        m128_d = FC.dram("m128", [128, 128])
        decay_d = nc.dram_tensor("decay", [T, T], F32)

        NH, HD = 8, 128
        NTT = T // 128
        scope_cm = FC.scope()
        al = scope_cm.__enter__()
        lb = LNBufs(al, nc, 512, with_x=False)
        xnb = al.sbl(DC, (128, T), BF16, "xnb")
        xtb = al.sbl(DC, (128, 512), BF16, "xtb")
        q_fm = al.sbl(NH, (128, T), BF16, "qfm")
        k_fm = al.sbl(NH, (128, T), BF16, "kfm")
        v_tm = al.sbl(NTT, (128, D), BF16, "vtm")
        u16k = al.sb((128, 8192), BF16, "u16k")
        wbuf = al.sb((128, 1024), BF16, "wbuf")
        dec = al.sb((128, 1024), F32, "dec")
        srow = al.sb((128, T), F32, "srow")
        p_bf = al.sb((128, T), BF16, "pbf")
        a_f32 = al.sb((128, 512), F32, "af32")
        e_t = al.sb((128, 512), F32, "et")
        acc_t = al.sb((128, 512), F32, "acct")
        n_i = al.sb((128, 512), mybir.dt.int32, "ni")
        nf = al.sb((128, 512), F32, "nf")
        ngt = al.sb((128, 8), F32, "ngt")
        m128 = al.sb((128, 128), F32, "m128s")
        nm = al.sb((128, 1), F32, "nm")
        zrow = al.sb((128, 1), F32, "zrow")
        rz = al.sb((128, 1), F32, "rz")
        bo_c = al.sb((128, 8), F32, "boc")
        ga_c = al.sb((128, 8), F32, "gac")
        be_c = al.sb((128, 8), F32, "bec")
        xold = al.sb((128, 1024), F32, "xold")
        t1 = al.sb((128, 1024), F32, "t1")
        ps_big = al.ps((128, 2048), "psbig")
        ps_pv = al.ps((128, 512), "pspv")

        p.dma(m128[:], m128_d.ap())
        p.dma(ngt[:], th_d.ap().broadcast_to([128, 8]))
        p.add("vector", lambda e: e.tensor_scalar_mul(ngt[:], ngt[:], -1.0))
        load_scale_cols(p, bo_c, bo_d, 8)
        load_scale_cols(p, ga_c, ga_d, 8)
        load_scale_cols(p, be_c, be_d, 8)

        for tt in range(NTT):
            t_end = (tt + 1) * 128
            for sb in range((t_end + 511) // 512):
                w = min(512, t_end - sb * 512)
                p.add("gpsimd", lambda e, tt=tt, sb=sb, w=w: e.iota(
                    n_i[:, 0:w], [[-1, w]], base=tt * 128 - sb * 512,
                    channel_multiplier=1))
                p.add("vector", lambda e, w=w: e.tensor_copy(nf[:, 0:w], n_i[:, 0:w]))
                p.add("vector", lambda e, w=w: e.tensor_scalar_max(nf[:, 0:w], nf[:, 0:w], 0.0))
                for h in range(NH):
                    if h == 0:
                        p.add("scalar", lambda e, w=w, h=h: e.activation(
                            acc_t[:, 0:w], nf[:, 0:w], AF.Exp, scale=ngt[:, h:h + 1]))
                    else:
                        p.add("scalar", lambda e, w=w, h=h: e.activation(
                            e_t[:, 0:w], nf[:, 0:w], AF.Exp, scale=ngt[:, h:h + 1]))
                        p.add("vector", lambda e, w=w: e.tensor_add(
                            acc_t[:, 0:w], acc_t[:, 0:w], e_t[:, 0:w]))
                p.add("vector", lambda e, w=w: e.tensor_scalar_mul(
                    acc_t[:, 0:w], acc_t[:, 0:w], 1.0 / (NH * float(np.sqrt(HD)))))
                p.dma(decay_d.ap()[tt * 128:(tt + 1) * 128, sb * 512:sb * 512 + w],
                      acc_t[:, 0:w])

        for b in range(BPC):
            tok0 = b * T
            for tb in range(T // 512):
                for c in range(DC):
                    p.cast_dma(xtb[c][:], x_d.ap()[c * 128:(c + 1) * 128,
                                                   tok0 + tb * 512:tok0 + (tb + 1) * 512])
                emit_ln(p, lb, ones_b, xnb, tb * 512, kk, xt=xtb, nt=512)
            for w_b, dst in ((wq_b, q_fm), (wk_b, k_fm)):
                for oc in range(NH):
                    p.dma(wbuf[:], w_b.ap()[oc])
                    for tb in range(4):
                        for c in range(DC):
                            p.add("tensor", lambda e, oc=oc, tb=tb, c=c, dst=dst: e.matmul(
                                ps_big[:, 0:512], wbuf[:, c * 128:(c + 1) * 128],
                                xnb[c][:, tb * 512:(tb + 1) * 512],
                                start=(c == 0), stop=(c == DC - 1)))
                        p.add("vector", lambda e, oc=oc, tb=tb, dst=dst: e.tensor_copy(
                            dst[oc][:, tb * 512:(tb + 1) * 512], ps_big[:, 0:512]))
            for c in range(DC):
                p.dma(u16k[:, c * 1024:(c + 1) * 1024],
                      wv_b.ap()[c * 128:(c + 1) * 128, :])
            for tt in range(NTT):
                for half in range(2):
                    for c in range(DC):
                        p.add("tensor", lambda e, tt=tt, c=c, half=half: e.matmul(
                            ps_big[:, half * 512:(half + 1) * 512],
                            xnb[c][:, tt * 128:(tt + 1) * 128],
                            u16k[:, c * 1024 + half * 512:c * 1024 + (half + 1) * 512],
                            start=(c == 0), stop=(c == DC - 1)))
                    p.add("vector", lambda e, tt=tt, half=half: e.tensor_copy(
                        v_tm[tt][:, half * 512:(half + 1) * 512],
                        ps_big[:, half * 512:(half + 1) * 512]))

            for h in range(NH):
                for g in range(4):
                    g_end = (g + 1) * 512
                    nsc = g_end // 128
                    for ttg in range(4):
                        tt = g * 4 + ttg
                        t_end = (tt + 1) * 128
                        for sb in range((t_end + 511) // 512):
                            w = min(512, t_end - sb * 512)
                            p.add("tensor", lambda e, h=h, tt=tt, sb=sb, w=w: e.matmul(
                                ps_big[:, sb * 512:sb * 512 + w],
                                q_fm[h][:, tt * 128:(tt + 1) * 128],
                                k_fm[h][:, sb * 512:sb * 512 + w],
                                start=True, stop=True))
                        for half in range((t_end + 1023) // 1024):
                            w = min(1024, t_end - half * 1024)
                            p.dma(dec[:, 0:w],
                                  decay_d.ap()[tt * 128:(tt + 1) * 128,
                                               half * 1024:half * 1024 + w])
                            p.add("vector", lambda e, half=half, w=w: e.tensor_tensor(
                                srow[:, half * 1024:half * 1024 + w],
                                ps_big[:, half * 1024:half * 1024 + w],
                                dec[:, 0:w], ALU.mult))
                        p.add("vector", lambda e, tt=tt, t_end=t_end: e.tensor_add(
                            srow[:, tt * 128:t_end], srow[:, tt * 128:t_end], m128[:]))
                        p.add("vector", lambda e, t_end=t_end: e.reduce_max(
                            nm[:], srow[:, 0:t_end], axis=AX.X, negate=True))
                        p.add("scalar", lambda e, t_end=t_end: e.activation(
                            srow[:, 0:t_end], srow[:, 0:t_end], AF.Exp,
                            bias=nm[:], accum_out=zrow[:]))
                        p.add("vector", lambda e: e.reciprocal(rz[:], zrow[:]))
                        p.add("vector", lambda e, t_end=t_end: e.tensor_scalar_mul(
                            p_bf[:, 0:t_end], srow[:, 0:t_end], rz[:]))
                        if t_end < g_end:
                            p.add("vector", lambda e, t_end=t_end, g_end=g_end: e.memset(
                                p_bf[:, t_end:g_end], 0.0))
                        for sc in range(nsc):
                            p.dma(u16k[:, sc * 512 + ttg * 128:sc * 512 + (ttg + 1) * 128],
                                  p_bf[:, sc * 128:(sc + 1) * 128], transpose=True)
                    for sc in range(nsc):
                        p.add("tensor", lambda e, h=h, sc=sc, nsc=nsc: e.matmul(
                            ps_pv[:], v_tm[sc][:, h * 128:(h + 1) * 128],
                            u16k[:, sc * 512:(sc + 1) * 512],
                            start=(sc == 0), stop=(sc == nsc - 1)))
                    p.add("vector", lambda e: e.tensor_copy(a_f32[:], ps_pv[:]))
                    emit_rowstats(p, lb, ones_b, [a_f32], 1, 128, kk, nt=512)
                    p.add("vector", lambda e: e.tensor_tensor(
                        lb.sq[:, 0:512], a_f32[:], lb.mbb[:, 0:512], ALU.subtract))
                    p.add("vector", lambda e: e.tensor_tensor(
                        lb.sq[:, 0:512], lb.sq[:, 0:512], lb.rbb[:, 0:512], ALU.mult))
                    p.add("vector", lambda e, h=h, g=g: e.tensor_scalar(
                        xnb[h][:, g * 512:(g + 1) * 512], lb.sq[:, 0:512],
                        ga_c[:, h:h + 1], be_c[:, h:h + 1], ALU.mult, ALU.add))

            for oc in range(DC):
                p.dma(wbuf[:], wo_b.ap()[oc])
                for tb in range(4):
                    for c in range(DC):
                        p.add("tensor", lambda e, oc=oc, tb=tb, c=c: e.matmul(
                            ps_big[:, 0:512], wbuf[:, c * 128:(c + 1) * 128],
                            xnb[c][:, tb * 512:(tb + 1) * 512],
                            start=(c == 0), stop=(c == DC - 1)))
                    p.add("vector", lambda e, oc=oc: e.tensor_scalar_add(
                        t1[:, 0:512], ps_big[:, 0:512], bo_c[:, oc:oc + 1]))
                    p.dma(xold[:, 0:512], x_d.ap()[oc * 128:(oc + 1) * 128,
                                                   tok0 + tb * 512:tok0 + (tb + 1) * 512])
                    p.add("vector", lambda e: e.tensor_add(t1[:, 0:512], t1[:, 0:512], xold[:, 0:512]))
                    p.dma(xo_d.ap()[oc * 128:(oc + 1) * 128,
                                    tok0 + tb * 512:tok0 + (tb + 1) * 512], t1[:, 0:512])
        scope_cm.__exit__(None, None, None)
        if standalone:
            return _finish()


def attn_a_inputs(sh):
    return {k: sh[k] for k in ("aa_Wqt", "aa_Wkt", "aa_Wv", "aa_Wot", "aa_bo",
                               "aa_theta", "aa_gamma", "aa_beta", "m128")}




def build_mamba(FC=None):
    standalone = FC is None
    with contextlib.ExitStack() as outer:
        if standalone:
            FC, _finish = _standalone_ctx(outer)
        nc, p, kk, ones_b = FC.nc, FC.p, FC.kk, FC.ones_b
        x_d, xo_d = FC.x_d, FC.xo_d
        win_b = FC.prep_wtiled("mb_WinTt", 32, 1024)
        wdt_b = FC.prep_wtiled("mb_WdtTt", 16, 2048)
        wout_b = FC.prep_wtiled("mb_WoutTt", 8, 2048)
        cw_d = FC.dram("conv_w", [2048, 4])
        cb_d = FC.dram("conv_b", [2048])
        alog_d = FC.dram("A_log", [2048, 16])
        bp_d = FC.dram("B_param", [2048, 16])
        cp_d = FC.dram("C_param", [2048, 16])
        dp_d = FC.dram("D_param", [2048])
        bdt_d = FC.dram("b_dt", [2048])
        dtb_d = FC.dram("dt_bias", [2048])

        NI = 16
        LB = 512
        NLB = T // LB
        scope_cm = FC.scope()
        al = scope_cm.__enter__()
        lb_ = LNBufs(al, nc, 512, with_x=False)
        xtb = al.sbl(DC, (128, 512), BF16, "xtb")
        xnb = al.sbl(DC, (128, T), BF16, "xnb")
        xbp = al.sbl(NI, (128, 3 + LB), F32, "xbp")
        u_f = al.sbl(NI, (128, LB), F32, "uf")
        u_b = al.sbl(NI, (128, LB), BF16, "ub")
        y_b = al.sbl(NI, (128, LB), BF16, "yb")
        carry = al.sb((128, NI * 16), F32, "carry")
        cw = al.sb((128, 64), F32, "cw")
        cb_c = al.sb((128, 16), F32, "cbc")
        A_sb = al.sb((128, 256), F32, "Asb")
        B_sb = al.sb((128, 256), F32, "Bsb")
        C_sb = al.sb((128, 256), F32, "Csb")
        D_c = al.sb((128, 16), F32, "Dc")
        bias_c = al.sb((128, 16), F32, "biasc")
        tmp16 = al.sb((128, 16), F32, "tmp16")
        wbuf = al.sb((128, 1024), BF16, "wbuf")
        wdtbuf = al.sb((128, 2048), BF16, "wdtbuf")
        sdt = al.sb((128, LB), F32, "sdt")
        tA = al.sb((128, LB), F32, "tA")
        tL = al.sb((128, LB), F32, "tL")
        dt_t = al.sb((128, LB), F32, "dtt")
        wdu = al.sb((128, LB), F32, "wdu")
        dA_t = al.sb((128, LB), F32, "dAt")
        dBu_t = al.sb((128, LB), F32, "dBut")
        s_t = al.sb((128, LB), F32, "st")
        y0 = al.sb((128, LB), F32, "y0")
        y1 = al.sb((128, LB), F32, "y1")
        g_t = al.sb((128, LB), F32, "gt")
        xold = al.sb((128, LB), F32, "xold")
        t1 = al.sb((128, LB), F32, "t1")
        ps1 = al.ps((128, 512), "ps1")
        ps2 = al.ps((128, 512), "ps2")

        load_scale_cols(p, cb_c, cb_d, 16)
        load_scale_cols(p, D_c, dp_d, 16)
        load_scale_cols(p, bias_c, bdt_d, 16)
        load_scale_cols(p, tmp16, dtb_d, 16)
        p.add("vector", lambda e: e.tensor_add(bias_c[:], bias_c[:], tmp16[:]))
        for c in range(NI):
            p.dma(cw[:, c * 4:(c + 1) * 4], cw_d.ap()[c * 128:(c + 1) * 128, :])
            p.dma(A_sb[:, c * 16:(c + 1) * 16], alog_d.ap()[c * 128:(c + 1) * 128, :])
            p.dma(B_sb[:, c * 16:(c + 1) * 16], bp_d.ap()[c * 128:(c + 1) * 128, :])
            p.dma(C_sb[:, c * 16:(c + 1) * 16], cp_d.ap()[c * 128:(c + 1) * 128, :])
        p.add("scalar", lambda e: e.activation(A_sb[:], A_sb[:], AF.Exp))
        p.add("vector", lambda e: e.tensor_scalar_mul(A_sb[:], A_sb[:], -1.0))

        for b in range(BPC):
            tok0 = b * T
            for tb in range(T // 512):
                for c in range(DC):
                    p.cast_dma(xtb[c][:], x_d.ap()[c * 128:(c + 1) * 128,
                                                   tok0 + tb * 512:tok0 + (tb + 1) * 512])
                emit_ln(p, lb_, ones_b, xnb, tb * 512, kk, xt=xtb, nt=512)

            for li in range(NLB):
                l0 = li * LB
                for oc in range(NI):
                    if li == 0:
                        p.add("vector", lambda e, oc=oc: e.memset(xbp[oc][:, 0:3], 0.0))
                    else:
                        p.add("vector", lambda e, oc=oc: e.tensor_copy(
                            xbp[oc][:, 0:3], xbp[oc][:, LB:LB + 3]))
                for oc in range(NI):
                    p.dma(wbuf[:], win_b.ap()[oc])
                    for c in range(DC):
                        p.add("tensor", lambda e, oc=oc, c=c, l0=l0: e.matmul(
                            ps1[:], wbuf[:, c * 128:(c + 1) * 128],
                            xnb[c][:, l0:l0 + LB], start=(c == 0), stop=(c == DC - 1)))
                    p.add("vector", lambda e, oc=oc: e.tensor_copy(xbp[oc][:, 3:3 + LB], ps1[:]))
                    p.add("vector", lambda e, oc=oc: e.tensor_scalar_mul(
                        y0[:], xbp[oc][:, 0:LB], cw[:, oc * 4:oc * 4 + 1]))
                    for kkk in range(1, 4):
                        p.add("vector", lambda e, oc=oc, kkk=kkk: e.scalar_tensor_tensor(
                            y0[:], xbp[oc][:, kkk:kkk + LB], cw[:, oc * 4 + kkk:oc * 4 + kkk + 1],
                            y0[:], ALU.mult, ALU.add))
                    p.add("scalar", lambda e, oc=oc: e.activation(
                        u_f[oc][:], y0[:], AF.Silu, bias=cb_c[:, oc:oc + 1]))
                    p.add("vector", lambda e, oc=oc: e.tensor_copy(u_b[oc][:], u_f[oc][:]))

                for oc in range(NI):
                    p.dma(wdtbuf[:], wdt_b.ap()[oc])
                    for c in range(NI):
                        p.add("tensor", lambda e, oc=oc, c=c: e.matmul(
                            ps1[:], wdtbuf[:, c * 128:(c + 1) * 128], u_b[c][:],
                            start=(c == 0), stop=(c == NI - 1)))
                    p.add("vector", lambda e, oc=oc: e.tensor_scalar_add(
                        sdt[:], ps1[:], bias_c[:, oc:oc + 1]))
                    k_m1 = kk(-1.0)
                    k_p1 = kk(1.0)
                    p.add("scalar", lambda e: e.activation(tA[:], sdt[:], AF.Abs))
                    p.add("scalar", lambda e, k_m1=k_m1: e.activation(tA[:], tA[:], AF.Exp, scale=k_m1))
                    p.add("scalar", lambda e, k_p1=k_p1: e.activation(tL[:], tA[:], AF.Ln, bias=k_p1))
                    p.add("vector", lambda e: e.tensor_scalar_max(dt_t[:], sdt[:], 0.0))
                    p.add("vector", lambda e: e.tensor_add(dt_t[:], dt_t[:], tL[:]))
                    p.add("vector", lambda e, oc=oc: e.tensor_tensor(
                        wdu[:], dt_t[:], u_f[oc][:], ALU.mult))
                    for n in range(16):
                        cn = oc * 16 + n
                        p.add("scalar", lambda e, cn=cn: e.activation(
                            dA_t[:], dt_t[:], AF.Exp, scale=A_sb[:, cn:cn + 1]))
                        p.add("vector", lambda e, cn=cn: e.tensor_scalar_mul(
                            dBu_t[:], wdu[:], B_sb[:, cn:cn + 1]))
                        if li == 0:
                            p.add("vector", lambda e: e.tensor_tensor_scan(
                                s_t[:], dA_t[:], dBu_t[:], 0.0, ALU.mult, ALU.add))
                        else:
                            p.add("vector", lambda e, cn=cn: e.tensor_tensor_scan(
                                s_t[:], dA_t[:], dBu_t[:], carry[:, cn:cn + 1],
                                ALU.mult, ALU.add))
                        p.add("vector", lambda e, cn=cn: e.tensor_copy(
                            carry[:, cn:cn + 1], s_t[:, LB - 1:LB]))
                        if n == 0:
                            p.add("vector", lambda e, cn=cn: e.tensor_scalar_mul(
                                y0[:], s_t[:], C_sb[:, cn:cn + 1]))
                        else:
                            src_y, dst_y = (y0, y1) if n % 2 == 1 else (y1, y0)
                            p.add("vector", lambda e, cn=cn, s=src_y, d=dst_y: e.scalar_tensor_tensor(
                                d[:], s_t[:], C_sb[:, cn:cn + 1], s[:], ALU.mult, ALU.add))
                    y_fin = y1 if 15 % 2 == 1 else y0
                    p.dma(wbuf[:], win_b.ap()[16 + oc])
                    for c in range(DC):
                        p.add("tensor", lambda e, oc=oc, c=c, l0=l0: e.matmul(
                            ps2[:], wbuf[:, c * 128:(c + 1) * 128],
                            xnb[c][:, l0:l0 + LB], start=(c == 0), stop=(c == DC - 1)))
                    p.add("scalar", lambda e: e.activation(g_t[:], ps2[:], AF.Silu))
                    p.add("vector", lambda e, oc=oc, y_fin=y_fin: e.scalar_tensor_tensor(
                        t1[:], u_f[oc][:], D_c[:, oc:oc + 1], y_fin[:], ALU.mult, ALU.add))
                    p.add("vector", lambda e, oc=oc: e.tensor_tensor(
                        y_b[oc][:], t1[:], g_t[:], ALU.mult))

                for oc in range(DC):
                    p.dma(wdtbuf[:], wout_b.ap()[oc])
                    for c in range(NI):
                        p.add("tensor", lambda e, oc=oc, c=c: e.matmul(
                            ps1[:], wdtbuf[:, c * 128:(c + 1) * 128], y_b[c][:],
                            start=(c == 0), stop=(c == NI - 1)))
                    p.dma(xold[:], x_d.ap()[oc * 128:(oc + 1) * 128, tok0 + l0:tok0 + l0 + LB])
                    p.add("vector", lambda e: e.tensor_tensor(t1[:], ps1[:], xold[:], ALU.add))
                    p.dma(xo_d.ap()[oc * 128:(oc + 1) * 128, tok0 + l0:tok0 + l0 + LB], t1[:])
        scope_cm.__exit__(None, None, None)
        if standalone:
            return _finish()


def mamba_inputs(sh):
    return {
        "mb_WinTt": sh["mb_WinTt"], "mb_WdtTt": sh["mb_WdtTt"], "mb_WoutTt": sh["mb_WoutTt"],
        "conv_w": sh["mb_conv_w"], "conv_b": sh["mb_conv_b"], "A_log": sh["mb_A_log"],
        "B_param": sh["mb_B"], "C_param": sh["mb_C"], "D_param": sh["mb_D"],
        "b_dt": sh["mb_b_dt"], "dt_bias": sh["mb_dt_bias"],
    }




def build_rails(FC=None):
    standalone = FC is None
    with contextlib.ExitStack() as outer:
        if standalone:
            FC, _finish = _standalone_ctx(outer)
        nc, p, kk, ones_b = FC.nc, FC.p, FC.kk, FC.ones_b
        x_d, xo_d = FC.x_d, FC.xo_d
        rm_b = FC.precast("rmBD", [4, 8, 128, 128])
        rup_b = FC.precast("rupBD", [4, 8, 128, 512])
        wc_src = FC.dram("wcT", [16384, 1024])
        if "wcT_bfh" not in FC._drams:
            dsth = nc.dram_tensor("wcT_bfh", [2, 16384, 512], BF16)
            FC._drams["wcT_bfh"] = dsth
            for halfi in range(2):
                for r0 in range(0, 16384, 128):
                    FC.p.cast_dma(dsth.ap()[halfi, r0:r0 + 128],
                                  wc_src.ap()[r0:r0 + 128, halfi * 512:(halfi + 1) * 512])
        wc_b = FC._drams["wcT_bfh"]

        NT = 512
        scope_cm = FC.scope()
        al = scope_cm.__enter__()
        lb = LNBufs(al, nc, NT)
        xn = al.sbl(DC, (128, NT), BF16, "xn")
        rm_sb = al.sb((128, 4096), BF16, "rmsb")
        rup_sb = al.sb((128, 16384), BF16, "rupsb")
        y_sb = al.sbl(32, (128, NT), BF16, "ysb")
        z_scrs = al.sbl(2, (128, NT), BF16, "zscr")
        wc_sbs = al.sbl(4, (128, 512), BF16, "wcsb")
        t1 = al.sb((128, NT), F32, "t1")
        pszs = [al.ps((128, 512), f"psz{j}") for j in range(2)]
        ps_o = al.ps((128, 2048), "pso")

        for h in range(4):
            for rb in range(8):
                i = h * 8 + rb
                p.dma(rm_sb[:, i * 128:(i + 1) * 128], rm_b.ap()[h, rb])
                p.dma(rup_sb[:, i * 512:(i + 1) * 512], rup_b.ap()[h, rb])

        for tb in range(NTOK // NT):
            col0 = tb * NT
            xt = emit_ln(p, lb, ones_b, xn, 0, kk, x_d=x_d, tok0=col0)
            ymk = []
            for i in range(32):
                rb = i % 8
                aft = ymk[i - 2] if i >= 2 else None
                pz = pszs[i % 2]
                p.add("tensor", lambda e, i=i, rb=rb, pz=pz: e.matmul(
                    pz[:], rm_sb[:, i * 128:(i + 1) * 128], xn[rb][:],
                    start=True, stop=True), after=aft)
                p.add("vector", lambda e, i=i, pz=pz: e.tensor_copy(y_sb[i][:], pz[:]))
                ymk.append(p.mark())
            for half in range(2):
                h0 = p.mark()
                am = {}
                for cpre in range(3):
                    p.dma(wc_sbs[cpre][:], wc_b.ap()[half, cpre * 128:(cpre + 1) * 128])
                for ch in range(129):
                    if ch < 128:
                        i, zt = ch // 4, ch % 4
                        pz, zs = pszs[ch % 2], z_scrs[ch % 2]
                        aft = am.get(ch - 2, h0)
                        p.add("tensor", lambda e, i=i, zt=zt, pz=pz: e.matmul(
                            pz[:], rup_sb[:, i * 512 + zt * 128:i * 512 + (zt + 1) * 128],
                            y_sb[i][:], start=True, stop=True), after=aft,
                            after_ds=True)
                        p.add("scalar", lambda e, pz=pz, zs=zs: e.activation(
                            zs[:], pz[:], AF.Gelu))
                        am[ch] = p.mark()
                    if ch >= 1:
                        cz = ch - 1
                        wcb, zs = wc_sbs[cz % 4], z_scrs[cz % 2]
                        for oi in range(4):
                            p.add("tensor", lambda e, cz=cz, oi=oi, wcb=wcb, zs=zs: e.matmul(
                                ps_o[:, oi * 512:(oi + 1) * 512],
                                wcb[:, oi * 128:(oi + 1) * 128], zs[:],
                                start=(cz == 0), stop=(cz == 127)), after=am[cz],
                                after_ds=True)
                    if ch + 3 < 128:
                        p.dma(wc_sbs[(ch + 3) % 4][:],
                              wc_b.ap()[half, (ch + 3) * 128:(ch + 4) * 128])
                for oi in range(4):
                    oc = half * 4 + oi
                    p.dma(lb.sq[:, 0:NT], x_d.ap()[oc * 128:(oc + 1) * 128, col0:col0 + NT])
                    p.add("vector", lambda e, oi=oi: e.tensor_tensor(
                        t1[:], ps_o[:, oi * 512:(oi + 1) * 512], lb.sq[:, 0:NT], ALU.add))
                    p.dma(xo_d.ap()[oc * 128:(oc + 1) * 128, col0:col0 + NT], t1[:])
        scope_cm.__exit__(None, None, None)
        if standalone:
            return _finish()


def rails_inputs(sh):
    return {"rmBD": sh["rf_rmBD"], "rupBD": sh["rf_rupBD"], "wcT": sh["rf_wcT"]}




def build_attn_a1(FC=None):
    standalone = FC is None
    with contextlib.ExitStack() as outer:
        if standalone:
            FC, _finish = _standalone_ctx(outer)
        nc, p, kk, ones_b = FC.nc, FC.p, FC.kk, FC.ones_b
        x_d, xo_d = FC.x_d, FC.xo_d
        wq_b2 = FC.precast("ma_Wq", [D, D])
        bq_d = FC.dram("ma_bq", [1, D])
        wd_b = FC.prep_wtiled("ma_Wdt", 2, 1024)
        bd_d = FC.dram("ma_bdown", [256])
        wu_b2 = FC.precast("ma_Wup", [256, 2048])
        bu_d = FC.dram("ma_bup", [1, 2048])
        wo_b2 = FC.prep_wtiled("ma_Wot", 8, 1024)
        bo_d = FC.dram("ma_bout", [D])
        cos_d = FC.dram("ma_cos", [T, 32])
        sin_d = FC.dram("ma_sin", [T, 32])

        scope_cm = FC.scope()
        al = scope_cm.__enter__()
        lb = LNBufs(al, nc, 512)
        xn = al.sbl(DC, (128, NTOK), BF16, "xn")
        lat = al.sbl(2, (128, NTOK), BF16, "lat")
        wq_sb = al.sbl(DC, (128, 1024), BF16, "wqsb")
        wu_sb = al.sbl(2, (128, 2048), BF16, "wusb")
        wbuf = al.sb((128, 1024), BF16, "wbuf")
        bq_b = al.sb((128, 1024), F32, "bqb")
        bu_b = al.sb((128, 2048), F32, "bub")
        bd_c = al.sb((128, 2), F32, "bdc")
        bo_c = al.sb((128, 8), F32, "boc")
        cos_t = al.sb((128, 32), F32, "cost")
        sin_t = al.sb((128, 32), F32, "sint")
        q_t = al.sb((128, 1024), F32, "qt")
        kv_t = al.sb((128, 2048), F32, "kvt")
        q_ro = al.sb((128, 1024), F32, "qro")
        k_ro = al.sb((128, 1024), F32, "kro")
        v_c = al.sb((128, 1024), F32, "vc")
        tr1 = al.sb((128, 512), F32, "tr1")
        tr2 = al.sb((128, 512), F32, "tr2")
        S1 = al.sb((128, 256), F32, "S1")
        S2 = al.sb((128, 256), F32, "S2")
        nmx = al.sb((128, 16), F32, "nmx")
        zs = al.sb((128, 16), F32, "zs")
        rzz = al.sb((128, 16), F32, "rzz")
        o_acc = al.sb((128, 1024), F32, "oacc")
        tmps = al.sb((128, 1024), F32, "tmps")
        o_bf = al.sb((128, 1024), BF16, "obf")
        a_tr = al.sb((128, 1024), BF16, "atr")
        xold = al.sb((128, 512), F32, "xold")
        t1 = al.sb((128, 512), F32, "t1")
        ps1 = al.ps((128, 512), "ps1")

        p.dma(bq_b[:], bq_d.ap().broadcast_to([128, 1024]))
        p.dma(bu_b[:], bu_d.ap().broadcast_to([128, 2048]))
        load_scale_cols(p, bd_c, bd_d, 2)
        load_scale_cols(p, bo_c, bo_d, 8)
        for c in range(DC):
            p.dma(wq_sb[c][:], wq_b2.ap()[c * 128:(c + 1) * 128, :])
        for c in range(2):
            p.dma(wu_sb[c][:], wu_b2.ap()[c * 128:(c + 1) * 128, :])

        for tb in range(NTOK // 512):
            emit_ln(p, lb, ones_b, xn, tb * 512, kk, x_d=x_d, tok0=tb * 512)
        for oc in range(2):
            p.dma(wbuf[:], wd_b.ap()[oc])
            for tb in range(NTOK // 512):
                for c in range(DC):
                    p.add("tensor", lambda e, oc=oc, tb=tb, c=c: e.matmul(
                        ps1[:], wbuf[:, c * 128:(c + 1) * 128],
                        xn[c][:, tb * 512:(tb + 1) * 512],
                        start=(c == 0), stop=(c == DC - 1)))
                p.add("vector", lambda e, oc=oc, tb=tb: e.tensor_scalar_add(
                    lat[oc][:, tb * 512:(tb + 1) * 512], ps1[:], bd_c[:, oc:oc + 1]))

        H = 16
        for tt in range(NTOK // 128):
            pos0 = (tt % 16) * 128
            for half in range(2):
                for c in range(DC):
                    p.add("tensor", lambda e, tt=tt, c=c, half=half: e.matmul(
                        ps1[:], xn[c][:, tt * 128:(tt + 1) * 128],
                        wq_sb[c][:, half * 512:(half + 1) * 512],
                        start=(c == 0), stop=(c == DC - 1)))
                p.add("vector", lambda e, half=half: e.tensor_tensor(
                    q_t[:, half * 512:(half + 1) * 512], ps1[:],
                    bq_b[:, half * 512:(half + 1) * 512], ALU.add))
            for half in range(4):
                for c in range(2):
                    p.add("tensor", lambda e, tt=tt, c=c, half=half: e.matmul(
                        ps1[:], lat[c][:, tt * 128:(tt + 1) * 128],
                        wu_sb[c][:, half * 512:(half + 1) * 512],
                        start=(c == 0), stop=(c == 1)))
                p.add("vector", lambda e, half=half: e.tensor_tensor(
                    kv_t[:, half * 512:(half + 1) * 512], ps1[:],
                    bu_b[:, half * 512:(half + 1) * 512], ALU.add))
            p.dma(cos_t[:], cos_d.ap()[pos0:pos0 + 128, :])
            p.dma(sin_t[:], sin_d.ap()[pos0:pos0 + 128, :])
            cosb = cos_t[:].unsqueeze(1).broadcast_to([128, 16, 32])
            sinb = sin_t[:].unsqueeze(1).broadcast_to([128, 16, 32])
            q1 = bass.AP(q_t, 0, [[1024, 128], [64, 16], [1, 32]])
            q2 = bass.AP(q_t, 32, [[1024, 128], [64, 16], [1, 32]])
            qoA = bass.AP(q_ro, 0, [[1024, 128], [64, 16], [1, 32]])
            qoB = bass.AP(q_ro, 32, [[1024, 128], [64, 16], [1, 32]])
            tv = tr1[:].rearrange("p (h d) -> p h d", h=16)
            tv2 = tr2[:].rearrange("p (h d) -> p h d", h=16)
            k1 = bass.AP(kv_t, 0, [[2048, 128], [128, 16], [1, 32]])
            k2 = bass.AP(kv_t, 32, [[2048, 128], [128, 16], [1, 32]])
            koA = bass.AP(k_ro, 0, [[1024, 128], [64, 16], [1, 32]])
            koB = bass.AP(k_ro, 32, [[1024, 128], [64, 16], [1, 32]])
            for (a1, a2, oA, oB) in ((q1, q2, qoA, qoB), (k1, k2, koA, koB)):
                p.add("vector", lambda e, a1=a1, cb=cosb: e.tensor_tensor(tv, a1, cb, ALU.mult))
                p.add("vector", lambda e, a2=a2, sb=sinb: e.tensor_tensor(tv2, a2, sb, ALU.mult))
                p.add("vector", lambda e, oA=oA: e.tensor_tensor(oA, tv, tv2, ALU.subtract))
                p.add("vector", lambda e, a1=a1, sb=sinb: e.tensor_tensor(tv, a1, sb, ALU.mult))
                p.add("vector", lambda e, a2=a2, cb=cosb: e.tensor_tensor(tv2, a2, cb, ALU.mult))
                p.add("vector", lambda e, oB=oB: e.tensor_tensor(oB, tv, tv2, ALU.add))
            vsrc = bass.AP(kv_t, 64, [[2048, 128], [128, 16], [1, 64]])
            p.add("vector", lambda e: e.tensor_copy(
                v_c[:].rearrange("p (h d) -> p h d", h=16), vsrc))
            qv = q_ro[:].rearrange("p (h d) -> p h d", h=16)
            for g in range(16):
                kg = k_ro[:, g * 64:(g + 1) * 64].unsqueeze(1).broadcast_to([128, 16, 64])
                p.add("vector", lambda e, kg=kg: e.tensor_tensor(
                    tmps[:].rearrange("p (h d) -> p h d", h=16), qv, kg, ALU.mult))
                sout = bass.AP(S1, g, [[256, 128], [16, 16]])
                p.add("vector", lambda e, sout=sout: e.tensor_reduce(
                    sout, tmps[:].rearrange("p (h d) -> p h d", h=16),
                    axis=AX.X, op=ALU.add))
            sv = S1[:].rearrange("p (h g) -> p h g", h=16)
            p.add("vector", lambda e: e.reduce_max(nmx[:], sv, axis=AX.X, negate=True))
            p.add("vector", lambda e: e.tensor_tensor(
                S2[:].rearrange("p (h g) -> p h g", h=16), sv,
                nmx[:].unsqueeze(2).broadcast_to([128, 16, 16]), ALU.add))
            k_s = kk(0.125)
            p.add("scalar", lambda e, k_s=k_s: e.activation(S2[:], S2[:], AF.Exp, scale=k_s))
            p.add("vector", lambda e: e.tensor_reduce(
                zs[:], S2[:].rearrange("p (h g) -> p h g", h=16), axis=AX.X, op=ALU.add))
            p.add("vector", lambda e: e.reciprocal(rzz[:], zs[:]))
            p.add("vector", lambda e: e.tensor_tensor(
                S2[:].rearrange("p (h g) -> p h g", h=16),
                S2[:].rearrange("p (h g) -> p h g", h=16),
                rzz[:].unsqueeze(2).broadcast_to([128, 16, 16]), ALU.mult))
            for g in range(16):
                pg = bass.AP(S2, g, [[256, 128], [16, 16], [0, 64]])
                vg = v_c[:, g * 64:(g + 1) * 64].unsqueeze(1).broadcast_to([128, 16, 64])
                if g == 0:
                    p.add("vector", lambda e, pg=pg, vg=vg: e.tensor_tensor(
                        o_acc[:].rearrange("p (h d) -> p h d", h=16), pg, vg, ALU.mult))
                else:
                    p.add("vector", lambda e, pg=pg, vg=vg: e.tensor_tensor(
                        tmps[:].rearrange("p (h d) -> p h d", h=16), pg, vg, ALU.mult))
                    p.add("vector", lambda e: e.tensor_add(o_acc[:], o_acc[:], tmps[:]))
            p.add("vector", lambda e: e.tensor_copy(o_bf[:], o_acc[:]))
            for c in range(DC):
                p.dma(a_tr[:, c * 128:(c + 1) * 128],
                      o_bf[:, c * 128:(c + 1) * 128], transpose=True)
            for c in range(DC):
                p.add("vector", lambda e, c=c, tt=tt: e.tensor_copy(
                    xn[c][:, tt * 128:(tt + 1) * 128], a_tr[:, c * 128:(c + 1) * 128]))

        for oc in range(DC):
            p.dma(wbuf[:], wo_b2.ap()[oc])
            for tb in range(NTOK // 512):
                for c in range(DC):
                    p.add("tensor", lambda e, oc=oc, tb=tb, c=c: e.matmul(
                        ps1[:], wbuf[:, c * 128:(c + 1) * 128],
                        xn[c][:, tb * 512:(tb + 1) * 512],
                        start=(c == 0), stop=(c == DC - 1)))
                p.add("vector", lambda e, oc=oc: e.tensor_scalar_add(
                    t1[:], ps1[:], bo_c[:, oc:oc + 1]))
                p.dma(xold[:], x_d.ap()[oc * 128:(oc + 1) * 128, tb * 512:(tb + 1) * 512])
                p.add("vector", lambda e: e.tensor_add(t1[:], t1[:], xold[:]))
                p.dma(xo_d.ap()[oc * 128:(oc + 1) * 128, tb * 512:(tb + 1) * 512], t1[:])
        scope_cm.__exit__(None, None, None)
        if standalone:
            return _finish()


def attn_a1_inputs(sh):
    return {k: sh[k] for k in ("ma_Wq", "ma_bq", "ma_Wdt", "ma_bdown", "ma_Wup",
                               "ma_bup", "ma_Wot", "ma_bout", "ma_cos", "ma_sin")}




def build_attn_b(FC=None):
    standalone = FC is None
    with contextlib.ExitStack() as outer:
        if standalone:
            FC, _finish = _standalone_ctx(outer)
        nc, p, kk, ones_b = FC.nc, FC.p, FC.kk, FC.ones_b
        x_d, xo_d = FC.x_d, FC.xo_d
        qw_d = FC.dram("ab_qw_sl", [128, 1024])
        kw_b = FC.precast("ab_k_w", [D, D])
        vw_b = FC.precast("ab_v_w", [D, D])
        ow_b = FC.prep_wtiled("ab_owt", 8, 1024)
        qb_d = FC.dram("ab_qb_sl", [1, 128])
        kb_d = FC.dram("ab_k_b", [1, D])
        vb_d = FC.dram("ab_v_b", [1, D])
        ob_d = FC.dram("ab_out_b", [D])
        bk_d = FC.dram("ab_bias_k", [1, D])
        bv_d = FC.dram("ab_bias_v", [1, D])
        gw_d = FC.dram("ab_grep_w", [64, 8])
        gb_d = FC.dram("ab_grep_b", [1, 8])
        ga_d = FC.dram("ab_grep_a", [1, 16])
        pb_d = FC.dram("ab_pb", [2, 272])

        cc_in = {}
        cc_out = {}
        for nm in ("k", "v"):
            cc_in[nm] = nc.dram_tensor(f"ccin_{nm}", [NTOK, D], BF16)
            cc_out[nm] = nc.dram_tensor(f"ccout_{nm}", [B * T, D], BF16)
        cc_in["xn"] = nc.dram_tensor("ccin_xn", [D, NTOK], BF16)
        cc_out["xn"] = nc.dram_tensor("ccout_xn", [NCORES * D, NTOK], BF16)
        scr_d = nc.dram_tensor("scr1", [1, 64], F32)

        scope_cm = FC.scope()
        al = scope_cm.__enter__()
        lb = LNBufs(al, nc, 512)
        xn = al.sbl(DC, (128, NTOK), BF16, "xn")
        wbuf = al.sb((128, 1024), BF16, "wbuf")
        biasrow = al.sb((128, 1024), F32, "biasrow")
        qtmp = al.sb((128, 1024), BF16, "qtmp")
        kf = al.sb((128, 17 * 1024), BF16, "kf")
        vf = al.sb((128, 17 * 1024), BF16, "vf")
        wsb = [kf[:, c * 1024:(c + 1) * 1024] for c in range(DC)]
        q_nn = al.sb((128, 2048), BF16, "qnn")
        qwsl = al.sb((128, 1024), BF16, "qwsl")
        qxt = al.sb((128, 1024), BF16, "qxt")
        qb_b = al.sb((128, 128), F32, "qbb")
        gv_b = al.sb((128, 64), F32, "gvb")
        ga_b = al.sb((128, 16), F32, "gab")
        gb_b = al.sb((128, 1), F32, "gbb")
        pb_b = al.sbl(2, (128, 272), F32, "pbb")
        gw_sb = al.sb((64, 8), F32, "gwsb")
        gv_col = al.sb((64, 1), F32, "gvcol")
        gb_row = al.sb((1, 8), F32, "gbrow")
        gb_one = al.sb((1, 1), F32, "gbone")
        S = al.sb((128, 272), F32, "S")
        tmp272 = al.sb((128, 272), F32, "tmp272")
        P_bf = al.sb((128, 272), BF16, "Pbf")
        nmx = al.sb((128, 16), F32, "nmx16")
        zs = al.sb((128, 16), F32, "zs16")
        rzz = al.sb((128, 16), F32, "rzz16")
        gdot = al.sb((128, 16), F32, "gdot")
        g0 = al.sb((128, 16), F32, "g0")
        tmps = al.sb((128, 1088), F32, "tmps")
        o_acc = al.sb((128, 1024), F32, "oacc")
        o_bf = al.sb((128, 1024), BF16, "obf")
        a_tr = al.sb((128, 1024), BF16, "atr")
        ob_c = al.sb((128, 8), F32, "obc")
        xold = al.sb((128, 512), F32, "xold")
        t1 = al.sb((128, 512), F32, "t1")
        ps1 = al.ps((128, 512), "ps1")
        ps2 = al.ps((128, 512), "ps2")

        load_scale_cols(p, ob_c, ob_d, 8)
        p.dma(gw_sb[:], gw_d.ap())
        p.add("vector", lambda e: e.tensor_reduce(gv_col[:], gw_sb[:], axis=AX.X, op=ALU.add))
        p.dma(gb_row[:], gb_d.ap())
        p.add("vector", lambda e: e.tensor_reduce(gb_one[:], gb_row[:], axis=AX.X, op=ALU.add))
        p.dma(scr_d.ap()[0:1, 0:1], gb_one[:])
        p.dma(scr_d.ap()[0:1, 1:2], gv_col[:].rearrange("p o -> o p")
              if False else gv_col[0:1, :])
        p.dma(gb_b[:], scr_d.ap()[0:1, 0:1].broadcast_to([128, 1]))
        gv_d = nc.dram_tensor("gvscr", [64, 1], F32)
        p.dma(gv_d.ap(), gv_col[:])
        p.dma(gv_b[:], bass.AP(gv_d, 0, [[0, 128], [1, 64]]))
        p.dma(ga_b[:], ga_d.ap().broadcast_to([128, 16]))
        for nl in range(2):
            p.dma(pb_b[nl][:], pb_d.ap()[nl:nl + 1, :].broadcast_to([128, 272]))
        p.cast_dma(qwsl[:], qw_d.ap())
        p.dma(qb_b[:], qb_d.ap().broadcast_to([128, 128]))

        for tb in range(NTOK // 512):
            emit_ln(p, lb, ones_b, xn, tb * 512, kk, x_d=x_d, tok0=tb * 512)

        for c in range(DC):
            p.dma(cc_in["xn"].ap()[c * 128:(c + 1) * 128, :], xn[c][:])
        p.collective("AllGather", ALU.bypass, [list(range(NCORES))],
                     [cc_in["xn"].ap()], [cc_out["xn"].ap()])
        for nm, w_b, b_d in (("k", kw_b, kb_d), ("v", vw_b, vb_d)):
            for c in range(DC):
                p.dma(wsb[c], w_b.ap()[c * 128:(c + 1) * 128, :])
            p.dma(biasrow[:], b_d.ap().broadcast_to([128, 1024]))
            for tt in range(NTOK // 128):
                for half in range(2):
                    for c in range(DC):
                        p.add("tensor", lambda e, tt=tt, c=c, half=half: e.matmul(
                            ps1[:], xn[c][:, tt * 128:(tt + 1) * 128],
                            wsb[c][:, half * 512:(half + 1) * 512],
                            start=(c == 0), stop=(c == DC - 1)))
                    p.add("vector", lambda e, half=half: e.tensor_tensor(
                        qtmp[:, half * 512:(half + 1) * 512], ps1[:],
                        biasrow[:, half * 512:(half + 1) * 512], ALU.add))
                p.dma(cc_in[nm].ap()[tt * 128:(tt + 1) * 128, :], qtmp[:])
            p.collective("AllGather", ALU.bypass, [list(range(NCORES))],
                         [cc_in[nm].ap()], [cc_out[nm].ap()])

        p.cast_dma(kf[:, 16 * 1024:].rearrange("p (o d) -> p o d", o=1),
                   bk_d.ap().unsqueeze(0).broadcast_to([128, 1, 1024]))
        p.cast_dma(vf[:, 16 * 1024:].rearrange("p (o d) -> p o d", o=1),
                   bv_d.ap().unsqueeze(0).broadcast_to([128, 1, 1024]))

        if DBG:
            dbg_kf = nc.dram_tensor("dbg_kf", [128, 2048], BF16, kind="ExternalOutput")
            dbg_qnn = nc.dram_tensor("dbg_qnn", [128, 2048], BF16, kind="ExternalOutput")
            dbg_S = nc.dram_tensor("dbg_S", [128, 272], F32, kind="ExternalOutput")
            dbg_gv = nc.dram_tensor("dbg_gv", [128, 64], F32, kind="ExternalOutput")
            dbg_gb = nc.dram_tensor("dbg_gb", [128, 1], F32, kind="ExternalOutput")
            dbg_g0 = nc.dram_tensor("dbg_g0", [128, 16], F32, kind="ExternalOutput")
            dbg_gd = nc.dram_tensor("dbg_gd", [128, 16], F32, kind="ExternalOutput")
            dbg_Sr = nc.dram_tensor("dbg_Sr", [128, 272], F32, kind="ExternalOutput")
            dbg_pb = nc.dram_tensor("dbg_pb", [128, 272], F32, kind="ExternalOutput")
            dbg_t272 = nc.dram_tensor("dbg_t272", [128, 272], F32, kind="ExternalOutput")
            p.dma(dbg_gv.ap(), gv_b[:])
            p.dma(dbg_gb.ap(), gb_b[:])
        for bt in range(16):
            for s2 in range(16):
                p.dma(kf[:, s2 * 1024:(s2 + 1) * 1024],
                      cc_out["k"].ap()[s2 * T + bt * 128:s2 * T + (bt + 1) * 128, :])
                p.dma(vf[:, s2 * 1024:(s2 + 1) * 1024],
                      cc_out["v"].ap()[s2 * T + bt * 128:s2 * T + (bt + 1) * 128, :])
            if DBG and bt == 0:
                p.dma(dbg_kf.ap(), kf[:, 0:2048])
            for tq in range(16):
                rbase = (tq // 2) * D
                cbase = (tq % 2) * T + bt * 128
                for c in range(DC):
                    p.dma(qxt[:, c * 128:(c + 1) * 128],
                          cc_out["xn"].ap()[rbase + c * 128:rbase + (c + 1) * 128,
                                            cbase:cbase + 128])
                for c in range(DC):
                    p.add("tensor", lambda e, c=c: e.matmul(
                        ps1[:, 0:128], qxt[:, c * 128:(c + 1) * 128],
                        qwsl[:, c * 128:(c + 1) * 128],
                        start=(c == 0), stop=(c == DC - 1)))
                p.add("vector", lambda e, tq=tq: e.tensor_tensor(
                    q_nn[:, tq * 128:(tq + 1) * 128], ps1[:, 0:128], qb_b[:], ALU.add))
            if DBG and bt == 0:
                p.dma(dbg_qnn.ap(), q_nn[:])
            for nl in range(2):
                for tq in range(16):
                    kwin = kf[:].rearrange("p (r h) -> p r h", h=64)[:, tq * 17:tq * 17 + 17, :]
                    qs = q_nn[:, tq * 128 + nl * 64:tq * 128 + nl * 64 + 64]\
                        .unsqueeze(1).broadcast_to([128, 17, 64])
                    p.add("vector", lambda e, qs=qs, kwin=kwin: e.tensor_tensor(
                        tmps[:].rearrange("p (r h) -> p r h", h=64), qs, kwin, ALU.mult))
                    p.add("vector", lambda e, tq=tq: e.tensor_reduce(
                        S[:, tq * 17:tq * 17 + 17],
                        tmps[:].rearrange("p (r h) -> p r h", h=64), axis=AX.X, op=ALU.add))
                qg = bass.AP(q_nn, nl * 64, [[2048, 128], [128, 16], [1, 64]])
                p.add("vector", lambda e, qg=qg: e.tensor_tensor(
                    tmps[:, 0:1024].rearrange("p (t d) -> p t d", t=16), qg,
                    gv_b[:].unsqueeze(1).broadcast_to([128, 16, 64]), ALU.mult))
                p.add("vector", lambda e: e.tensor_reduce(
                    gdot[:], tmps[:, 0:1024].rearrange("p (t d) -> p t d", t=16),
                    axis=AX.X, op=ALU.add))
                p.add("scalar", lambda e: e.activation(g0[:], gdot[:], AF.Sigmoid, bias=gb_b[:]))
                p.add("vector", lambda e: e.tensor_tensor(g0[:], g0[:], ga_b[:], ALU.mult))
                if DBG and bt == 0 and nl == 0:
                    p.dma(dbg_g0.ap(), g0[:])
                    p.dma(dbg_gd.ap(), gdot[:])
                    p.dma(dbg_Sr.ap(), S[:])
                    p.dma(dbg_pb.ap(), pb_b[0][:])
                p.add("vector", lambda e, nl=nl: e.tensor_tensor(
                    tmp272[:].rearrange("p (t s) -> p t s", t=16),
                    g0[:].unsqueeze(2).broadcast_to([128, 16, 17]),
                    pb_b[nl][:].rearrange("p (t s) -> p t s", t=16), ALU.mult))
                p.add("vector", lambda e: e.scalar_tensor_tensor(
                    S[:], S[:], 1.0 / 256.0, tmp272[:], ALU.mult, ALU.add))
                if DBG and bt == 0 and nl == 0:
                    p.dma(dbg_S.ap(), S[:])
                    p.dma(dbg_t272.ap(), tmp272[:])
                sv = S[:].rearrange("p (t s) -> p t s", t=16)
                p.add("vector", lambda e, sv=sv: e.reduce_max(nmx[:], sv, axis=AX.X, negate=True))
                p.add("vector", lambda e, sv=sv: e.tensor_tensor(
                    sv, sv, nmx[:].unsqueeze(2).broadcast_to([128, 16, 17]), ALU.add))
                p.add("scalar", lambda e: e.activation(S[:], S[:], AF.Exp))
                p.add("vector", lambda e, sv=sv: e.tensor_reduce(zs[:], sv, axis=AX.X, op=ALU.add))
                p.add("vector", lambda e: e.reciprocal(rzz[:], zs[:]))
                p.add("vector", lambda e, sv=sv: e.tensor_tensor(
                    sv, sv, rzz[:].unsqueeze(2).broadcast_to([128, 16, 17]), ALU.mult))
                p.add("vector", lambda e: e.tensor_copy(P_bf[:], S[:]))
                for tq in range(16):
                    pg = P_bf[:, tq * 17:tq * 17 + 17].unsqueeze(2).broadcast_to([128, 17, 64])
                    vwin = vf[:].rearrange("p (r h) -> p r h", h=64)[:, tq * 17:tq * 17 + 17, :]
                    oT = bass.AP(tmps, 0, [[1088, 128], [1, 17], [17, 64]])
                    p.add("vector", lambda e, pg=pg, vwin=vwin, oT=oT: e.tensor_tensor(
                        oT, pg, vwin, ALU.mult))
                    p.add("vector", lambda e, tq=tq: e.tensor_reduce(
                        o_acc[:, tq * 64:(tq + 1) * 64],
                        tmps[:, 0:1088].rearrange("p (d s) -> p d s", d=64),
                        axis=AX.X, op=ALU.add))
                p.add("vector", lambda e: e.tensor_copy(o_bf[:], o_acc[:]))
                for c in range(DC):
                    p.dma(a_tr[:, c * 128:(c + 1) * 128],
                          o_bf[:, c * 128:(c + 1) * 128], transpose=True)
                for c in range(DC):
                    p.add("vector", lambda e, c=c, nl=nl, bt=bt: e.tensor_copy(
                        xn[c][:, nl * T + bt * 128:nl * T + (bt + 1) * 128],
                        a_tr[:, c * 128:(c + 1) * 128]))

        for oc in range(DC):
            p.dma(wbuf[:], ow_b.ap()[oc])
            for tb in range(NTOK // 512):
                for c in range(DC):
                    p.add("tensor", lambda e, oc=oc, tb=tb, c=c: e.matmul(
                        ps2[:], wbuf[:, c * 128:(c + 1) * 128],
                        xn[c][:, tb * 512:(tb + 1) * 512],
                        start=(c == 0), stop=(c == DC - 1)))
                p.add("vector", lambda e, oc=oc: e.tensor_scalar_add(
                    t1[:], ps2[:], ob_c[:, oc:oc + 1]))
                p.dma(xold[:], x_d.ap()[oc * 128:(oc + 1) * 128, tb * 512:(tb + 1) * 512])
                p.add("vector", lambda e: e.tensor_add(t1[:], t1[:], xold[:]))
                p.dma(xo_d.ap()[oc * 128:(oc + 1) * 128, tb * 512:(tb + 1) * 512], t1[:])
        for nm2 in ("k", "v", "xn"):
            p.dma(qtmp[:], cc_out[nm2].ap()[0:128, 0:1024])
        scope_cm.__exit__(None, None, None)
        if standalone:
            return _finish()


def attn_b_inputs(sh, core):
    d = {k: sh[k] for k in ("ab_k_w", "ab_v_w", "ab_owt", "ab_k_b", "ab_v_b",
                            "ab_out_b", "ab_bias_k", "ab_bias_v", "ab_grep_w",
                            "ab_grep_b", "ab_grep_a")}
    d["ab_qw_sl"] = sh["ab_qw_sl"][core]
    d["ab_qb_sl"] = sh["ab_qb_sl"][core]
    d["ab_pb"] = sh["ab_pb"][core]
    return d




def _t(a):
    return np.ascontiguousarray(np.asarray(a).T)


def _tiled(w_in_out, noc):
    w = np.asarray(w_in_out, np.float32)
    nin, nout = w.shape
    ncc = nin // 128
    assert noc * 128 == nout
    t = w.reshape(ncc, 128, noc, 128).transpose(2, 1, 0, 3).reshape(noc, 128, ncc * 128)
    return np.ascontiguousarray(t)


def host_prep(inputs):
    sh = {}
    sh["w1Tt"] = _tiled(_t(inputs["L0_ffn_w1_w"]), 32)
    sh["w2Tt"] = _tiled(_t(inputs["L0_ffn_w2_w"]), 32)
    sh["h1Tt"] = _tiled(_t(inputs["L0_ffn_h1_w"]), 32)
    sh["h2Tt"] = _tiled(_t(inputs["L0_ffn_h2_w"]), 32)
    sh["w3Tt"] = _tiled(_t(inputs["L0_ffn_w3_w"]), 8)
    sh["w1s"] = np.asarray(inputs["L0_ffn_w1_s"])
    sh["w2s"] = np.asarray(inputs["L0_ffn_w2_s"])
    sh["h1s"] = np.asarray(inputs["L0_ffn_h1_s"])
    sh["h2s"] = np.asarray(inputs["L0_ffn_h2_s"])
    sh["w3s"] = np.asarray(inputs["L0_ffn_w3_s"])

    sh["aa_Wqt"] = _tiled(np.asarray(inputs["L0_attn_a_Wq"]), 8)
    sh["aa_Wkt"] = _tiled(np.asarray(inputs["L0_attn_a_Wk"]), 8)
    sh["aa_Wv"] = np.asarray(inputs["L0_attn_a_Wv"])
    sh["aa_Wot"] = _tiled(np.asarray(inputs["L0_attn_a_Wo"]), 8)
    sh["aa_bo"] = np.asarray(inputs["L0_attn_a_bo"])
    sh["aa_theta"] = np.ascontiguousarray(np.asarray(inputs["L0_attn_a_theta"]).reshape(1, 8))
    sh["aa_gamma"] = np.asarray(inputs["L0_attn_a_gamma"])
    sh["aa_beta"] = np.asarray(inputs["L0_attn_a_beta"])
    ii = np.arange(128)
    sh["m128"] = np.where(ii[:, None] >= ii[None, :], 0.0, -1e30).astype(np.float32)

    sh["mb_WinTt"] = _tiled(_t(inputs["L1_attn_b_W_in"]), 32)
    sh["mb_WdtTt"] = _tiled(_t(inputs["L1_attn_b_W_dt"]), 16)
    sh["mb_WoutTt"] = _tiled(_t(inputs["L1_attn_b_W_out"]), 8)
    sh["mb_conv_w"] = np.asarray(inputs["L1_attn_b_conv_w"])
    sh["mb_conv_b"] = np.asarray(inputs["L1_attn_b_conv_b"])
    sh["mb_A_log"] = np.asarray(inputs["L1_attn_b_A_log"])
    sh["mb_B"] = np.asarray(inputs["L1_attn_b_B_param"])
    sh["mb_C"] = np.asarray(inputs["L1_attn_b_C_param"])
    sh["mb_D"] = np.asarray(inputs["L1_attn_b_D_param"])
    sh["mb_b_dt"] = np.asarray(inputs["L1_attn_b_b_dt"])
    sh["mb_dt_bias"] = np.asarray(inputs["L1_attn_b_dt_bias"])

    rm = np.asarray(inputs["L1_ffn_rail_maker"])
    rup = np.asarray(inputs["L1_ffn_rail_up_proj"])
    rmBD = np.zeros((4, 8, 128, 128), np.float32)
    rupBD = np.zeros((4, 8, 128, 512), np.float32)
    for h in range(4):
        for rb in range(8):
            for rr in range(8):
                r = rb * 8 + rr
                rmBD[h, rb, rr * 16:(rr + 1) * 16, rr * 16:(rr + 1) * 16] = rm[h, r]
                rupBD[h, rb, rr * 16:(rr + 1) * 16, rr * 64:(rr + 1) * 64] = rup[h, r].T
    sh["rf_rmBD"] = rmBD
    sh["rf_rupBD"] = rupBD
    sh["rf_wcT"] = _t(inputs["L1_ffn_w_cproj"])

    sh["ma_Wq"] = np.asarray(inputs["L1_attn_a_Wq"])
    sh["ma_bq"] = np.ascontiguousarray(np.asarray(inputs["L1_attn_a_bq"]).reshape(1, -1))
    sh["ma_Wdt"] = _tiled(np.asarray(inputs["L1_attn_a_Wdown"]), 2)
    sh["ma_bdown"] = np.asarray(inputs["L1_attn_a_bdown"])
    sh["ma_Wup"] = np.asarray(inputs["L1_attn_a_Wup"])
    sh["ma_bup"] = np.ascontiguousarray(np.asarray(inputs["L1_attn_a_bup"]).reshape(1, -1))
    sh["ma_Wot"] = _tiled(np.asarray(inputs["L1_attn_a_Wout"]), 8)
    sh["ma_bout"] = np.asarray(inputs["L1_attn_a_bout"])
    inv = (1.0 / 10000.0 ** (np.arange(0, 64, 2, dtype=np.float32) / np.float32(64.0))).astype(np.float32)
    fr = (np.arange(T, dtype=np.float32)[:, None] * inv[None, :]).astype(np.float32)
    sh["ma_cos"] = np.cos(fr).astype(np.float32)
    sh["ma_sin"] = np.sin(fr).astype(np.float32)

    qw = np.asarray(inputs["L0_attn_b_q_w"])
    qb = np.asarray(inputs["L0_attn_b_q_b"])
    sh["ab_qw_sl"] = [np.ascontiguousarray(
        qw[:, c * 128:(c + 1) * 128].reshape(8, 128, 128).transpose(1, 0, 2).reshape(128, 1024))
        for c in range(NCORES)]
    sh["ab_qb_sl"] = [np.ascontiguousarray(qb[c * 128:(c + 1) * 128].reshape(1, 128))
                      for c in range(NCORES)]
    sh["ab_k_w"] = np.asarray(inputs["L0_attn_b_k_w"])
    sh["ab_v_w"] = np.asarray(inputs["L0_attn_b_v_w"])
    sh["ab_owt"] = _tiled(np.asarray(inputs["L0_attn_b_out_w"]), 8)
    sh["ab_k_b"] = np.ascontiguousarray(np.asarray(inputs["L0_attn_b_k_b"]).reshape(1, -1))
    sh["ab_v_b"] = np.ascontiguousarray(np.asarray(inputs["L0_attn_b_v_b"]).reshape(1, -1))
    sh["ab_out_b"] = np.asarray(inputs["L0_attn_b_out_b"])
    sh["ab_bias_k"] = np.ascontiguousarray(np.asarray(inputs["L0_attn_b_bias_k"]).reshape(1, -1))
    sh["ab_bias_v"] = np.ascontiguousarray(np.asarray(inputs["L0_attn_b_bias_v"]).reshape(1, -1))
    sh["ab_grep_w"] = np.asarray(inputs["L0_attn_b_grep_w"])
    sh["ab_grep_b"] = np.ascontiguousarray(np.asarray(inputs["L0_attn_b_grep_b"]).reshape(1, -1))
    sh["ab_grep_a"] = np.ascontiguousarray(np.asarray(inputs["L0_attn_b_grep_a"]).reshape(1, -1))
    rel_pos = np.arange(17)[None, :] - np.arange(16)[:, None]
    nb = 16
    buckets = (rel_pos > 0).astype(np.int32) * nb
    rp = np.abs(rel_pos)
    max_exact = 8
    is_small = rp < max_exact
    large = max_exact + (np.log(rp.astype(np.float32) / max_exact + 1e-6)
                         / np.log(128.0 / 8) * (nb - max_exact)).astype(np.int32)
    large = np.minimum(large, nb - 1)
    buckets = buckets + np.where(is_small, rp, large)
    rel_emb = np.asarray(inputs["L0_attn_b_rel_emb"])
    pb_all = []
    for c in range(NCORES):
        m = np.zeros((2, 272), np.float32)
        for nl in range(2):
            n = 2 * c + nl
            for tq in range(16):
                for s in range(17):
                    m[nl, tq * 17 + s] = rel_emb[buckets[n, s], tq]
        pb_all.append(m)
    sh["ab_pb"] = pb_all

    x = np.asarray(inputs["L0_attn_a_x"])
    xs = []
    for c in range(NCORES):
        xc = x[BPC * c:BPC * (c + 1)].reshape(NTOK, D)
        xs.append(np.ascontiguousarray(xc.T))
    return sh, xs


def run_launch(build_fn, per_core_maps):
    nc = build_fn()
    res = run_bass_kernel_spmd(nc, per_core_maps, core_ids=list(range(NCORES)))
    return res


FUSED = True


def build_fused():
    nc = bass.Bass("TRN2", target_bir_lowering=False)
    x0_d = nc.dram_tensor("x", [D, NTOK], F32, kind="ExternalInput")
    xo_d = nc.dram_tensor("xo", [D, NTOK], F32, kind="ExternalOutput")
    stages_x = [x0_d] + [nc.dram_tensor(f"xstage{i}", [D, NTOK], F32)
                         for i in range(1, 8)] + [xo_d]
    with contextlib.ExitStack() as stack:
        al = Alloc(nc, stack)
        p = Prog(nc)
        sems = {k: al.sem(f"ch_{k}") for k in ("c", "ds", "dg")}
        kk = Konst(al, p)
        ones_b = al.sb((128, 1), BF16, "onesb")
        p.add("vector", lambda e: e.memset(ones_b[:], 1.0))
        FC = FuseCtx(nc, al, p, kk, ones_b)
        seq = [build_attn_a, build_ffn0, build_attn_b, build_ffn0,
               build_attn_a1, build_rails, build_mamba, build_rails]
        for i, fn in enumerate(seq):
            FC.x_d = stages_x[i]
            FC.xo_d = stages_x[i + 1]
            fn(FC)
        p.emit(sems, al)
    return nc


def fused_inputs(sh, core):
    d = {}
    d.update(attn_a_inputs(sh))
    d.update(ffn0_inputs(sh))
    d.update(attn_b_inputs(sh, core))
    d.update(attn_a1_inputs(sh))
    d.update(rails_inputs(sh))
    d.update(mamba_inputs(sh))
    return d


def kernel(**inputs):
    sh, xs = host_prep(inputs)
    if FUSED:
        nc = build_fused()
        maps = [dict(fused_inputs(sh, c), x=xs[c]) for c in range(NCORES)]
        res = run_bass_kernel_spmd(nc, maps, core_ids=list(range(NCORES)))
        xs = [np.asarray(res.results[c]["xo"]) for c in range(NCORES)]
    else:
        stages = [
            (build_attn_a, lambda c: attn_a_inputs(sh)),
            (build_ffn0, lambda c: ffn0_inputs(sh)),
            (build_attn_b, lambda c: attn_b_inputs(sh, c)),
            (build_ffn0, lambda c: ffn0_inputs(sh)),
            (build_attn_a1, lambda c: attn_a1_inputs(sh)),
            (build_rails, lambda c: rails_inputs(sh)),
            (build_mamba, lambda c: mamba_inputs(sh)),
            (build_rails, lambda c: rails_inputs(sh)),
        ]
        built = {}
        for build_fn, mapfn in stages:
            if build_fn not in built:
                built[build_fn] = build_fn()
            nc = built[build_fn]
            maps = [dict(mapfn(c), x=xs[c]) for c in range(NCORES)]
            res = run_bass_kernel_spmd(nc, maps, core_ids=list(range(NCORES)))
            xs = [np.asarray(res.results[c]["xo"]) for c in range(NCORES)]
    out = np.stack([x.T.reshape(BPC, T, D) for x in xs]).reshape(B, T, D)
    return np.ascontiguousarray(out)
